# version 02
# baseline (speedup 1.0000x reference)
import numpy as np
import concourse.bass as bass
import concourse.mybir as mybir
from concourse.bass_utils import run_bass_kernel_spmd

F32 = mybir.dt.float32
BF16 = mybir.dt.bfloat16
AF = mybir.ActivationFunctionType
ALU = mybir.AluOpType
AX = mybir.AxisListType.X

N_CORES = 8
D = 1024
L = 4096
SEQ = 2048
NB = 2
LC = 512
NLC = L // LC
ND = D // 128


class Seq:

    def __init__(self):
        self.ops = []

    def op(self, eng, fn, inc=1):
        assert not getattr(self, "frozen", False), "Seq.op after emit started"
        self.ops.append((eng, fn, inc, False))

    def dma(self, eng, fn):
        assert not getattr(self, "frozen", False), "Seq.dma after emit started"
        self.ops.append((eng, fn, 16, False))

    def cc(self, fn):
        assert not getattr(self, "frozen", False)
        idx = getattr(self, "n_cc", 0)
        self.n_cc = idx + 1
        self.ops.append(("gpsimd", fn, 0, ("cc", idx)))

    EPOCH_LIMIT = 24000

    def emit(self, nc, sems, cc_sems=()):
        self.frozen = True
        assert getattr(self, "n_cc", 0) <= len(cc_sems)
        n = len(self.ops)
        epoch = []
        epoch_cum = []
        ep, cum = 0, 0
        for (eng, fn, inc, vl) in self.ops:
            if cum + inc > self.EPOCH_LIMIT:
                ep += 1
                cum = 0
            epoch.append(ep)
            epoch_cum.append(cum)
            cum += inc
        n_ep = ep + 1
        assert n_ep <= len(sems), f"need {n_ep} semaphores, have {len(sems)}"
        totals = [0] * n_ep
        for i, (eng, fn, inc, vl) in enumerate(self.ops):
            totals[epoch[i]] = epoch_cum[i] + inc

        def section(engobj, name):
            last_ep_seen = -1
            for i, (eng, fn, inc, vl) in enumerate(self.ops):
                if eng != name:
                    continue
                ep_i = epoch[i]
                if ep_i != last_ep_seen:
                    for k in range(last_ep_seen if last_ep_seen >= 0 else 0, ep_i):
                        engobj.wait_ge(sems[k], totals[k])
                    last_ep_seen = ep_i
                if epoch_cum[i] > 0:
                    engobj.wait_ge(sems[ep_i], epoch_cum[i])
                if i > 0 and isinstance(self.ops[i - 1][3], tuple):
                    engobj.wait_ge(cc_sems[self.ops[i - 1][3][1]], 1)
                r = fn(engobj)
                if isinstance(vl, tuple):
                    r.then_inc(cc_sems[vl[1]])
                elif inc > 0:
                    r.then_inc(sems[ep_i], inc)
            if name == "tensor":
                for k in range(n_ep):
                    engobj.wait_ge(sems[k], totals[k])

        with nc.Block() as block:
            @block.sync
            def _(e):
                section(e, "sync")

            @block.vector
            def _(e):
                section(e, "vector")

            @block.scalar
            def _(e):
                section(e, "scalar")

            @block.tensor
            def _(e):
                section(e, "tensor")

            @block.gpsimd
            def _(e):
                section(e, "gpsimd")


def host_constants():
    c = {}
    c["ident_bf16"] = np.eye(128, dtype=np.float32)
    import ml_dtypes
    c["ident_bf16"] = np.eye(128, dtype=ml_dtypes.bfloat16)
    c["ident_f32"] = np.eye(128, dtype=np.float32)
    c["ones_row_f32"] = np.ones((1, 128), dtype=np.float32)
    c["ones_row_bf16"] = np.ones((1, 128), dtype=ml_dtypes.bfloat16)
    c["ones_col_f32"] = np.ones((128, 1), dtype=np.float32)
    c["ones_col_bf16"] = np.ones((128, 1), dtype=ml_dtypes.bfloat16)
    c["ut128"] = np.triu(np.ones((128, 128), dtype=np.float32))
    return c


def declare_constants(nc):
    aps = {}
    aps["ident_bf16"] = nc.dram_tensor("ident_bf16", [128, 128], BF16, kind="ExternalInput").ap()
    aps["ident_f32"] = nc.dram_tensor("ident_f32", [128, 128], F32, kind="ExternalInput").ap()
    aps["ones_row_f32"] = nc.dram_tensor("ones_row_f32", [1, 128], F32, kind="ExternalInput").ap()
    aps["ones_row_bf16"] = nc.dram_tensor("ones_row_bf16", [1, 128], BF16, kind="ExternalInput").ap()
    aps["ones_col_f32"] = nc.dram_tensor("ones_col_f32", [128, 1], F32, kind="ExternalInput").ap()
    aps["ones_col_bf16"] = nc.dram_tensor("ones_col_bf16", [128, 1], BF16, kind="ExternalInput").ap()
    aps["ut128"] = nc.dram_tensor("ut128", [128, 128], F32, kind="ExternalInput").ap()
    return aps


class Ctx:

    def __init__(self, nc, sq, stack, caps, psum_scratch=True):
        self.nc = nc
        self.sq = sq
        sb = lambda name, shape, dt: stack.enter_context(nc.sbuf_tensor(name, shape, dt))
        ps = lambda name, shape, dt: stack.enter_context(nc.psum_tensor(name, shape, dt))
        self.sb = sb
        self.ps = ps
        self.ident_bf16 = sb("c_ident_b", [128, 128], BF16)
        self.ident_f32 = sb("c_ident_f", [128, 128], F32)
        self.ones_row_f32 = sb("c_or_f", [1, 128], F32)
        self.ones_row_bf16 = sb("c_or_b", [1, 128], BF16)
        self.ones_col_f32 = sb("c_oc_f", [128, 1], F32)
        self.ones_col_bf16 = sb("c_oc_b", [128, 1], BF16)
        self.consts = sb("c_consts", [128, 8], F32)
        self.n_consts = 0
        self.const_map = {}
        for name in ["ident_bf16", "ident_f32", "ones_row_f32", "ones_row_bf16",
                     "ones_col_f32", "ones_col_bf16"]:
            dst = getattr(self, name)
            src = caps[name]
            self.sq.dma("sync", lambda e, d=dst, s=src: e.dma_start(out=d[:], in_=s))
        if psum_scratch:
            self.pt = ps("p_tr", [128, 128], F32)

    def const_col(self, val):
        if val in self.const_map:
            return self.const_map[val]
        i = self.n_consts
        assert i < 8
        self.n_consts += 1
        col = self.consts[:, i:i + 1]
        self.sq.op("vector", lambda e, c=col, v=val: e.memset(c, v))
        self.const_map[val] = col
        return col

    def load_vec_cols(self, dram_ap, n, name, dt=F32):
        k = n // 128
        assert k * 128 == n and k <= 128
        stage = self.sb(name + "_stg", [max(k, 1), 128], F32)
        out = self.sb(name, [128, k], dt)
        self.sq.dma("sync", lambda e: e.dma_start(
            out=stage[:], in_=dram_ap.rearrange("(k p) -> k p", k=k)))
        self.sq.op("tensor", lambda e: e.transpose(self.pt[:, 0:k], stage[:], self.ident_f32[0:k, 0:k]))
        self.sq.op("vector", lambda e: e.tensor_copy(out[:], self.pt[:, 0:k]))
        return out

    def load_scalar(self, dram_ap, name):
        out = self.sb(name + "_sb", [1, 1], F32)
        self.sq.dma("sync", lambda e: e.dma_start(out=out[:], in_=dram_ap.rearrange("(a x) -> a x", a=1)))
        return out

    def load_w_bf16_in(self, stack, dram_ap, din, dout, name, stage=None):
        ndc = din // 128
        if stage is None:
            stage = stack.enter_context(self.nc.sbuf_tensor(name + "_stg", [128, dout], F32))
        out = stack.enter_context(self.nc.sbuf_tensor(name, [128, ndc * dout], BF16))
        for dc in range(ndc):
            self.sq.dma("sync", lambda e, dc=dc: e.dma_start(
                out=stage[:, 0:dout], in_=dram_ap[dc * 128:(dc + 1) * 128, :]))
            self.sq.op("vector", lambda e, dc=dc: e.tensor_copy(
                out[:, dc * dout:(dc + 1) * dout], stage[:, 0:dout]))
        return out

    def load_w_bf16(self, dram_ap, din, dout, name):
        ndc = din // 128
        stage = self.sb(name + "_stg", [128, dout], F32)
        out = self.sb(name, [128, ndc * dout], BF16)
        for dc in range(ndc):
            self.sq.dma("sync", lambda e, dc=dc: e.dma_start(
                out=stage[:], in_=dram_ap[dc * 128:(dc + 1) * 128, :]))
            self.sq.op("vector", lambda e, dc=dc: e.tensor_copy(
                out[:, dc * dout:(dc + 1) * dout], stage[:]))
        return out

    def bcast_row(self, psum_ap, row_ap, n, dt="f32", m=128):
        ones = self.ones_row_f32 if dt == "f32" else self.ones_row_bf16
        self.sq.op("tensor", lambda e: e.matmul(psum_ap, ones[:, 0:m], row_ap, start=True, stop=True))


def build_s1(n_lc=NLC, n_cyc=3, debug=False):
    import contextlib
    nc = bass.Bass("TRN2", target_bir_lowering=False)
    x_in = nc.dram_tensor("x", [D, L], F32, kind="ExternalInput").ap()
    x_out = nc.dram_tensor("xo", [D, L], F32, kind="ExternalOutput").ap()
    w = {}
    for nm, shp in [("cn_w1", [D, D]), ("cn_w2", [D, 512]), ("cn_w3", [512, 256]),
                    ("cn_w4", [256, 1]), ("gc_w1", [D, 512]), ("gc_w2", [512, 256]),
                    ("gc_w3", [256, 1])]:
        w[nm] = nc.dram_tensor(nm, shp, F32, kind="ExternalInput").ap()
    for nm, n in [("cn_b1", D), ("cn_b2", 512), ("cn_b3", 256), ("gc_b1", 512),
                  ("gc_b2", 256)]:
        w[nm] = nc.dram_tensor(nm, [n], F32, kind="ExternalInput").ap()
    for nm in ["cn_b4", "gc_b3", "ar"]:
        w[nm] = nc.dram_tensor(nm, [1], F32, kind="ExternalInput").ap()
    caps = declare_constants(nc)
    dbg = nc.dram_tensor("dbg", [8, L], F32, kind="ExternalOutput").ap() if debug else None

    sq = Seq()
    with contextlib.ExitStack() as stack:
        cx = Ctx(nc, sq, stack, caps)
        sb, ps = cx.sb, cx.ps
        sems = [stack.enter_context(nc.semaphore(f"S{i}")) for i in range(12)]

        w1 = cx.load_w_bf16(w["cn_w1"], D, D, "w1")
        w2 = cx.load_w_bf16(w["cn_w2"], D, 512, "w2")
        w3 = cx.load_w_bf16(w["cn_w3"], 512, 256, "w3")
        w4 = cx.load_w_bf16(w["cn_w4"], 256, 1, "w4")
        g1w = cx.load_w_bf16(w["gc_w1"], D, 512, "g1w")
        g2w = cx.load_w_bf16(w["gc_w2"], 512, 256, "g2w")
        g3w = cx.load_w_bf16(w["gc_w3"], 256, 1, "g3w")
        b1 = cx.load_vec_cols(w["cn_b1"], D, "b1")
        b2 = cx.load_vec_cols(w["cn_b2"], 512, "b2")
        b3 = cx.load_vec_cols(w["cn_b3"], 256, "b3")
        gb1 = cx.load_vec_cols(w["gc_b1"], 512, "gb1")
        gb2 = cx.load_vec_cols(w["gc_b2"], 256, "gb2")
        b4 = cx.load_scalar(w["cn_b4"], "b4")
        gb3 = cx.load_scalar(w["gc_b3"], "gb3")
        ar = cx.load_scalar(w["ar"], "ar")

        outb = sb("outb", [128, ND * L], BF16)
        h1b = sb("h1b", [128, 8 * LC], BF16)
        h2b = sb("h2b", [128, 4 * LC], BF16)
        h3b = sb("h3b", [128, 2 * LC], BF16)
        xs8 = sb("xs8", [128, ND * LC], F32)
        sqs = sb("sqs", [128, LC], F32)
        rows = sb("rows", [1, 4 * LC], F32)
        mean_r = rows[:, 0 * LC:1 * LC]
        msq_r = rows[:, 1 * LC:2 * LC]
        A_r = rows[:, 2 * LC:3 * LC]
        B_r = rows[:, 3 * LC:4 * LC]
        tf_r = sb("tf_r", [1, LC], F32)
        gc = sb("gc", [128, 2 * ND], F32)
        gcb = sb("gcb", [128, 2 * ND], BF16)
        g1o = sb("g1o", [128, 2 * 4], BF16)
        g2o = sb("g2o", [128, 2 * 2], BF16)
        gf = sb("gf", [1, 2], F32)
        s_ap = sb("s_ap", [1, 1], F32)
        o_ap = sb("o_ap", [1, 2], F32)
        tmp1 = sb("tmp1", [1, 2], F32)
        twt = sb("twt", [128, LC], F32)
        twu = sb("twu", [128, LC], F32)

        eps_col = cx.const_col(1e-5)
        pa = ps("pa", [128, LC], F32)
        pb = ps("pb", [128, LC], F32)
        prow = ps("prow", [1, LC], F32)
        prow2 = ps("prow2", [1, LC], F32)
        pg = ps("pg", [128, 1], F32)

        OP = sq.op
        DMA = sq.dma

        for lc in range(n_lc):
            l0 = lc * LC
            b = 0 if lc < NLC // 2 else 1
            for dc in range(ND):
                DMA("sync", lambda e, dc=dc, l0=l0: e.dma_start(
                    out=xs8[:, dc * LC:(dc + 1) * LC],
                    in_=x_in[dc * 128:(dc + 1) * 128, l0:l0 + LC]))
                OP("tensor", lambda e, dc=dc: e.matmul(
                    prow[:], cx.ones_col_f32[:], xs8[:, dc * LC:(dc + 1) * LC],
                    start=(dc == 0), stop=(dc == ND - 1)))
                OP("scalar", lambda e, dc=dc: e.activation(
                    sqs[:], xs8[:, dc * LC:(dc + 1) * LC], AF.Square))
                OP("tensor", lambda e, dc=dc: e.matmul(
                    prow2[:], cx.ones_col_f32[:], sqs[:], start=(dc == 0), stop=(dc == ND - 1)))
            OP("vector", lambda e: e.tensor_scalar(mean_r, prow[:], 1.0 / D, None, ALU.mult))
            OP("vector", lambda e: e.tensor_scalar(msq_r, prow2[:], 1.0 / D, None, ALU.mult))
            OP("vector", lambda e: e.tensor_tensor(A_r, mean_r, mean_r, ALU.mult))
            OP("vector", lambda e: e.tensor_tensor(A_r, msq_r, A_r, ALU.subtract))
            OP("scalar", lambda e: e.activation(A_r, A_r, AF.Sqrt, bias=eps_col[0:1, :]))
            OP("vector", lambda e: e.reciprocal(A_r, A_r))
            OP("vector", lambda e: e.tensor_tensor(B_r, mean_r, A_r, ALU.mult))
            OP("vector", lambda e: e.tensor_scalar(B_r, B_r, -1.0, None, ALU.mult))
            if debug and lc == 0:
                DMA("sync", lambda e: e.dma_start(out=dbg[0:1, 0:4*LC], in_=rows[:]))
            cx.bcast_row(pa[:], A_r, LC)
            cx.bcast_row(pb[:], B_r, LC)
            for dc in range(ND):
                ob = lambda dc=dc, l0=l0: outb[:, dc * L + l0:dc * L + l0 + LC]
                OP("vector", lambda e, dc=dc: e.tensor_tensor(
                    twt[:], xs8[:, dc * LC:(dc + 1) * LC], pa[:], ALU.mult))
                OP("vector", lambda e, ob=ob: e.tensor_tensor(ob(), twt[:], pb[:], ALU.add))
        for b in range(NB):
            for dc in range(ND):
                seg = outb[:, dc * L + b * SEQ: dc * L + (b + 1) * SEQ]
                OP("vector", lambda e, seg=seg, b=b, dc=dc: e.tensor_reduce(
                    gc[:, b * ND + dc:b * ND + dc + 1], seg, AX, ALU.add))
        OP("vector", lambda e: e.tensor_scalar(gcb[:], gc[:], 1.0 / SEQ, None, ALU.mult))

        for b in range(NB):
            for ft in range(4):
                for dc in range(ND):
                    OP("tensor", lambda e, b=b, ft=ft, dc=dc: e.matmul(
                        pg[:], g1w[:, dc * 512 + ft * 128: dc * 512 + ft * 128 + 128],
                        gcb[:, b * ND + dc:b * ND + dc + 1],
                        start=(dc == 0), stop=(dc == ND - 1)))
                OP("scalar", lambda e, b=b, ft=ft: e.activation(
                    g1o[:, b * 4 + ft:b * 4 + ft + 1], pg[:], AF.Gelu, bias=gb1[:, ft:ft + 1]))
            for ft in range(2):
                for dc in range(4):
                    OP("tensor", lambda e, b=b, ft=ft, dc=dc: e.matmul(
                        pg[:], g2w[:, dc * 256 + ft * 128: dc * 256 + ft * 128 + 128],
                        g1o[:, b * 4 + dc:b * 4 + dc + 1],
                        start=(dc == 0), stop=(dc == 3)))
                OP("scalar", lambda e, b=b, ft=ft: e.activation(
                    g2o[:, b * 2 + ft:b * 2 + ft + 1], pg[:], AF.Gelu, bias=gb2[:, ft:ft + 1]))
            for dc in range(2):
                OP("tensor", lambda e, b=b, dc=dc: e.matmul(
                    pg[0:1, :], g3w[:, dc:dc + 1], g2o[:, b * 2 + dc:b * 2 + dc + 1],
                    start=(dc == 0), stop=(dc == 1)))
            OP("scalar", lambda e, b=b: e.activation(
                gf[:, b:b + 1], pg[0:1, :], AF.Sigmoid, bias=gb3[:]))
        if debug:
            DMA("sync", lambda e: e.dma_start(out=dbg[1:2, 0:2], in_=gf[:]))
        OP("vector", lambda e: e.tensor_scalar(s_ap[:], ar[:], 0.6, None, ALU.mult))
        OP("vector", lambda e: e.tensor_scalar(tmp1[:, 0:1], ar[:], 1.4, None, ALU.mult))
        for b in range(NB):
            OP("vector", lambda e, b=b: e.tensor_tensor(
                o_ap[:, b:b + 1], gf[:, b:b + 1], tmp1[:, 0:1], ALU.mult))
            OP("vector", lambda e, b=b: e.tensor_tensor(
                o_ap[:, b:b + 1], o_ap[:, b:b + 1], ar[:], ALU.subtract))
            OP("vector", lambda e, b=b: e.tensor_scalar(
                o_ap[:, b:b + 1], o_ap[:, b:b + 1], 0.1, None, ALU.add))

        SGN = [1.0, 1.0, -1.0, -1.0, -1.0, -1.0, 1.0, 1.0]
        SRC = [4, 5, 6, 7, 0, 1, 2, 3]
        for cyc in range(n_cyc):
            for lc in range(n_lc):
                l0 = lc * LC
                b = 0 if lc < NLC // 2 else 1
                ob = lambda dc, l0=l0: outb[:, dc * L + l0:dc * L + l0 + LC]
                for ft in range(8):
                    for dc in range(ND):
                        OP("tensor", lambda e, ft=ft, dc=dc, ob=ob: e.matmul(
                            pa[:], w1[:, dc * D + ft * 128: dc * D + ft * 128 + 128],
                            ob(dc),
                            start=(dc == 0), stop=(dc == ND - 1)))
                    OP("scalar", lambda e, ft=ft: e.activation(
                        h1b[:, ft * LC:(ft + 1) * LC], pa[:], AF.Gelu, bias=b1[:, ft:ft + 1]))
                for ft in range(4):
                    for dc in range(8):
                        OP("tensor", lambda e, ft=ft, dc=dc: e.matmul(
                            pa[:], w2[:, dc * 512 + ft * 128: dc * 512 + ft * 128 + 128],
                            h1b[:, dc * LC:(dc + 1) * LC],
                            start=(dc == 0), stop=(dc == 7)))
                    OP("scalar", lambda e, ft=ft: e.activation(
                        h2b[:, ft * LC:(ft + 1) * LC], pa[:], AF.Gelu, bias=b2[:, ft:ft + 1]))
                for ft in range(2):
                    for dc in range(4):
                        OP("tensor", lambda e, ft=ft, dc=dc: e.matmul(
                            pa[:], w3[:, dc * 256 + ft * 128: dc * 256 + ft * 128 + 128],
                            h2b[:, dc * LC:(dc + 1) * LC],
                            start=(dc == 0), stop=(dc == 3)))
                    OP("scalar", lambda e, ft=ft: e.activation(
                        h3b[:, ft * LC:(ft + 1) * LC], pa[:], AF.Gelu, bias=b3[:, ft:ft + 1]))
                for dc in range(2):
                    OP("tensor", lambda e, dc=dc: e.matmul(
                        prow[:], w4[:, dc:dc + 1], h3b[:, dc * LC:(dc + 1) * LC],
                        start=(dc == 0), stop=(dc == 1)))
                OP("scalar", lambda e: e.activation(tf_r[:], prow[:], AF.Sigmoid, bias=b4[:]))
                OP("vector", lambda e: e.tensor_scalar(tf_r[:], tf_r[:], s_ap[:], None, ALU.mult))
                OP("vector", lambda e, b=b: e.tensor_scalar(
                    tf_r[:], tf_r[:], o_ap[:, b:b + 1], None, ALU.add))
                if debug and cyc == 0:
                    DMA("sync", lambda e, lc=lc: e.dma_start(out=dbg[2:3, lc*LC:(lc+1)*LC], in_=tf_r[:]))
                cx.bcast_row(pb[:], tf_r[:], LC)
                for c in range(4):
                    s2 = SRC[c]
                    OP("vector", lambda e, ob=ob, c=c: e.tensor_copy(twt[:], ob(c)))
                    OP("vector", lambda e, ob=ob, s2=s2: e.tensor_tensor(twu[:], ob(s2), pb[:], ALU.mult))
                    OP("vector", lambda e, ob=ob, c=c: e.scalar_tensor_tensor(
                        ob(c), twu[:], SGN[c], ob(c), ALU.mult, ALU.add))
                    OP("vector", lambda e: e.tensor_tensor(twu[:], twt[:], pb[:], ALU.mult))
                    OP("vector", lambda e, ob=ob, s2=s2: e.scalar_tensor_tensor(
                        ob(s2), twu[:], SGN[s2], ob(s2), ALU.mult, ALU.add))

        for lc in range(n_lc):
            l0 = lc * LC
            for dc in range(ND):
                DMA("sync", lambda e, dc=dc, l0=l0: e.dma_start(
                    out=xs8[:, dc * LC:(dc + 1) * LC],
                    in_=x_in[dc * 128:(dc + 1) * 128, l0:l0 + LC]))
                OP("vector", lambda e, dc=dc, l0=l0: e.scalar_tensor_tensor(
                    xs8[:, dc * LC:(dc + 1) * LC],
                    outb[:, dc * L + l0:dc * L + l0 + LC], 0.5,
                    xs8[:, dc * LC:(dc + 1) * LC], ALU.mult, ALU.add))
                DMA("sync", lambda e, dc=dc, l0=l0: e.dma_start(
                    out=x_out[dc * 128:(dc + 1) * 128, l0:l0 + LC],
                    in_=xs8[:, dc * LC:(dc + 1) * LC]))

        sq.emit(nc, sems)
    return nc


def _chunked(c):
    return c


def run_s1(x_fm_shards, weights):
    nc = build_s1()
    hc = host_constants()
    in_maps = []
    for c in range(N_CORES):
        m = {"x": x_fm_shards[c]}
        m.update({
            "cn_w1": weights["L0_ffn_cn_w1"], "cn_w2": weights["L0_ffn_cn_w2"],
            "cn_w3": weights["L0_ffn_cn_w3"], "cn_w4": weights["L0_ffn_cn_w4"],
            "gc_w1": weights["L0_ffn_gc_w1"], "gc_w2": weights["L0_ffn_gc_w2"],
            "gc_w3": weights["L0_ffn_gc_w3"],
            "cn_b1": weights["L0_ffn_cn_b1"], "cn_b2": weights["L0_ffn_cn_b2"],
            "cn_b3": weights["L0_ffn_cn_b3"], "cn_b4": weights["L0_ffn_cn_b4"],
            "gc_b1": weights["L0_ffn_gc_b1"], "gc_b2": weights["L0_ffn_gc_b2"],
            "gc_b3": weights["L0_ffn_gc_b3"], "ar": weights["L0_ffn_adaptive_range"],
        })
        m.update(hc)
        in_maps.append(m)
    res = run_bass_kernel_spmd(nc, in_maps, core_ids=list(range(N_CORES)))
    return [r["xo"] for r in res.results]


def to_fm_shards(x):
    out = []
    for c in range(N_CORES):
        xc = x[2 * c:2 * c + 2].reshape(L, D).T
        out.append(np.ascontiguousarray(xc))
    return out


def from_fm_shards(shards):
    x = np.empty((16, SEQ, D), dtype=np.float32)
    for c in range(N_CORES):
        x[2 * c:2 * c + 2] = shards[c].T.reshape(2, SEQ, D)
    return x


def kernel(**inputs):
    x = inputs["L0_attn_x"]
    shards = to_fm_shards(x)
    shards = run_s1(shards, inputs)
    shards = run_s2(shards, inputs)
    shards = run_s1(shards, inputs)
    shards = run_s4(shards, inputs)
    shards = run_s5(shards, inputs)
    shards = run_s4(shards, inputs)
    return from_fm_shards(shards)


def build_s2():
    import contextlib
    nc = bass.Bass("TRN2", target_bir_lowering=False)
    x_in = nc.dram_tensor("x", [D, L], F32, kind="ExternalInput").ap()
    x_out = nc.dram_tensor("xo", [D, L], F32, kind="ExternalOutput").ap()
    wq_d = nc.dram_tensor("wq", [D, D], F32, kind="ExternalInput").ap()
    wk_d = nc.dram_tensor("wk", [D, D], F32, kind="ExternalInput").ap()
    wv_d = nc.dram_tensor("wv", [D, D], F32, kind="ExternalInput").ap()
    caps = declare_constants(nc)
    NH = 16
    NQC = SEQ // LC
    NKT = SEQ // 128

    sq = Seq()
    with contextlib.ExitStack() as stack:
        cx = Ctx(nc, sq, stack, caps, psum_scratch=False)
        sb, ps = cx.sb, cx.ps
        sems = [stack.enter_context(nc.semaphore(f"S{i}")) for i in range(12)]
        eps_col = cx.const_col(1e-5)

        wq = cx.load_w_bf16(wq_d, D, D, "wq_sb")
        wk = cx.load_w_bf16(wk_d, D, D, "wk_sb")
        wv = cx.load_w_bf16(wv_d, D, D, "wv_sb")

        xn = sb("xn", [128, ND * SEQ], BF16)
        xs8 = sb("xs8", [128, ND * LC], F32)
        sqs = sb("sqs", [128, LC], F32)
        rows = sb("rows", [1, 4 * LC], F32)
        mean_r = rows[:, 0:LC]
        msq_r = rows[:, LC:2 * LC]
        A_r = rows[:, 2 * LC:3 * LC]
        B_r = rows[:, 3 * LC:4 * LC]
        vtm = sb("vtm", [128, NKT * D], BF16)
        qh = sb("qh", [64, NQC * LC], BF16)
        kh = sb("kh", [64, NKT * 128], BF16)
        sT = sb("sT", [128, NKT * LC], BF16)
        zrow = sb("zrow", [1, LC], F32)
        avs = sb("avs", [64, LC], F32)
        xtmp = sb("xtmp", [64, LC], F32)

        pa = ps("pa", [128, LC], F32)
        pa2 = ps("pa2", [128, LC], F32)
        pz = ps("pz", [1, LC], F32)
        pav = ps("pav", [64, LC], F32)
        pbc = ps("pbc", [64, LC], F32)
        prow = ps("prow", [1, LC], F32)
        prow2 = ps("prow2", [1, LC], F32)

        OP = sq.op
        DMA = sq.dma

        for b in range(NB):
            base = b * SEQ
            for lc in range(SEQ // LC):
                l0 = base + lc * LC
                for dc in range(ND):
                    DMA("sync", lambda e, dc=dc, l0=l0: e.dma_start(
                        out=xs8[:, dc * LC:(dc + 1) * LC],
                        in_=x_in[dc * 128:(dc + 1) * 128, l0:l0 + LC]))
                    OP("tensor", lambda e, dc=dc: e.matmul(
                        prow[:], cx.ones_col_f32[:], xs8[:, dc * LC:(dc + 1) * LC],
                        start=(dc == 0), stop=(dc == ND - 1)))
                    OP("scalar", lambda e, dc=dc: e.activation(
                        sqs[:], xs8[:, dc * LC:(dc + 1) * LC], AF.Square))
                    OP("tensor", lambda e, dc=dc: e.matmul(
                        prow2[:], cx.ones_col_f32[:], sqs[:], start=(dc == 0), stop=(dc == ND - 1)))
                OP("vector", lambda e: e.tensor_scalar(mean_r, prow[:], 1.0 / D, None, ALU.mult))
                OP("vector", lambda e: e.tensor_scalar(msq_r, prow2[:], 1.0 / D, None, ALU.mult))
                OP("vector", lambda e: e.tensor_tensor(A_r, mean_r, mean_r, ALU.mult))
                OP("vector", lambda e: e.tensor_tensor(A_r, msq_r, A_r, ALU.subtract))
                OP("scalar", lambda e: e.activation(A_r, A_r, AF.Sqrt, bias=eps_col[0:1, :]))
                OP("vector", lambda e: e.reciprocal(A_r, A_r))
                OP("vector", lambda e: e.tensor_tensor(B_r, mean_r, A_r, ALU.mult))
                OP("vector", lambda e: e.tensor_scalar(B_r, B_r, -1.0, None, ALU.mult))
                cx.bcast_row(pa[:], A_r, LC)
                cx.bcast_row(pa2[:], B_r, LC)
                for dc in range(ND):
                    OP("vector", lambda e, dc=dc: e.tensor_tensor(
                        sqs[:], xs8[:, dc * LC:(dc + 1) * LC], pa[:], ALU.mult))
                    OP("vector", lambda e, dc=dc, lc=lc: e.tensor_tensor(
                        xn[:, dc * SEQ + lc * LC: dc * SEQ + (lc + 1) * LC],
                        sqs[:], pa2[:], ALU.add))
            for lt in range(NKT):
                for half in range(2):
                    for dc in range(ND):
                        OP("tensor", lambda e, lt=lt, half=half, dc=dc: e.matmul(
                            pa[:], xn[:, dc * SEQ + lt * 128: dc * SEQ + (lt + 1) * 128],
                            wv[:, dc * D + half * 512:(dc * D) + (half + 1) * 512],
                            start=(dc == 0), stop=(dc == ND - 1)))
                    OP("vector", lambda e, lt=lt, half=half: e.tensor_copy(
                        vtm[:, lt * D + half * 512: lt * D + (half + 1) * 512], pa[:]))
            for h in range(NH):
                for (dst, wmat) in ((qh, wq), (kh, wk)):
                    for qc in range(NQC):
                        for dc in range(ND):
                            OP("tensor", lambda e, dst=dst, wmat=wmat, qc=qc, dc=dc, h=h: e.matmul(
                                pav[:], wmat[:, dc * D + h * 64: dc * D + h * 64 + 64],
                                xn[:, dc * SEQ + qc * LC: dc * SEQ + (qc + 1) * LC],
                                start=(dc == 0), stop=(dc == ND - 1)))
                        OP("vector", lambda e, dst=dst, qc=qc: e.tensor_copy(
                            dst[:, qc * LC:(qc + 1) * LC], pav[:]))
                for qc in range(NQC):
                    for kt in range(NKT):
                        OP("tensor", lambda e, kt=kt, qc=qc: e.matmul(
                            pa[:], kh[:, kt * 128:(kt + 1) * 128],
                            qh[:, qc * LC:(qc + 1) * LC], start=True, stop=True))
                        OP("scalar", lambda e, kt=kt: e.activation(
                            sT[:, kt * LC:(kt + 1) * LC], pa[:], AF.Exp, scale=0.125))
                        OP("tensor", lambda e, kt=kt: e.matmul(
                            pz[:], cx.ones_col_bf16[:], sT[:, kt * LC:(kt + 1) * LC],
                            start=(kt == 0), stop=(kt == NKT - 1)))
                    for kt in range(NKT):
                        OP("tensor", lambda e, kt=kt, h=h: e.matmul(
                            pav[:], vtm[:, kt * D + h * 64: kt * D + h * 64 + 64],
                            sT[:, kt * LC:(kt + 1) * LC],
                            start=(kt == 0), stop=(kt == NKT - 1)))
                    OP("vector", lambda e: e.reciprocal(zrow[:], pz[:]))
                    cx.bcast_row(pbc[:], zrow[:], LC, dt="f32", m=64)
                    OP("scalar", lambda e: e.activation(avs[:], pav[:], AF.Copy))
                    OP("vector", lambda e: e.tensor_tensor(avs[:], avs[:], pbc[:], ALU.mult))
                    dc = h // 2
                    p0 = (h % 2) * 64
                    l0 = base + qc * LC
                    DMA("sync", lambda e, dc=dc, p0=p0, l0=l0: e.dma_start(
                        out=xtmp[:], in_=x_in[dc * 128 + p0: dc * 128 + p0 + 64, l0:l0 + LC]))
                    OP("vector", lambda e: e.tensor_tensor(xtmp[:], xtmp[:], avs[:], ALU.add))
                    DMA("sync", lambda e, dc=dc, p0=p0, l0=l0: e.dma_start(
                        out=x_out[dc * 128 + p0: dc * 128 + p0 + 64, l0:l0 + LC], in_=xtmp[:]))

        sq.emit(nc, sems)
    return nc


def _reorder_W_qkv(W):
    H, Fqk, Fv = 16, 64, 64
    Wr = W.reshape(1024, H, 192)
    Wq = np.ascontiguousarray(Wr[:, :, 0:64].reshape(1024, 1024))
    Wk = np.ascontiguousarray(Wr[:, :, 64:128].reshape(1024, 1024))
    Wv = np.ascontiguousarray(Wr[:, :, 128:192].reshape(1024, 1024))
    return Wq, Wk, Wv


def run_s2(x_fm_shards, weights):
    nc = build_s2()
    hc = host_constants()
    Wq, Wk, Wv = _reorder_W_qkv(weights["L0_attn_W"])
    in_maps = []
    for c in range(N_CORES):
        m = {"x": x_fm_shards[c], "wq": Wq, "wk": Wk, "wv": Wv}
        m.update(hc)
        in_maps.append(m)
    res = run_bass_kernel_spmd(nc, in_maps, core_ids=list(range(N_CORES)))
    return [r["xo"] for r in res.results]


CAP = 10240


def build_s4(debug=False):
    import contextlib
    nc = bass.Bass("TRN2", target_bir_lowering=False)
    x_in = nc.dram_tensor("x", [D, L], F32, kind="ExternalInput").ap()
    x_out = nc.dram_tensor("xo", [D, L], F32, kind="ExternalOutput").ap()
    wrT_d = nc.dram_tensor("wrT", [D, 8], F32, kind="ExternalInput").ap()
    w1_d = nc.dram_tensor("w1", [8 * D, 4096], F32, kind="ExternalInput").ap()
    w2_d = nc.dram_tensor("w2", [8 * 4096, D], F32, kind="ExternalInput").ap()
    b1_d = nc.dram_tensor("b1", [8 * 4096], F32, kind="ExternalInput").ap()
    b2_d = nc.dram_tensor("b2", [8 * D], F32, kind="ExternalInput").ap()
    cmask_d = nc.dram_tensor("cmask", [8, 1], F32, kind="ExternalInput").ap()
    caps = declare_constants(nc)
    w1b_d = nc.dram_tensor("w1b_i", [8 * D, 4096], BF16, kind="Internal").ap()
    w2b_d = nc.dram_tensor("w2b_i", [8 * 4096, D], BF16, kind="Internal").ap()
    xnd = nc.dram_tensor("xn_i", [D, L], BF16, kind="Internal").ap()
    cc_in = nc.dram_tensor("cc_in", [1, 8], F32, kind="Internal").ap()
    cc_out = nc.dram_tensor("cc_out", [8, 8], F32, kind="Internal").ap()
    dbg = nc.dram_tensor("dbg", [16, L], F32, kind="ExternalOutput").ap() if debug else None

    NT = L // 128

    sq = Seq()
    with contextlib.ExitStack() as stack:
        cx = Ctx(nc, sq, stack, caps, psum_scratch=True)
        sb, ps = cx.sb, cx.ps
        sems = [stack.enter_context(nc.semaphore(f"S{i}")) for i in range(12)]
        cc_sems = [stack.enter_context(nc.semaphore("SCC0"))]
        eps_col = cx.const_col(1e-5)
        OP = sq.op
        DMA = sq.dma

        ut128 = sb("ut128_sb", [128, 128], F32)
        DMA("sync", lambda e: e.dma_start(out=ut128[:], in_=caps["ut128"]))
        cmask = sb("cmask_sb", [8, 1], F32)
        DMA("sync", lambda e: e.dma_start(out=cmask[:], in_=cmask_d))
        wrT32 = sb("wrT32", [128, 8 * 8], F32)
        for dc in range(8):
            DMA("sync", lambda e, dc=dc: e.dma_start(
                out=wrT32[:, dc * 8:(dc + 1) * 8], in_=wrT_d[dc * 128:(dc + 1) * 128, :]))

        stgb = sb("stgb", [128, 4096], BF16)
        wr8 = sb("wr8", [8, NT * 128], F32)
        off_row = sb("off_row", [1, 8], F32)
        coff_row = sb("coff_row", [1, 8], F32)
        ccsb = sb("ccsb", [8, 8], F32)
        wrow_lc = sb("wrow_lc", [1, LC], F32)
        wbc = sb("wbc", [128, LC], F32)
        tmpy = sb("tmpy", [128, LC], F32)
        xtmp = sb("xtmp", [128, LC], F32)

        pa = ps("pa", [128, LC], F32)
        pa2 = ps("pa2", [128, LC], F32)
        pwb = ps("pwb", [128, LC], F32)
        plog = ps("plog", [8, LC], F32)
        ppos = ps("ppos", [128, 8], F32)
        prow8 = ps("prow8", [1, 8], F32)
        prow = ps("prow", [1, LC], F32)

        OP("vector", lambda e: e.memset(off_row[:], 0.0))

        with contextlib.ExitStack() as stA:
            sa = lambda name, shape, dt: stA.enter_context(nc.sbuf_tensor(name, shape, dt))
            stage = sa("stage", [128, 4096], F32)
            sqs = sa("sqs", [128, LC], F32)
            rows = sa("rows", [1, 4 * LC], F32)
            mean_r = rows[:, 0:LC]
            msq_r = rows[:, LC:2 * LC]
            A_r = rows[:, 2 * LC:3 * LC]
            B_r = rows[:, 3 * LC:4 * LC]
            lgfm = sa("lgfm", [8, L], F32)
            lgtm = sa("lgtm", [128, NT * 8], F32)
            wtsb = sa("wtsb", [128, NT * 8], F32)
            maskb = sa("maskb", [128, NT * 8], F32)
            poss = sa("poss", [128, NT * 8], F32)
            csA = sa("csA", [128, 8], F32)
            csB = sa("csB", [128, 8], F32)
            e1b = sa("e1b", [128, 8], F32)
            e2b = sa("e2b", [128, 8], F32)
            lg2b = sa("lg2b", [128, 8], F32)
            is1b = sa("is1b", [128, 8], F32)
            is2b = sa("is2b", [128, 8], F32)
            t8 = sa("t8", [128, 8], F32)
            colsx = sa("colsx", [128, 4], F32)

            for e_ in range(8):
                for dc in range(8):
                    r0 = e_ * D + dc * 128
                    DMA("sync", lambda e, r0=r0: e.dma_start(out=stage[:], in_=w1_d[r0:r0 + 128, :]))
                    OP("vector", lambda e: e.tensor_copy(stgb[:], stage[:]))
                    DMA("sync", lambda e, r0=r0: e.dma_start(out=w1b_d[r0:r0 + 128, :], in_=stgb[:]))
                for fc in range(32):
                    r0 = e_ * 4096 + fc * 128
                    DMA("sync", lambda e, r0=r0: e.dma_start(out=stage[:, 0:1024], in_=w2_d[r0:r0 + 128, :]))
                    OP("vector", lambda e: e.tensor_copy(stgb[:, 0:1024], stage[:, 0:1024]))
                    DMA("sync", lambda e, r0=r0: e.dma_start(out=w2b_d[r0:r0 + 128, :], in_=stgb[:, 0:1024]))

            for lc in range(NLC):
                l0 = lc * LC
                for dc in range(ND):
                    DMA("sync", lambda e, dc=dc, l0=l0: e.dma_start(
                        out=stage[:, dc * LC:(dc + 1) * LC],
                        in_=x_in[dc * 128:(dc + 1) * 128, l0:l0 + LC]))
                    OP("tensor", lambda e, dc=dc: e.matmul(
                        prow[:], cx.ones_col_f32[:], stage[:, dc * LC:(dc + 1) * LC],
                        start=(dc == 0), stop=(dc == ND - 1)))
                    OP("scalar", lambda e, dc=dc: e.activation(
                        sqs[:], stage[:, dc * LC:(dc + 1) * LC], AF.Square))
                    OP("tensor", lambda e, dc=dc: e.matmul(
                        pa2[0:1, :], cx.ones_col_f32[:], sqs[:], start=(dc == 0), stop=(dc == ND - 1)))
                OP("vector", lambda e: e.tensor_scalar(mean_r, prow[:], 1.0 / D, None, ALU.mult))
                OP("vector", lambda e: e.tensor_scalar(msq_r, pa2[0:1, :], 1.0 / D, None, ALU.mult))
                OP("vector", lambda e: e.tensor_tensor(A_r, mean_r, mean_r, ALU.mult))
                OP("vector", lambda e: e.tensor_tensor(A_r, msq_r, A_r, ALU.subtract))
                OP("scalar", lambda e: e.activation(A_r, A_r, AF.Sqrt, bias=eps_col[0:1, :]))
                OP("vector", lambda e: e.reciprocal(A_r, A_r))
                OP("vector", lambda e: e.tensor_tensor(B_r, mean_r, A_r, ALU.mult))
                OP("vector", lambda e: e.tensor_scalar(B_r, B_r, -1.0, None, ALU.mult))
                cx.bcast_row(pa[:], A_r, LC)
                cx.bcast_row(pa2[:], B_r, LC)
                for dc in range(ND):
                    OP("vector", lambda e, dc=dc: e.tensor_tensor(
                        sqs[:], stage[:, dc * LC:(dc + 1) * LC], pa[:], ALU.mult))
                    OP("vector", lambda e, dc=dc: e.tensor_tensor(
                        tmpy[:], sqs[:], pa2[:], ALU.add))
                    OP("vector", lambda e, dc=dc: e.tensor_copy(
                        stgb[:, dc * LC:(dc + 1) * LC], tmpy[:]))
                    DMA("sync", lambda e, dc=dc, l0=l0: e.dma_start(
                        out=xnd[dc * 128:(dc + 1) * 128, l0:l0 + LC],
                        in_=stgb[:, dc * LC:(dc + 1) * LC]))
                    OP("tensor", lambda e, dc=dc: e.matmul(
                        plog[:], wrT32[:, dc * 8:(dc + 1) * 8], tmpy[:],
                        start=(dc == 0), stop=(dc == ND - 1)))
                OP("vector", lambda e, l0=l0: e.tensor_copy(lgfm[:, l0:l0 + LC], plog[:]))

            def prefix8(src, dst1, dst2):
                OP("vector", lambda e, src=src: e.tensor_copy(dst1[:, 0:1], src[:, 0:1]))
                OP("vector", lambda e, src=src: e.tensor_tensor(dst1[:, 1:8], src[:, 1:8], src[:, 0:7], ALU.add))
                OP("vector", lambda e: e.tensor_copy(dst2[:, 0:2], dst1[:, 0:2]))
                OP("vector", lambda e: e.tensor_tensor(dst2[:, 2:8], dst1[:, 2:8], dst1[:, 0:6], ALU.add))
                OP("vector", lambda e: e.tensor_copy(dst1[:, 0:4], dst2[:, 0:4]))
                OP("vector", lambda e: e.tensor_tensor(dst1[:, 4:8], dst2[:, 4:8], dst2[:, 0:4], ALU.add))
                return dst1

            m1 = colsx[:, 0:1]
            m2 = colsx[:, 1:2]
            g1 = colsx[:, 2:3]
            g2 = colsx[:, 3:4]
            for t in range(NT):
                lg = lgtm[:, t * 8:(t + 1) * 8]
                OP("tensor", lambda e, t=t: e.transpose(
                    cx.pt[:, 0:8], lgfm[:, t * 128:(t + 1) * 128], cx.ident_f32[0:8, 0:8]))
                OP("vector", lambda e, lg=lg: e.tensor_copy(lg, cx.pt[:, 0:8]))
                OP("vector", lambda e, lg=lg: e.tensor_reduce(m1, lg, AX, ALU.max))
                OP("vector", lambda e, lg=lg: e.tensor_scalar(e1b[:], lg, m1, None, ALU.is_ge))
                cs = prefix8(e1b, csA, csB)
                OP("vector", lambda e, cs=cs: e.tensor_scalar(t8[:], cs[:], 1.0, None, ALU.is_equal))
                OP("vector", lambda e: e.tensor_tensor(is1b[:], t8[:], e1b[:], ALU.mult))
                OP("vector", lambda e, lg=lg: e.scalar_tensor_tensor(
                    lg2b[:], is1b[:], -1e30, lg, ALU.mult, ALU.add))
                OP("vector", lambda e: e.tensor_reduce(m2, lg2b[:], AX, ALU.max))
                OP("vector", lambda e: e.tensor_scalar(e2b[:], lg2b[:], m2, None, ALU.is_ge))
                cs = prefix8(e2b, csA, csB)
                OP("vector", lambda e, cs=cs: e.tensor_scalar(t8[:], cs[:], 1.0, None, ALU.is_equal))
                OP("vector", lambda e: e.tensor_tensor(is2b[:], t8[:], e2b[:], ALU.mult))
                OP("vector", lambda e: e.tensor_tensor(g1, m1, m2, ALU.subtract))
                OP("scalar", lambda e: e.activation(g1, g1, AF.Sigmoid))
                OP("vector", lambda e: e.tensor_scalar(g2, g1, -1.0, 1.0, ALU.mult, ALU.add))
                OP("vector", lambda e: e.tensor_scalar(t8[:], is1b[:], g1, None, ALU.mult))
                OP("vector", lambda e, t=t: e.scalar_tensor_tensor(
                    wtsb[:, t * 8:(t + 1) * 8], is2b[:], g2, t8[:], ALU.mult, ALU.add))
                OP("vector", lambda e, t=t: e.tensor_tensor(
                    maskb[:, t * 8:(t + 1) * 8], is1b[:], is2b[:], ALU.add))
                OP("tensor", lambda e, t=t: e.matmul(
                    ppos[:], ut128[:], maskb[:, t * 8:(t + 1) * 8], start=True, stop=False))
                OP("tensor", lambda e: e.matmul(
                    ppos[:], cx.ones_row_f32[:], off_row[:], start=False, stop=True))
                OP("vector", lambda e, t=t: e.tensor_copy(poss[:, t * 8:(t + 1) * 8], ppos[:]))
                OP("tensor", lambda e, t=t: e.matmul(
                    prow8[:], cx.ones_col_f32[:], maskb[:, t * 8:(t + 1) * 8], start=True, stop=True))
                OP("vector", lambda e: e.tensor_tensor(off_row[:], off_row[:], prow8[:], ALU.add))

            DMA("sync", lambda e: e.dma_start(out=cc_in, in_=off_row[:]))
            sq.cc(lambda e: e.collective_compute(
                "AllGather", ALU.bypass, replica_groups=[list(range(8))],
                ins=[cc_in.opt()], outs=[cc_out.opt()]))
            DMA("sync", lambda e: e.dma_start(out=ccsb[:], in_=cc_out))
            OP("tensor", lambda e: e.matmul(prow8[:], cmask[:], ccsb[:], start=True, stop=True))
            OP("vector", lambda e: e.tensor_copy(coff_row[:], prow8[:]))

            for t in range(NT):
                cx.bcast_row(ppos[:], coff_row[:], 8)
                OP("vector", lambda e, t=t: e.tensor_tensor(
                    t8[:], poss[:, t * 8:(t + 1) * 8], ppos[:], ALU.add))
                OP("vector", lambda e: e.tensor_scalar(t8[:], t8[:], CAP + 0.5, None, ALU.is_le))
                OP("vector", lambda e, t=t: e.tensor_tensor(
                    wtsb[:, t * 8:(t + 1) * 8], wtsb[:, t * 8:(t + 1) * 8], t8[:], ALU.mult))
                OP("vector", lambda e, t=t: e.tensor_scalar(
                    wtsb[:, t * 8:(t + 1) * 8], wtsb[:, t * 8:(t + 1) * 8], 0.5, None, ALU.mult))
                OP("tensor", lambda e, t=t: e.transpose(
                    cx.pt[0:8, 0:128], wtsb[:, t * 8:(t + 1) * 8], cx.ident_f32[:]))
                OP("vector", lambda e, t=t: e.tensor_copy(wr8[:, t * 128:(t + 1) * 128], cx.pt[0:8, 0:128]))

            if debug:
                DMA("sync", lambda e: e.dma_start(out=dbg[0:8, :], in_=lgfm[:]))
                DMA("sync", lambda e: e.dma_start(out=dbg[8:16, :], in_=wr8[:]))

        w1b = sb("w1b", [128, 8 * 4096], BF16)
        w2b = sb("w2b", [128, 32 * D], BF16)
        h1b = sb("h1b", [128, 32 * LC], BF16)
        for e_ in range(8):
            for dc in range(8):
                DMA("sync", lambda e, e_=e_, dc=dc: e.dma_start(
                    out=w1b[:, dc * 4096:(dc + 1) * 4096],
                    in_=w1b_d[e_ * D + dc * 128: e_ * D + (dc + 1) * 128, :]))
            for fc in range(32):
                DMA("sync", lambda e, e_=e_, fc=fc: e.dma_start(
                    out=w2b[:, fc * D:(fc + 1) * D],
                    in_=w2b_d[e_ * 4096 + fc * 128: e_ * 4096 + (fc + 1) * 128, :]))
            b1c = cx.load_vec_cols(b1_d[e_ * 4096:(e_ + 1) * 4096], 4096, f"b1c_{e_}")
            b2c = cx.load_vec_cols(b2_d[e_ * D:(e_ + 1) * D], D, f"b2c_{e_}")
            for lc in range(NLC):
                l0 = lc * LC
                DMA("sync", lambda e, e_=e_, l0=l0: e.dma_start(
                    out=wrow_lc[:], in_=wr8[e_:e_ + 1, l0:l0 + LC]))
                for dc in range(ND):
                    DMA("sync", lambda e, dc=dc, l0=l0: e.dma_start(
                        out=stgb[:, dc * LC:(dc + 1) * LC],
                        in_=xnd[dc * 128:(dc + 1) * 128, l0:l0 + LC]))
                for ft in range(32):
                    for dc in range(ND):
                        OP("tensor", lambda e, ft=ft, dc=dc: e.matmul(
                            pa[:], w1b[:, dc * 4096 + ft * 128: dc * 4096 + ft * 128 + 128],
                            stgb[:, dc * LC:(dc + 1) * LC],
                            start=(dc == 0), stop=(dc == ND - 1)))
                    OP("scalar", lambda e, ft=ft, b1c=b1c: e.activation(
                        h1b[:, ft * LC:(ft + 1) * LC], pa[:], AF.Gelu_apprx_tanh,
                        bias=b1c[:, ft:ft + 1]))
                cx.bcast_row(pwb[:], wrow_lc[:], LC)
                OP("vector", lambda e: e.tensor_copy(wbc[:], pwb[:]))
                for dc2 in range(ND):
                    for ft in range(32):
                        OP("tensor", lambda e, dc2=dc2, ft=ft: e.matmul(
                            pa2[:], w2b[:, ft * D + dc2 * 128: ft * D + dc2 * 128 + 128],
                            h1b[:, ft * LC:(ft + 1) * LC],
                            start=(ft == 0), stop=(ft == 31)))
                    OP("vector", lambda e, dc2=dc2, b2c=b2c: e.scalar_tensor_tensor(
                        tmpy[:], pa2[:], b2c[:, dc2:dc2 + 1], wbc[:], ALU.add, ALU.mult))
                    src_x = x_in if e_ == 0 else x_out
                    DMA("sync", lambda e, dc2=dc2, l0=l0, src_x=src_x: e.dma_start(
                        out=xtmp[:], in_=src_x[dc2 * 128:(dc2 + 1) * 128, l0:l0 + LC]))
                    OP("vector", lambda e: e.tensor_tensor(xtmp[:], xtmp[:], tmpy[:], ALU.add))
                    DMA("sync", lambda e, dc2=dc2, l0=l0: e.dma_start(
                        out=x_out[dc2 * 128:(dc2 + 1) * 128, l0:l0 + LC], in_=xtmp[:]))

        sq.emit(nc, sems, cc_sems)
    return nc


def run_s4(x_fm_shards, weights, prefix="L1_ffn"):
    nc = build_s4()
    hc = host_constants()
    wrT = np.ascontiguousarray(weights[prefix + "_wr"].T)
    w1 = np.ascontiguousarray(weights[prefix + "_w1"].reshape(8 * 1024, 4096))
    w2 = np.ascontiguousarray(weights[prefix + "_w2"].reshape(8 * 4096, 1024))
    b1 = np.ascontiguousarray(weights[prefix + "_b1"].reshape(-1))
    b2 = np.ascontiguousarray(weights[prefix + "_b2"].reshape(-1))
    in_maps = []
    for c in range(N_CORES):
        cm = np.zeros((8, 1), np.float32)
        cm[:c] = 1.0
        m = {"x": x_fm_shards[c], "wrT": wrT, "w1": w1, "w2": w2, "b1": b1,
             "b2": b2, "cmask": cm}
        m.update(hc)
        in_maps.append(m)
    res = run_bass_kernel_spmd(nc, in_maps, core_ids=list(range(N_CORES)))
    return [r["xo"] for r in res.results]


CH = 256
NCH = SEQ // CH
EPS_OUT = 1.1920929e-07
EPS_GATE = 1e-05


def build_s5():
    import contextlib
    nc = bass.Bass("TRN2", target_bir_lowering=False)
    x_in = nc.dram_tensor("x", [D, L], F32, kind="ExternalInput").ap()
    x_out = nc.dram_tensor("xo", [D, L], F32, kind="ExternalOutput").ap()
    w_in_d = nc.dram_tensor("w_in", [D, 4384], F32, kind="ExternalInput").ap()
    convw_d = nc.dram_tensor("conv_w", [2304, 4], F32, kind="ExternalInput").ap()
    convb_d = nc.dram_tensor("conv_b", [2304], F32, kind="ExternalInput").ap()
    dtb_d = nc.dram_tensor("dt_bias", [32], F32, kind="ExternalInput").ap()
    alog_d = nc.dram_tensor("a_log", [32], F32, kind="ExternalInput").ap()
    dskip_d = nc.dram_tensor("d_skip", [32], F32, kind="ExternalInput").ap()
    normw_d = nc.dram_tensor("norm_w", [2048], F32, kind="ExternalInput").ap()
    w_out_d = nc.dram_tensor("w_out", [2048, D], F32, kind="ExternalInput").ap()
    w_fc_d = nc.dram_tensor("w_fc", [D, 4096], F32, kind="ExternalInput").ap()
    w_proj_d = nc.dram_tensor("w_proj", [4096, D], F32, kind="ExternalInput").ap()
    ltm_d = nc.dram_tensor("ltm", [128, 512], F32, kind="ExternalInput").ap()
    caps = declare_constants(nc)
    xi_d = nc.dram_tensor("xi_i", [D, L], BF16, kind="Internal").ap()
    xnm_d = nc.dram_tensor("xnm_i", [D, L], BF16, kind="Internal").ap()
    h_d = nc.dram_tensor("h_i", [D, L], F32, kind="Internal").ap()

    sq = Seq()
    with contextlib.ExitStack() as stack:
        cx = Ctx(nc, sq, stack, caps, psum_scratch=True)
        sb, ps = cx.sb, cx.ps
        sems = [stack.enter_context(nc.semaphore(f"S{i}")) for i in range(12)]
        eps_ln = cx.const_col(1e-5)
        eps_out_c = cx.const_col(EPS_OUT)
        epsg_col = cx.const_col(EPS_GATE)[:]
        OP = sq.op
        DMA = sq.dma

        ptb = ps("ptb", [128, 128], BF16)
        pa = ps("pa", [128, LC], F32)
        pb2 = ps("pb2", [128, LC], F32)
        pcb = ps("pcb", [128, LC], F32)
        py = ps("py", [128, 64], F32)
        pdt = ps("pdt", [32, CH], F32)
        prow = ps("prow", [1, LC], F32)

        def load32col(dram_ap, name):
            row = sb(name + "_row", [1, 32], F32)
            col = sb(name + "_col", [32, 1], F32)
            DMA("sync", lambda e: e.dma_start(out=row[:], in_=dram_ap.rearrange("(a x) -> a x", a=1)))
            OP("tensor", lambda e: e.transpose(cx.pt[0:32, 0:1], row[:], cx.ident_f32[0:1, 0:1]))
            OP("vector", lambda e: e.tensor_copy(col[:], cx.pt[0:32, 0:1]))
            return row, col

        dtb_row, dtb_col = load32col(dtb_d, "dtb")
        alog_row, alog_col = load32col(alog_d, "alog")
        dskip_row, dskip_col = load32col(dskip_d, "dskip")
        negea = sb("negea", [32, 1], F32)
        OP("scalar", lambda e: e.activation(negea[:], alog_col[:], AF.Exp))
        OP("vector", lambda e: e.tensor_scalar(negea[:], negea[:], -1.0, None, ALU.mult))
        dskip_bc = sb("dskip_bc", [128, 32], F32)
        cx.bcast_row(cx.pt[:, 0:32], dskip_row[:], 32)
        OP("vector", lambda e: e.tensor_copy(dskip_bc[:], cx.pt[:, 0:32]))
        convw = sb("convw", [128, 18 * 4], F32)
        for ch in range(18):
            DMA("sync", lambda e, ch=ch: e.dma_start(
                out=convw[:, ch * 4:(ch + 1) * 4], in_=convw_d[ch * 128:(ch + 1) * 128, :]))
        convb = cx.load_vec_cols(convb_d, 2304, "convb")
        ltm = sb("ltm_sb", [128, 512], F32)
        DMA("sync", lambda e: e.dma_start(out=ltm[:], in_=ltm_d))

        with contextlib.ExitStack() as stA:
            sa = lambda name, shape, dt: stA.enter_context(nc.sbuf_tensor(name, shape, dt))
            wstage = sa("wstage", [128, 4384], F32)
            w_inb = cx.load_w_bf16_in(stA, w_in_d, D, 4384, "w_inb", stage=wstage)
            w_outb = cx.load_w_bf16_in(stA, w_out_d, 2048, D, "w_outb", stage=wstage)
            state_sb = sa("state_sb", [128, 32 * 64], F32)
            state_bf = sa("state_bf", [128, 32 * 64], BF16)
            y_tm = sa("y_tm", [128, 2 * 2048], BF16)
            gaten = sa("gaten", [128, 2048], BF16)
            g2buf = wstage[:, 0:2048]
            mcol = sa("mcol", [128, 4], F32)
            xs8 = wstage[:, 0:ND * LC]
            arena = sa("arena32", [128, 2048], F32)
            sqs = sa("sqs", [128, LC], F32)
            mean_r = arena[0:1, 0:LC]
            msq_r = arena[0:1, LC:2 * LC]
            A_r = arena[0:1, 2 * LC:3 * LC]
            B_r = arena[0:1, 3 * LC:4 * LC]
            xnmc = sa("xnmc", [128, ND * CH], BF16)
            z_tm = sa("z_tm", [128, 2 * 2048], BF16)
            xbcf = sa("xbcf", [128, 18 * 260], BF16)
            xh_fm = sa("xh_fm", [128, 16 * CH], BF16)
            B_fm = sa("B_fm", [128, CH], BF16)
            C_fm = sa("C_fm", [128, CH], BF16)
            B_tm = sa("B_tm", [128, 2 * 128], BF16)
            xh_tm = sa("xh_tm", [128, 2 * 2048], BF16)
            upre_sm = sa("upre_sm", [128, 2 * 64], BF16)
            dtr_f = arena[0:32, 0:CH]
            dt_fm = arena[0:32, CH:2 * CH]
            acum = arena[0:32, 2 * CH:3 * CH]
            DSf = arena[0:32, 3 * CH:4 * CH]
            PREf = arena[0:32, 4 * CH:5 * CH]
            EAf = arena[0:32, 5 * CH:6 * CH]
            scr32 = arena[0:32, 6 * CH:7 * CH]
            endc = sa("endc", [32, 2], F32)
            tpose = sa("tpose", [128, 6 * 32], F32)
            acumT = lambda lt: tpose[:, lt * 32:(lt + 1) * 32]
            dtT = lambda lt: tpose[:, 64 + lt * 32: 64 + (lt + 1) * 32]
            PRET = lambda lt: tpose[:, 128 + lt * 32: 128 + (lt + 1) * 32]
            dec_bc = sa("dec_bc", [128, 32], F32)
            arow = sa("arow", [1, CH], F32)
            PTs = sa("PTs", [128, 2 * CH], BF16)
            Csh = sa("Csh", [128, CH], BF16)
            t256a = wstage[:, 2048:2048 + CH]
            t256b = wstage[:, 2048 + CH:2048 + 2 * CH]
            xtc = sa("xtc", [128, CH], BF16)
            xib = sa("xib", [128, LC], BF16)
            normw_row = sa("normw_row", [1, 512], F32)
            normw_bc = sa("normw_bc", [128, 2048], BF16)
            for j in range(4):
                DMA("sync", lambda e, j=j: e.dma_start(
                    out=normw_row[:],
                    in_=normw_d.rearrange("(a x) -> a x", a=1)[0:1, j * 512:(j + 1) * 512]))
                cx.bcast_row(pa[:], normw_row[:], 512)
                OP("vector", lambda e, j=j: e.tensor_copy(normw_bc[:, j * 512:(j + 1) * 512], pa[:]))
            htc = sa("htc", [128, CH], F32)

            for b in range(NB):
                for lc in range(SEQ // LC):
                    l0 = b * SEQ + lc * LC
                    for dc in range(ND):
                        DMA("sync", lambda e, dc=dc, l0=l0: e.dma_start(
                            out=xs8[:, dc * LC:(dc + 1) * LC],
                            in_=x_in[dc * 128:(dc + 1) * 128, l0:l0 + LC]))
                        OP("tensor", lambda e, dc=dc: e.matmul(
                            prow[:], cx.ones_col_f32[:], xs8[:, dc * LC:(dc + 1) * LC],
                            start=(dc == 0), stop=(dc == ND - 1)))
                        OP("scalar", lambda e, dc=dc: e.activation(
                            sqs[:], xs8[:, dc * LC:(dc + 1) * LC], AF.Square))
                        OP("tensor", lambda e, dc=dc: e.matmul(
                            pb2[0:1, :], cx.ones_col_f32[:], sqs[:], start=(dc == 0), stop=(dc == ND - 1)))
                    OP("vector", lambda e: e.tensor_scalar(mean_r, prow[:], 1.0 / D, None, ALU.mult))
                    OP("vector", lambda e: e.tensor_scalar(msq_r, pb2[0:1, :], 1.0 / D, None, ALU.mult))
                    OP("vector", lambda e: e.tensor_tensor(A_r, mean_r, mean_r, ALU.mult))
                    OP("vector", lambda e: e.tensor_tensor(A_r, msq_r, A_r, ALU.subtract))
                    OP("scalar", lambda e: e.activation(A_r, A_r, AF.Sqrt, bias=eps_ln[0:1, :]))
                    OP("vector", lambda e: e.reciprocal(A_r, A_r))
                    OP("vector", lambda e: e.tensor_tensor(B_r, mean_r, A_r, ALU.mult))
                    OP("vector", lambda e: e.tensor_scalar(B_r, B_r, -1.0, None, ALU.mult))
                    cx.bcast_row(pa[:], A_r, LC)
                    cx.bcast_row(pb2[:], B_r, LC)
                    for dc in range(ND):
                        OP("vector", lambda e, dc=dc: e.tensor_tensor(
                            sqs[:], xs8[:, dc * LC:(dc + 1) * LC], pa[:], ALU.mult))
                        OP("vector", lambda e, dc=dc: e.tensor_tensor(
                            xs8[:, dc * LC:(dc + 1) * LC], sqs[:], pb2[:], ALU.add))
                        OP("vector", lambda e, dc=dc: e.tensor_copy(
                            xib[:], xs8[:, dc * LC:(dc + 1) * LC]))
                        DMA("sync", lambda e, dc=dc, l0=l0: e.dma_start(
                            out=xi_d[dc * 128:(dc + 1) * 128, l0:l0 + LC],
                            in_=xib[:]))
                        OP("scalar", lambda e, dc=dc: e.activation(
                            sqs[:], xs8[:, dc * LC:(dc + 1) * LC], AF.Square))
                        OP("tensor", lambda e, dc=dc: e.matmul(
                            prow[:], cx.ones_col_f32[:], sqs[:], start=(dc == 0), stop=(dc == ND - 1)))
                    OP("vector", lambda e: e.tensor_scalar(A_r, prow[:], 1.0 / D, None, ALU.mult))
                    OP("scalar", lambda e: e.activation(A_r, A_r, AF.Sqrt, bias=eps_out_c[0:1, :]))
                    OP("vector", lambda e: e.reciprocal(A_r, A_r))
                    cx.bcast_row(pa[:], A_r, LC)
                    for dc in range(ND):
                        OP("vector", lambda e, dc=dc: e.tensor_tensor(
                            sqs[:], xs8[:, dc * LC:(dc + 1) * LC], pa[:], ALU.mult))
                        OP("vector", lambda e: e.tensor_copy(xib[:], sqs[:]))
                        DMA("sync", lambda e, dc=dc, l0=l0: e.dma_start(
                            out=xnm_d[dc * 128:(dc + 1) * 128, l0:l0 + LC],
                            in_=xib[:]))
                OP("vector", lambda e: e.memset(state_sb[:], 0.0))
                OP("vector", lambda e: e.memset(state_bf[:], 0.0))

                for c in range(NCH):
                    l0 = b * SEQ + c * CH
                    for dc in range(ND):
                        DMA("sync", lambda e, dc=dc, l0=l0: e.dma_start(
                            out=xnmc[:, dc * CH:(dc + 1) * CH],
                            in_=xnm_d[dc * 128:(dc + 1) * 128, l0:l0 + CH]))
                    for lt in range(2):
                        for zb in range(4):
                            for dc in range(ND):
                                OP("tensor", lambda e, lt=lt, zb=zb, dc=dc: e.matmul(
                                    pa[:], xnmc[:, dc * CH + lt * 128: dc * CH + lt * 128 + 128],
                                    w_inb[:, dc * 4384 + zb * 512: dc * 4384 + (zb + 1) * 512],
                                    start=(dc == 0), stop=(dc == ND - 1)))
                            OP("scalar", lambda e, lt=lt, zb=zb: e.activation(
                                z_tm[:, lt * 2048 + zb * 512: lt * 2048 + (zb + 1) * 512],
                                pa[:], AF.Silu))
                    if c == 0:
                        for ch in range(18):
                            OP("vector", lambda e, ch=ch: e.memset(xbcf[:, ch * 260:ch * 260 + 3], 0.0))
                    for ch in range(18):
                        for dc in range(ND):
                            OP("tensor", lambda e, ch=ch, dc=dc: e.matmul(
                                pcb[:, 0:CH],
                                w_inb[:, dc * 4384 + 2048 + ch * 128: dc * 4384 + 2048 + (ch + 1) * 128],
                                xnmc[:, dc * CH:(dc + 1) * CH],
                                start=(dc == 0), stop=(dc == ND - 1)))
                        OP("vector", lambda e, ch=ch: e.tensor_copy(
                            xbcf[:, ch * 260 + 3:ch * 260 + 259], pcb[:, 0:CH]))
                    for ch in range(18):
                        base = ch * 260
                        OP("vector", lambda e, base=base, ch=ch: e.tensor_scalar(
                            t256a[:], xbcf[:, base + 3:base + 259], convw[:, ch * 4 + 3:ch * 4 + 4],
                            None, ALU.mult))
                        for k in range(3):
                            src_t = t256a if k % 2 == 0 else t256b
                            dst_t = t256b if k % 2 == 0 else t256a
                            OP("vector", lambda e, base=base, ch=ch, k=k, src_t=src_t, dst_t=dst_t:
                               e.scalar_tensor_tensor(
                                   dst_t[:], xbcf[:, base + k:base + k + 256],
                                   convw[:, ch * 4 + k:ch * 4 + k + 1], src_t[:],
                                   ALU.mult, ALU.add))
                        dst = (xh_fm[:, (ch) * CH:(ch + 1) * CH] if ch < 16 else
                               (B_fm[:] if ch == 16 else C_fm[:]))
                        OP("scalar", lambda e, dst=dst, ch=ch: e.activation(
                            dst, t256b[:], AF.Silu, bias=convb[:, ch:ch + 1]))
                        OP("vector", lambda e, base=base: e.tensor_copy(
                            xbcf[:, base:base + 3], xbcf[:, base + 256:base + 259]))
                    for dc in range(ND):
                        OP("tensor", lambda e, dc=dc: e.matmul(
                            pdt[:], w_inb[:, dc * 4384 + 4352: dc * 4384 + 4384],
                            xnmc[:, dc * CH:(dc + 1) * CH],
                            start=(dc == 0), stop=(dc == ND - 1)))
                    OP("vector", lambda e: e.tensor_scalar(dtr_f, pdt[:], dtb_col[:], None, ALU.add))
                    OP("scalar", lambda e: e.activation(scr32[:], dtr_f, AF.Abs))
                    OP("scalar", lambda e: e.activation(scr32[:], scr32[:], AF.Exp, scale=-1.0))
                    OP("scalar", lambda e: e.activation(scr32[:], scr32[:], AF.Ln, bias=1.0))
                    OP("vector", lambda e: e.tensor_scalar(dt_fm, dtr_f, 0.0, None, ALU.max))
                    OP("vector", lambda e: e.tensor_tensor(dt_fm, dt_fm, scr32[:], ALU.add))
                    OP("vector", lambda e: e.tensor_scalar(acum, dt_fm, negea[:], None, ALU.mult))
                    OP("vector", lambda e: e.tensor_tensor_scan(acum, acum, acum, 0.0, ALU.add, ALU.bypass))
                    OP("vector", lambda e: e.tensor_copy(endc[:, 0:1], acum[:, CH - 1:CH]))
                    OP("vector", lambda e: e.tensor_scalar(DSf, acum, endc[:, 0:1], None, ALU.subtract))
                    OP("scalar", lambda e: e.activation(DSf, DSf, AF.Exp, scale=-1.0))
                    OP("vector", lambda e: e.tensor_tensor(PREf, dt_fm, DSf, ALU.mult))
                    OP("scalar", lambda e: e.activation(EAf, acum, AF.Exp))
                    OP("scalar", lambda e: e.activation(endc[:, 1:2], endc[:, 0:1], AF.Exp))
                    OP("tensor", lambda e: e.transpose(cx.pt[0:1, 0:32], endc[:, 1:2], cx.ident_f32[0:32, 0:32]))
                    OP("vector", lambda e: e.tensor_copy(arow[:, 0:32], cx.pt[0:1, 0:32]))
                    cx.bcast_row(cx.pt[:, 0:32], arow[:, 0:32], 32)
                    OP("vector", lambda e: e.tensor_copy(dec_bc[:], cx.pt[:, 0:32]))
                    for (off, dstf) in ((2 * CH, acumT), (CH, dtT), (4 * CH, PRET)):
                        for lt in range(2):
                            OP("tensor", lambda e, off=off, lt=lt: e.transpose(
                                cx.pt[:, 0:32],
                                arena[0:32, off + lt * 128: off + (lt + 1) * 128],
                                cx.ident_f32[0:32, 0:32]))
                            OP("vector", lambda e, dstf=dstf, lt=lt: e.tensor_copy(
                                dstf(lt), cx.pt[:, 0:32]))
                    for ch in range(16):
                        for lt in range(2):
                            OP("tensor", lambda e, ch=ch, lt=lt: e.transpose(
                                ptb[:], xh_fm[:, ch * CH + lt * 128: ch * CH + lt * 128 + 128],
                                cx.ident_bf16[:]))
                            OP("vector", lambda e, ch=ch, lt=lt: e.tensor_copy(
                                xh_tm[:, lt * 2048 + ch * 128: lt * 2048 + (ch + 1) * 128], ptb[:]))
                    for st in range(2):
                        OP("tensor", lambda e, st=st: e.transpose(
                            ptb[:], B_fm[:, st * 128:(st + 1) * 128], cx.ident_bf16[:]))
                        OP("vector", lambda e, st=st: e.tensor_copy(
                            B_tm[:, st * 128:(st + 1) * 128], ptb[:]))
                    for st in range(2):
                        OP("tensor", lambda e, st=st: e.matmul(
                            pcb[:, st * CH:(st + 1) * CH], B_fm[:, st * 128:(st + 1) * 128],
                            C_fm[:], start=True, stop=True))
                    for h in range(32):
                        DMA("sync", lambda e, h=h: e.dma_start(
                            out=arow[:], in_=arena[h:h + 1, 2 * CH:3 * CH]))
                        for st in range(2):
                            cx.bcast_row(pb2[:, 0:CH], arow[:], CH)
                            OP("vector", lambda e, st=st, h=h: e.tensor_scalar(
                                t256a[:], pb2[:, 0:CH], acumT(st)[:, h:h + 1], None, ALU.subtract))
                            OP("vector", lambda e: e.tensor_scalar(
                                t256a[:], t256a[:], 0.0, None, ALU.min))
                            OP("scalar", lambda e: e.activation(t256a[:], t256a[:], AF.Exp))
                            OP("vector", lambda e, st=st: e.tensor_tensor(
                                t256b[:], ltm[:, st * CH:(st + 1) * CH], pcb[:, st * CH:(st + 1) * CH],
                                ALU.mult))
                            OP("vector", lambda e: e.tensor_tensor(t256b[:], t256b[:], t256a[:], ALU.mult))
                            OP("vector", lambda e, st=st, h=h: e.tensor_scalar(
                                PTs[:, st * CH:(st + 1) * CH], t256b[:], dtT(st)[:, h:h + 1],
                                None, ALU.mult))
                        DMA("sync", lambda e, h=h: e.dma_start(
                            out=arow[:], in_=arena[h:h + 1, 5 * CH:6 * CH]))
                        cx.bcast_row(pb2[:, 0:CH], arow[:], CH)
                        OP("vector", lambda e: e.tensor_tensor(Csh[:], C_fm[:], pb2[:, 0:CH], ALU.mult))
                        for lt in range(2):
                            for st in range(2):
                                OP("tensor", lambda e, lt=lt, st=st, h=h: e.matmul(
                                    py[:], PTs[:, st * CH + lt * 128: st * CH + lt * 128 + 128],
                                    xh_tm[:, st * 2048 + h * 64: st * 2048 + (h + 1) * 64],
                                    start=(st == 0), stop=False))
                            OP("tensor", lambda e, lt=lt, h=h: e.matmul(
                                py[:], Csh[:, lt * 128:(lt + 1) * 128],
                                state_bf[:, h * 64:(h + 1) * 64], start=False, stop=True))
                            OP("vector", lambda e, lt=lt, h=h: e.scalar_tensor_tensor(
                                y_tm[:, lt * 2048 + h * 64: lt * 2048 + (h + 1) * 64],
                                xh_tm[:, lt * 2048 + h * 64: lt * 2048 + (h + 1) * 64],
                                dskip_bc[:, h:h + 1], py[:], ALU.mult, ALU.add))
                        for st in range(2):
                            OP("vector", lambda e, st=st, h=h: e.tensor_scalar(
                                upre_sm[:, st * 64:(st + 1) * 64],
                                xh_tm[:, st * 2048 + h * 64: st * 2048 + (h + 1) * 64],
                                PRET(st)[:, h:h + 1], None, ALU.mult))
                        for st in range(2):
                            OP("tensor", lambda e, st=st, h=h: e.matmul(
                                pb2[:, 0:64], B_tm[:, st * 128:(st + 1) * 128],
                                upre_sm[:, st * 64:(st + 1) * 64],
                                start=(st == 0), stop=(st == 1)))
                        OP("vector", lambda e, h=h: e.scalar_tensor_tensor(
                            state_sb[:, h * 64:(h + 1) * 64], state_sb[:, h * 64:(h + 1) * 64],
                            dec_bc[:, h:h + 1], pb2[:, 0:64], ALU.mult, ALU.add))
                        OP("vector", lambda e, h=h: e.tensor_copy(
                            state_bf[:, h * 64:(h + 1) * 64], state_sb[:, h * 64:(h + 1) * 64]))
                    for lt in range(2):
                        OP("vector", lambda e, lt=lt: e.tensor_tensor(
                            g2buf[:], y_tm[:, lt * 2048:(lt + 1) * 2048],
                            z_tm[:, lt * 2048:(lt + 1) * 2048], ALU.mult))
                        OP("scalar", lambda e: e.activation(
                            gaten[:], g2buf[:], AF.Square, accum_out=mcol[:, 0:1]))
                        OP("vector", lambda e: e.tensor_scalar(mcol[:, 1:2], mcol[:, 0:1], 1.0 / 2048, None, ALU.mult))
                        OP("scalar", lambda e: e.activation(mcol[:, 1:2], mcol[:, 1:2], AF.Sqrt,
                                                            bias=epsg_col))
                        OP("vector", lambda e: e.reciprocal(mcol[:, 1:2], mcol[:, 1:2]))
                        OP("vector", lambda e: e.scalar_tensor_tensor(
                            gaten[:], g2buf[:], mcol[:, 1:2],
                            normw_bc[:], ALU.mult, ALU.mult))
                        for ch in range(16):
                            OP("tensor", lambda e, ch=ch: e.transpose(
                                ptb[:], gaten[:, ch * 128:(ch + 1) * 128],
                                cx.ident_bf16[:]))
                            OP("vector", lambda e, ch=ch, lt=lt: e.tensor_copy(
                                xh_fm[:, ch * CH + lt * 128: ch * CH + lt * 128 + 128], ptb[:]))
                    for dcO in range(ND):
                        for ch in range(16):
                            OP("tensor", lambda e, dcO=dcO, ch=ch: e.matmul(
                                pa[:, 0:CH], w_outb[:, ch * D + dcO * 128: ch * D + (dcO + 1) * 128],
                                xh_fm[:, ch * CH:(ch + 1) * CH],
                                start=(ch == 0), stop=(ch == 15)))
                        DMA("sync", lambda e, dcO=dcO, l0=l0: e.dma_start(
                            out=xtc[:], in_=xi_d[dcO * 128:(dcO + 1) * 128, l0:l0 + CH]))
                        OP("vector", lambda e, dcO=dcO: e.tensor_tensor(htc[:], xtc[:], pa[:, 0:CH], ALU.add))
                        DMA("sync", lambda e, dcO=dcO, l0=l0: e.dma_start(
                            out=h_d[dcO * 128:(dcO + 1) * 128, l0:l0 + CH], in_=htc[:]))

        with contextlib.ExitStack() as stB:
            sbb = lambda name, shape, dt: stB.enter_context(nc.sbuf_tensor(name, shape, dt))
            stageB = sbb("stageB", [128, 4096], F32)
            w_fcb = cx.load_w_bf16_in(stB, w_fc_d, D, 4096, "w_fcb", stage=stageB)
            w_prb = cx.load_w_bf16_in(stB, w_proj_d, 4096, D, "w_prb", stage=stageB)
            hs8 = stageB[:, 0:ND * LC]
            hnb = sbb("hnb", [128, ND * LC], BF16)
            r1 = sbb("r1", [128, LC], BF16)
            r2 = sbb("r2", [128, 32 * LC], BF16)
            sq2 = sbb("sq2", [128, LC], F32)
            rowB = sbb("rowB", [1, LC], F32)
            xt5 = sbb("xt5", [128, LC], F32)
            for lc in range(NLC):
                l0 = lc * LC
                for dc in range(ND):
                    DMA("sync", lambda e, dc=dc, l0=l0: e.dma_start(
                        out=hs8[:, dc * LC:(dc + 1) * LC],
                        in_=h_d[dc * 128:(dc + 1) * 128, l0:l0 + LC]))
                    OP("scalar", lambda e, dc=dc: e.activation(
                        sq2[:], hs8[:, dc * LC:(dc + 1) * LC], AF.Square))
                    OP("tensor", lambda e, dc=dc: e.matmul(
                        prow[:], cx.ones_col_f32[:], sq2[:], start=(dc == 0), stop=(dc == ND - 1)))
                OP("vector", lambda e: e.tensor_scalar(rowB[:], prow[:], 1.0 / D, None, ALU.mult))
                OP("scalar", lambda e: e.activation(rowB[:], rowB[:], AF.Sqrt, bias=eps_out_c[0:1, :]))
                OP("vector", lambda e: e.reciprocal(rowB[:], rowB[:]))
                cx.bcast_row(pa[:], rowB[:], LC)
                for dc in range(ND):
                    OP("vector", lambda e, dc=dc: e.tensor_tensor(
                        hnb[:, dc * LC:(dc + 1) * LC], hs8[:, dc * LC:(dc + 1) * LC],
                        pa[:], ALU.mult))
                for ft in range(32):
                    for dc in range(ND):
                        OP("tensor", lambda e, ft=ft, dc=dc: e.matmul(
                            pa[:], w_fcb[:, dc * 4096 + ft * 128: dc * 4096 + ft * 128 + 128],
                            hnb[:, dc * LC:(dc + 1) * LC],
                            start=(dc == 0), stop=(dc == ND - 1)))
                    OP("scalar", lambda e, ft=ft: e.activation(r1[:], pa[:], AF.Relu))
                    OP("scalar", lambda e, ft=ft: e.activation(
                        r2[:, ft * LC:(ft + 1) * LC], r1[:], AF.Square))
                for dc2 in range(ND):
                    for ft in range(32):
                        OP("tensor", lambda e, dc2=dc2, ft=ft: e.matmul(
                            pb2[:], w_prb[:, ft * D + dc2 * 128: ft * D + dc2 * 128 + 128],
                            r2[:, ft * LC:(ft + 1) * LC],
                            start=(ft == 0), stop=(ft == 31)))
                    DMA("sync", lambda e, dc2=dc2, l0=l0: e.dma_start(
                        out=xt5[:], in_=x_in[dc2 * 128:(dc2 + 1) * 128, l0:l0 + LC]))
                    OP("vector", lambda e, dc2=dc2: e.tensor_tensor(
                        xt5[:], xt5[:], hs8[:, dc2 * LC:(dc2 + 1) * LC], ALU.add))
                    OP("vector", lambda e: e.tensor_tensor(xt5[:], xt5[:], pb2[:], ALU.add))
                    DMA("sync", lambda e, dc2=dc2, l0=l0: e.dma_start(
                        out=x_out[dc2 * 128:(dc2 + 1) * 128, l0:l0 + LC], in_=xt5[:]))

        sq.emit(nc, sems)
    return nc


def host_ltm():
    ltm = np.zeros((128, 512), np.float32)
    for st in range(2):
        for p in range(128):
            s = st * 128 + p
            ltm[p, st * 256 + s:st * 256 + 256] = 1.0
    return ltm


def run_s5(x_fm_shards, weights):
    nc = build_s5()
    hc = host_constants()
    in_maps = []
    for c in range(N_CORES):
        m = {"x": x_fm_shards[c],
             "w_in": weights["L1_attn_w_in"], "conv_w": weights["L1_attn_conv_w"],
             "conv_b": weights["L1_attn_conv_b"], "dt_bias": weights["L1_attn_dt_bias"],
             "a_log": weights["L1_attn_a_log"], "d_skip": weights["L1_attn_d_skip"],
             "norm_w": weights["L1_attn_norm_w"], "w_out": weights["L1_attn_w_out"],
             "w_fc": weights["L1_attn_w_fc"], "w_proj": weights["L1_attn_w_proj"],
             "ltm": host_ltm()}
        m.update(hc)
        in_maps.append(m)
    res = run_bass_kernel_spmd(nc, in_maps, core_ids=list(range(N_CORES)))
    return [r["xo"] for r in res.results]


# version 03
# speedup vs baseline: 1.0272x; 1.0272x over previous
import numpy as np
import concourse.bass as bass
import concourse.mybir as mybir
from concourse.bass_utils import run_bass_kernel_spmd

F32 = mybir.dt.float32
BF16 = mybir.dt.bfloat16
AF = mybir.ActivationFunctionType
ALU = mybir.AluOpType
AX = mybir.AxisListType.X

N_CORES = 8
D = 1024
L = 4096
SEQ = 2048
NB = 2
LC = 512
NLC = L // LC
ND = D // 128


class Seq:

    def __init__(self):
        self.ops = []

    def op(self, eng, fn, inc=1):
        assert not getattr(self, "frozen", False), "Seq.op after emit started"
        self.ops.append((eng, fn, inc, False))

    def dma(self, eng, fn):
        assert not getattr(self, "frozen", False), "Seq.dma after emit started"
        self.ops.append((eng, fn, 16, False))

    def cc(self, fn):
        assert not getattr(self, "frozen", False)
        idx = getattr(self, "n_cc", 0)
        self.n_cc = idx + 1
        self.ops.append(("gpsimd", fn, 0, ("cc", idx)))

    EPOCH_LIMIT = 24000

    def emit(self, nc, sems, cc_sems=()):
        self.frozen = True
        assert getattr(self, "n_cc", 0) <= len(cc_sems)
        n = len(self.ops)
        epoch = []
        epoch_cum = []
        ep, cum = 0, 0
        for (eng, fn, inc, vl) in self.ops:
            if cum + inc > self.EPOCH_LIMIT:
                ep += 1
                cum = 0
            epoch.append(ep)
            epoch_cum.append(cum)
            cum += inc
        n_ep = ep + 1
        assert n_ep <= len(sems), f"need {n_ep} semaphores, have {len(sems)}"
        totals = [0] * n_ep
        for i, (eng, fn, inc, vl) in enumerate(self.ops):
            totals[epoch[i]] = epoch_cum[i] + inc

        def section(engobj, name):
            last_ep_seen = -1
            for i, (eng, fn, inc, vl) in enumerate(self.ops):
                if eng != name:
                    continue
                ep_i = epoch[i]
                if ep_i != last_ep_seen:
                    for k in range(last_ep_seen if last_ep_seen >= 0 else 0, ep_i):
                        engobj.wait_ge(sems[k], totals[k])
                    last_ep_seen = ep_i
                if epoch_cum[i] > 0:
                    engobj.wait_ge(sems[ep_i], epoch_cum[i])
                if i > 0 and isinstance(self.ops[i - 1][3], tuple):
                    engobj.wait_ge(cc_sems[self.ops[i - 1][3][1]], 1)
                r = fn(engobj)
                if isinstance(vl, tuple):
                    r.then_inc(cc_sems[vl[1]])
                elif inc > 0:
                    r.then_inc(sems[ep_i], inc)
            if name == "tensor":
                for k in range(n_ep):
                    engobj.wait_ge(sems[k], totals[k])

        with nc.Block() as block:
            @block.sync
            def _(e):
                section(e, "sync")

            @block.vector
            def _(e):
                section(e, "vector")

            @block.scalar
            def _(e):
                section(e, "scalar")

            @block.tensor
            def _(e):
                section(e, "tensor")

            @block.gpsimd
            def _(e):
                section(e, "gpsimd")


def host_constants():
    c = {}
    c["ident_bf16"] = np.eye(128, dtype=np.float32)
    import ml_dtypes
    c["ident_bf16"] = np.eye(128, dtype=ml_dtypes.bfloat16)
    c["ident_f32"] = np.eye(128, dtype=np.float32)
    c["ones_row_f32"] = np.ones((1, 128), dtype=np.float32)
    c["ones_row_bf16"] = np.ones((1, 128), dtype=ml_dtypes.bfloat16)
    c["ones_col_f32"] = np.ones((128, 1), dtype=np.float32)
    c["ones_col_bf16"] = np.ones((128, 1), dtype=ml_dtypes.bfloat16)
    c["ut128"] = np.triu(np.ones((128, 128), dtype=np.float32))
    return c


def declare_constants(nc):
    aps = {}
    aps["ident_bf16"] = nc.dram_tensor("ident_bf16", [128, 128], BF16, kind="ExternalInput").ap()
    aps["ident_f32"] = nc.dram_tensor("ident_f32", [128, 128], F32, kind="ExternalInput").ap()
    aps["ones_row_f32"] = nc.dram_tensor("ones_row_f32", [1, 128], F32, kind="ExternalInput").ap()
    aps["ones_row_bf16"] = nc.dram_tensor("ones_row_bf16", [1, 128], BF16, kind="ExternalInput").ap()
    aps["ones_col_f32"] = nc.dram_tensor("ones_col_f32", [128, 1], F32, kind="ExternalInput").ap()
    aps["ones_col_bf16"] = nc.dram_tensor("ones_col_bf16", [128, 1], BF16, kind="ExternalInput").ap()
    aps["ut128"] = nc.dram_tensor("ut128", [128, 128], F32, kind="ExternalInput").ap()
    return aps


class Ctx:

    def __init__(self, nc, sq, stack, caps, psum_scratch=True, prefix=""):
        self.nc = nc
        self.sq = sq
        self.P = prefix
        sb = lambda name, shape, dt: stack.enter_context(nc.sbuf_tensor(prefix + name, shape, dt))
        ps = lambda name, shape, dt: stack.enter_context(nc.psum_tensor(prefix + name, shape, dt))
        self.sb = sb
        self.ps = ps
        self.ident_bf16 = sb("c_ident_b", [128, 128], BF16)
        self.ident_f32 = sb("c_ident_f", [128, 128], F32)
        self.ones_row_f32 = sb("c_or_f", [1, 128], F32)
        self.ones_row_bf16 = sb("c_or_b", [1, 128], BF16)
        self.ones_col_f32 = sb("c_oc_f", [128, 1], F32)
        self.ones_col_bf16 = sb("c_oc_b", [128, 1], BF16)
        self.consts = sb("c_consts", [128, 8], F32)
        self.n_consts = 0
        self.const_map = {}
        for name in ["ident_bf16", "ident_f32", "ones_row_f32", "ones_row_bf16",
                     "ones_col_f32", "ones_col_bf16"]:
            dst = getattr(self, name)
            src = caps[name]
            self.sq.dma("sync", lambda e, d=dst, s=src: e.dma_start(out=d[:], in_=s))
        if psum_scratch:
            self.pt = ps("p_tr", [128, 128], F32)

    def const_col(self, val):
        if val in self.const_map:
            return self.const_map[val]
        i = self.n_consts
        assert i < 8
        self.n_consts += 1
        col = self.consts[:, i:i + 1]
        self.sq.op("vector", lambda e, c=col, v=val: e.memset(c, v))
        self.const_map[val] = col
        return col

    def load_vec_cols(self, dram_ap, n, name, dt=F32):
        k = n // 128
        assert k * 128 == n and k <= 128
        stage = self.sb(name + "_stg", [max(k, 1), 128], F32)
        out = self.sb(name, [128, k], dt)
        self.sq.dma("sync", lambda e: e.dma_start(
            out=stage[:], in_=dram_ap.rearrange("(k p) -> k p", k=k)))
        self.sq.op("tensor", lambda e: e.transpose(self.pt[:, 0:k], stage[:], self.ident_f32[0:k, 0:k]))
        self.sq.op("vector", lambda e: e.tensor_copy(out[:], self.pt[:, 0:k]))
        return out

    def load_scalar(self, dram_ap, name):
        out = self.sb(name + "_sb", [1, 1], F32)
        self.sq.dma("sync", lambda e: e.dma_start(out=out[:], in_=dram_ap.rearrange("(a x) -> a x", a=1)))
        return out

    def load_w_bf16_in(self, stack, dram_ap, din, dout, name, stage=None):
        ndc = din // 128
        if stage is None:
            stage = stack.enter_context(self.nc.sbuf_tensor(self.P + name + "_stg", [128, dout], F32))
        out = stack.enter_context(self.nc.sbuf_tensor(self.P + name, [128, ndc * dout], BF16))
        for dc in range(ndc):
            self.sq.dma("sync", lambda e, dc=dc: e.dma_start(
                out=stage[:, 0:dout], in_=dram_ap[dc * 128:(dc + 1) * 128, :]))
            self.sq.op("vector", lambda e, dc=dc: e.tensor_copy(
                out[:, dc * dout:(dc + 1) * dout], stage[:, 0:dout]))
        return out

    def load_w_bf16(self, dram_ap, din, dout, name):
        ndc = din // 128
        stage = self.sb(name + "_stg", [128, dout], F32)
        out = self.sb(name, [128, ndc * dout], BF16)
        for dc in range(ndc):
            self.sq.dma("sync", lambda e, dc=dc: e.dma_start(
                out=stage[:], in_=dram_ap[dc * 128:(dc + 1) * 128, :]))
            self.sq.op("vector", lambda e, dc=dc: e.tensor_copy(
                out[:, dc * dout:(dc + 1) * dout], stage[:]))
        return out

    def bcast_row(self, psum_ap, row_ap, n, dt="f32", m=128):
        ones = self.ones_row_f32 if dt == "f32" else self.ones_row_bf16
        self.sq.op("tensor", lambda e: e.matmul(psum_ap, ones[:, 0:m], row_ap, start=True, stop=True))


def declare_w_s1(nc):
    w = {}
    for nm, shp in [("cn_w1", [D, D]), ("cn_w2", [D, 512]), ("cn_w3", [512, 256]),
                    ("cn_w4", [256, 1]), ("gc_w1", [D, 512]), ("gc_w2", [512, 256]),
                    ("gc_w3", [256, 1])]:
        w[nm] = nc.dram_tensor(nm, shp, F32, kind="ExternalInput").ap()
    for nm, n in [("cn_b1", D), ("cn_b2", 512), ("cn_b3", 256), ("gc_b1", 512),
                  ("gc_b2", 256)]:
        w[nm] = nc.dram_tensor(nm, [n], F32, kind="ExternalInput").ap()
    for nm in ["cn_b4", "gc_b3", "ar"]:
        w[nm] = nc.dram_tensor(nm, [1], F32, kind="ExternalInput").ap()
    return w


def build_s1(n_lc=NLC, n_cyc=3, debug=False, fctx=None):
    import contextlib
    if fctx is None:
        nc = bass.Bass("TRN2", target_bir_lowering=False)
        x_in = nc.dram_tensor("x", [D, L], F32, kind="ExternalInput").ap()
        x_out = nc.dram_tensor("xo", [D, L], F32, kind="ExternalOutput").ap()
        w = declare_w_s1(nc)
        caps = declare_constants(nc)
        sq = Seq()
        P = ""
        own = True
    else:
        nc, sq, caps, P, x_in, x_out, w = (fctx["nc"], fctx["sq"], fctx["caps"],
                                           fctx["P"], fctx["x_in"], fctx["x_out"], fctx["w"])
        own = False
    dbg = nc.dram_tensor("dbg", [8, L], F32, kind="ExternalOutput").ap() if debug else None

    with contextlib.ExitStack() as stack:
        cx = Ctx(nc, sq, stack, caps, prefix=P)
        sb, ps = cx.sb, cx.ps
        if own:
            sems = [stack.enter_context(nc.semaphore(f"S{i}")) for i in range(12)]

        w1 = cx.load_w_bf16(w["cn_w1"], D, D, "w1")
        w2 = cx.load_w_bf16(w["cn_w2"], D, 512, "w2")
        w3 = cx.load_w_bf16(w["cn_w3"], 512, 256, "w3")
        w4 = cx.load_w_bf16(w["cn_w4"], 256, 1, "w4")
        g1w = cx.load_w_bf16(w["gc_w1"], D, 512, "g1w")
        g2w = cx.load_w_bf16(w["gc_w2"], 512, 256, "g2w")
        g3w = cx.load_w_bf16(w["gc_w3"], 256, 1, "g3w")
        b1 = cx.load_vec_cols(w["cn_b1"], D, "b1")
        b2 = cx.load_vec_cols(w["cn_b2"], 512, "b2")
        b3 = cx.load_vec_cols(w["cn_b3"], 256, "b3")
        gb1 = cx.load_vec_cols(w["gc_b1"], 512, "gb1")
        gb2 = cx.load_vec_cols(w["gc_b2"], 256, "gb2")
        b4 = cx.load_scalar(w["cn_b4"], "b4")
        gb3 = cx.load_scalar(w["gc_b3"], "gb3")
        ar = cx.load_scalar(w["ar"], "ar")

        outb = sb("outb", [128, ND * L], BF16)
        h1b = sb("h1b", [128, 8 * LC], BF16)
        h2b = sb("h2b", [128, 4 * LC], BF16)
        h3b = sb("h3b", [128, 2 * LC], BF16)
        xs8 = sb("xs8", [128, ND * LC], F32)
        sqs = sb("sqs", [128, LC], F32)
        rows = sb("rows", [1, 4 * LC], F32)
        mean_r = rows[:, 0 * LC:1 * LC]
        msq_r = rows[:, 1 * LC:2 * LC]
        A_r = rows[:, 2 * LC:3 * LC]
        B_r = rows[:, 3 * LC:4 * LC]
        tf_r = sb("tf_r", [1, LC], F32)
        gc = sb("gc", [128, 2 * ND], F32)
        gcb = sb("gcb", [128, 2 * ND], BF16)
        g1o = sb("g1o", [128, 2 * 4], BF16)
        g2o = sb("g2o", [128, 2 * 2], BF16)
        gf = sb("gf", [1, 2], F32)
        s_ap = sb("s_ap", [1, 1], F32)
        o_ap = sb("o_ap", [1, 2], F32)
        tmp1 = sb("tmp1", [1, 2], F32)
        twt = sb("twt", [128, LC], F32)
        twu = sb("twu", [128, LC], F32)

        eps_col = cx.const_col(1e-5)
        pa = ps("pa", [128, LC], F32)
        pb = ps("pb", [128, LC], F32)
        prow = ps("prow", [1, LC], F32)
        prow2 = ps("prow2", [1, LC], F32)
        pg = ps("pg", [128, 1], F32)

        OP = sq.op
        DMA = sq.dma

        for lc in range(n_lc):
            l0 = lc * LC
            b = 0 if lc < NLC // 2 else 1
            for dc in range(ND):
                DMA("sync", lambda e, dc=dc, l0=l0: e.dma_start(
                    out=xs8[:, dc * LC:(dc + 1) * LC],
                    in_=x_in[dc * 128:(dc + 1) * 128, l0:l0 + LC]))
                OP("tensor", lambda e, dc=dc: e.matmul(
                    prow[:], cx.ones_col_f32[:], xs8[:, dc * LC:(dc + 1) * LC],
                    start=(dc == 0), stop=(dc == ND - 1)))
                OP("scalar", lambda e, dc=dc: e.activation(
                    sqs[:], xs8[:, dc * LC:(dc + 1) * LC], AF.Square))
                OP("tensor", lambda e, dc=dc: e.matmul(
                    prow2[:], cx.ones_col_f32[:], sqs[:], start=(dc == 0), stop=(dc == ND - 1)))
            OP("vector", lambda e: e.tensor_scalar(mean_r, prow[:], 1.0 / D, None, ALU.mult))
            OP("vector", lambda e: e.tensor_scalar(msq_r, prow2[:], 1.0 / D, None, ALU.mult))
            OP("vector", lambda e: e.tensor_tensor(A_r, mean_r, mean_r, ALU.mult))
            OP("vector", lambda e: e.tensor_tensor(A_r, msq_r, A_r, ALU.subtract))
            OP("scalar", lambda e: e.activation(A_r, A_r, AF.Sqrt, bias=eps_col[0:1, :]))
            OP("vector", lambda e: e.reciprocal(A_r, A_r))
            OP("vector", lambda e: e.tensor_tensor(B_r, mean_r, A_r, ALU.mult))
            OP("vector", lambda e: e.tensor_scalar(B_r, B_r, -1.0, None, ALU.mult))
            if debug and lc == 0:
                DMA("sync", lambda e: e.dma_start(out=dbg[0:1, 0:4*LC], in_=rows[:]))
            cx.bcast_row(pa[:], A_r, LC)
            cx.bcast_row(pb[:], B_r, LC)
            for dc in range(ND):
                ob = lambda dc=dc, l0=l0: outb[:, dc * L + l0:dc * L + l0 + LC]
                OP("vector", lambda e, dc=dc: e.tensor_tensor(
                    twt[:], xs8[:, dc * LC:(dc + 1) * LC], pa[:], ALU.mult))
                OP("vector", lambda e, ob=ob: e.tensor_tensor(ob(), twt[:], pb[:], ALU.add))
        for b in range(NB):
            for dc in range(ND):
                seg = outb[:, dc * L + b * SEQ: dc * L + (b + 1) * SEQ]
                OP("vector", lambda e, seg=seg, b=b, dc=dc: e.tensor_reduce(
                    gc[:, b * ND + dc:b * ND + dc + 1], seg, AX, ALU.add))
        OP("vector", lambda e: e.tensor_scalar(gcb[:], gc[:], 1.0 / SEQ, None, ALU.mult))

        for b in range(NB):
            for ft in range(4):
                for dc in range(ND):
                    OP("tensor", lambda e, b=b, ft=ft, dc=dc: e.matmul(
                        pg[:], g1w[:, dc * 512 + ft * 128: dc * 512 + ft * 128 + 128],
                        gcb[:, b * ND + dc:b * ND + dc + 1],
                        start=(dc == 0), stop=(dc == ND - 1)))
                OP("scalar", lambda e, b=b, ft=ft: e.activation(
                    g1o[:, b * 4 + ft:b * 4 + ft + 1], pg[:], AF.Gelu, bias=gb1[:, ft:ft + 1]))
            for ft in range(2):
                for dc in range(4):
                    OP("tensor", lambda e, b=b, ft=ft, dc=dc: e.matmul(
                        pg[:], g2w[:, dc * 256 + ft * 128: dc * 256 + ft * 128 + 128],
                        g1o[:, b * 4 + dc:b * 4 + dc + 1],
                        start=(dc == 0), stop=(dc == 3)))
                OP("scalar", lambda e, b=b, ft=ft: e.activation(
                    g2o[:, b * 2 + ft:b * 2 + ft + 1], pg[:], AF.Gelu, bias=gb2[:, ft:ft + 1]))
            for dc in range(2):
                OP("tensor", lambda e, b=b, dc=dc: e.matmul(
                    pg[0:1, :], g3w[:, dc:dc + 1], g2o[:, b * 2 + dc:b * 2 + dc + 1],
                    start=(dc == 0), stop=(dc == 1)))
            OP("scalar", lambda e, b=b: e.activation(
                gf[:, b:b + 1], pg[0:1, :], AF.Sigmoid, bias=gb3[:]))
        if debug:
            DMA("sync", lambda e: e.dma_start(out=dbg[1:2, 0:2], in_=gf[:]))
        OP("vector", lambda e: e.tensor_scalar(s_ap[:], ar[:], 0.6, None, ALU.mult))
        OP("vector", lambda e: e.tensor_scalar(tmp1[:, 0:1], ar[:], 1.4, None, ALU.mult))
        for b in range(NB):
            OP("vector", lambda e, b=b: e.tensor_tensor(
                o_ap[:, b:b + 1], gf[:, b:b + 1], tmp1[:, 0:1], ALU.mult))
            OP("vector", lambda e, b=b: e.tensor_tensor(
                o_ap[:, b:b + 1], o_ap[:, b:b + 1], ar[:], ALU.subtract))
            OP("vector", lambda e, b=b: e.tensor_scalar(
                o_ap[:, b:b + 1], o_ap[:, b:b + 1], 0.1, None, ALU.add))

        SGN = [1.0, 1.0, -1.0, -1.0, -1.0, -1.0, 1.0, 1.0]
        SRC = [4, 5, 6, 7, 0, 1, 2, 3]
        for cyc in range(n_cyc):
            for lc in range(n_lc):
                l0 = lc * LC
                b = 0 if lc < NLC // 2 else 1
                ob = lambda dc, l0=l0: outb[:, dc * L + l0:dc * L + l0 + LC]
                for ft in range(8):
                    for dc in range(ND):
                        OP("tensor", lambda e, ft=ft, dc=dc, ob=ob: e.matmul(
                            pa[:], w1[:, dc * D + ft * 128: dc * D + ft * 128 + 128],
                            ob(dc),
                            start=(dc == 0), stop=(dc == ND - 1)))
                    OP("scalar", lambda e, ft=ft: e.activation(
                        h1b[:, ft * LC:(ft + 1) * LC], pa[:], AF.Gelu, bias=b1[:, ft:ft + 1]))
                for ft in range(4):
                    for dc in range(8):
                        OP("tensor", lambda e, ft=ft, dc=dc: e.matmul(
                            pa[:], w2[:, dc * 512 + ft * 128: dc * 512 + ft * 128 + 128],
                            h1b[:, dc * LC:(dc + 1) * LC],
                            start=(dc == 0), stop=(dc == 7)))
                    OP("scalar", lambda e, ft=ft: e.activation(
                        h2b[:, ft * LC:(ft + 1) * LC], pa[:], AF.Gelu, bias=b2[:, ft:ft + 1]))
                for ft in range(2):
                    for dc in range(4):
                        OP("tensor", lambda e, ft=ft, dc=dc: e.matmul(
                            pa[:], w3[:, dc * 256 + ft * 128: dc * 256 + ft * 128 + 128],
                            h2b[:, dc * LC:(dc + 1) * LC],
                            start=(dc == 0), stop=(dc == 3)))
                    OP("scalar", lambda e, ft=ft: e.activation(
                        h3b[:, ft * LC:(ft + 1) * LC], pa[:], AF.Gelu, bias=b3[:, ft:ft + 1]))
                for dc in range(2):
                    OP("tensor", lambda e, dc=dc: e.matmul(
                        prow[:], w4[:, dc:dc + 1], h3b[:, dc * LC:(dc + 1) * LC],
                        start=(dc == 0), stop=(dc == 1)))
                OP("scalar", lambda e: e.activation(tf_r[:], prow[:], AF.Sigmoid, bias=b4[:]))
                OP("vector", lambda e: e.tensor_scalar(tf_r[:], tf_r[:], s_ap[:], None, ALU.mult))
                OP("vector", lambda e, b=b: e.tensor_scalar(
                    tf_r[:], tf_r[:], o_ap[:, b:b + 1], None, ALU.add))
                if debug and cyc == 0:
                    DMA("sync", lambda e, lc=lc: e.dma_start(out=dbg[2:3, lc*LC:(lc+1)*LC], in_=tf_r[:]))
                cx.bcast_row(pb[:], tf_r[:], LC)
                for c in range(4):
                    s2 = SRC[c]
                    OP("vector", lambda e, ob=ob, c=c: e.tensor_copy(twt[:], ob(c)))
                    OP("vector", lambda e, ob=ob, s2=s2: e.tensor_tensor(twu[:], ob(s2), pb[:], ALU.mult))
                    OP("vector", lambda e, ob=ob, c=c: e.scalar_tensor_tensor(
                        ob(c), twu[:], SGN[c], ob(c), ALU.mult, ALU.add))
                    OP("vector", lambda e: e.tensor_tensor(twu[:], twt[:], pb[:], ALU.mult))
                    OP("vector", lambda e, ob=ob, s2=s2: e.scalar_tensor_tensor(
                        ob(s2), twu[:], SGN[s2], ob(s2), ALU.mult, ALU.add))

        for lc in range(n_lc):
            l0 = lc * LC
            for dc in range(ND):
                DMA("sync", lambda e, dc=dc, l0=l0: e.dma_start(
                    out=xs8[:, dc * LC:(dc + 1) * LC],
                    in_=x_in[dc * 128:(dc + 1) * 128, l0:l0 + LC]))
                OP("vector", lambda e, dc=dc, l0=l0: e.scalar_tensor_tensor(
                    xs8[:, dc * LC:(dc + 1) * LC],
                    outb[:, dc * L + l0:dc * L + l0 + LC], 0.5,
                    xs8[:, dc * LC:(dc + 1) * LC], ALU.mult, ALU.add))
                DMA("sync", lambda e, dc=dc, l0=l0: e.dma_start(
                    out=x_out[dc * 128:(dc + 1) * 128, l0:l0 + LC],
                    in_=xs8[:, dc * LC:(dc + 1) * LC]))

        if own:
            sq.emit(nc, sems)
    return nc


def _chunked(c):
    return c


def run_s1(x_fm_shards, weights):
    nc = build_s1()
    hc = host_constants()
    in_maps = []
    for c in range(N_CORES):
        m = {"x": x_fm_shards[c]}
        m.update({
            "cn_w1": weights["L0_ffn_cn_w1"], "cn_w2": weights["L0_ffn_cn_w2"],
            "cn_w3": weights["L0_ffn_cn_w3"], "cn_w4": weights["L0_ffn_cn_w4"],
            "gc_w1": weights["L0_ffn_gc_w1"], "gc_w2": weights["L0_ffn_gc_w2"],
            "gc_w3": weights["L0_ffn_gc_w3"],
            "cn_b1": weights["L0_ffn_cn_b1"], "cn_b2": weights["L0_ffn_cn_b2"],
            "cn_b3": weights["L0_ffn_cn_b3"], "cn_b4": weights["L0_ffn_cn_b4"],
            "gc_b1": weights["L0_ffn_gc_b1"], "gc_b2": weights["L0_ffn_gc_b2"],
            "gc_b3": weights["L0_ffn_gc_b3"], "ar": weights["L0_ffn_adaptive_range"],
        })
        m.update(hc)
        in_maps.append(m)
    res = run_bass_kernel_spmd(nc, in_maps, core_ids=list(range(N_CORES)))
    return [r["xo"] for r in res.results]


def to_fm_shards(x):
    out = []
    for c in range(N_CORES):
        xc = x[2 * c:2 * c + 2].reshape(L, D).T
        out.append(np.ascontiguousarray(xc))
    return out


def from_fm_shards(shards):
    x = np.empty((16, SEQ, D), dtype=np.float32)
    for c in range(N_CORES):
        x[2 * c:2 * c + 2] = shards[c].T.reshape(2, SEQ, D)
    return x


def kernel(**inputs):
    return kernel_fused(**inputs)


def declare_w_s2(nc):
    return {"wq": nc.dram_tensor("wq", [D, D], F32, kind="ExternalInput").ap(),
            "wk": nc.dram_tensor("wk", [D, D], F32, kind="ExternalInput").ap(),
            "wv": nc.dram_tensor("wv", [D, D], F32, kind="ExternalInput").ap()}


def build_s2(fctx=None):
    import contextlib
    if fctx is None:
        nc = bass.Bass("TRN2", target_bir_lowering=False)
        x_in = nc.dram_tensor("x", [D, L], F32, kind="ExternalInput").ap()
        x_out = nc.dram_tensor("xo", [D, L], F32, kind="ExternalOutput").ap()
        w2d = declare_w_s2(nc)
        caps = declare_constants(nc)
        sq = Seq()
        P = ""
        own = True
    else:
        nc, sq, caps, P, x_in, x_out, w2d = (fctx["nc"], fctx["sq"], fctx["caps"],
                                             fctx["P"], fctx["x_in"], fctx["x_out"], fctx["w"])
        own = False
    wq_d, wk_d, wv_d = w2d["wq"], w2d["wk"], w2d["wv"]
    NH = 16
    NQC = SEQ // LC
    NKT = SEQ // 128

    with contextlib.ExitStack() as stack:
        cx = Ctx(nc, sq, stack, caps, psum_scratch=False, prefix=P)
        sb, ps = cx.sb, cx.ps
        if own:
            sems = [stack.enter_context(nc.semaphore(f"S{i}")) for i in range(12)]
        eps_col = cx.const_col(1e-5)

        wq = cx.load_w_bf16(wq_d, D, D, "wq_sb")
        wk = cx.load_w_bf16(wk_d, D, D, "wk_sb")
        wv = cx.load_w_bf16(wv_d, D, D, "wv_sb")

        xn = sb("xn", [128, ND * SEQ], BF16)
        xs8 = sb("xs8", [128, ND * LC], F32)
        sqs = sb("sqs", [128, LC], F32)
        rows = sb("rows", [1, 4 * LC], F32)
        mean_r = rows[:, 0:LC]
        msq_r = rows[:, LC:2 * LC]
        A_r = rows[:, 2 * LC:3 * LC]
        B_r = rows[:, 3 * LC:4 * LC]
        vtm = sb("vtm", [128, NKT * D], BF16)
        qh = sb("qh", [64, NQC * LC], BF16)
        kh = sb("kh", [64, NKT * 128], BF16)
        sT = sb("sT", [128, NKT * LC], BF16)
        zrow = sb("zrow", [1, LC], F32)
        avs = sb("avs", [64, LC], F32)
        xtmp = sb("xtmp", [64, LC], F32)

        pa = ps("pa", [128, LC], F32)
        pa2 = ps("pa2", [128, LC], F32)
        pz = ps("pz", [1, LC], F32)
        pav = ps("pav", [64, LC], F32)
        pbc = ps("pbc", [64, LC], F32)
        prow = ps("prow", [1, LC], F32)
        prow2 = ps("prow2", [1, LC], F32)

        OP = sq.op
        DMA = sq.dma

        for b in range(NB):
            base = b * SEQ
            for lc in range(SEQ // LC):
                l0 = base + lc * LC
                for dc in range(ND):
                    DMA("sync", lambda e, dc=dc, l0=l0: e.dma_start(
                        out=xs8[:, dc * LC:(dc + 1) * LC],
                        in_=x_in[dc * 128:(dc + 1) * 128, l0:l0 + LC]))
                    OP("tensor", lambda e, dc=dc: e.matmul(
                        prow[:], cx.ones_col_f32[:], xs8[:, dc * LC:(dc + 1) * LC],
                        start=(dc == 0), stop=(dc == ND - 1)))
                    OP("scalar", lambda e, dc=dc: e.activation(
                        sqs[:], xs8[:, dc * LC:(dc + 1) * LC], AF.Square))
                    OP("tensor", lambda e, dc=dc: e.matmul(
                        prow2[:], cx.ones_col_f32[:], sqs[:], start=(dc == 0), stop=(dc == ND - 1)))
                OP("vector", lambda e: e.tensor_scalar(mean_r, prow[:], 1.0 / D, None, ALU.mult))
                OP("vector", lambda e: e.tensor_scalar(msq_r, prow2[:], 1.0 / D, None, ALU.mult))
                OP("vector", lambda e: e.tensor_tensor(A_r, mean_r, mean_r, ALU.mult))
                OP("vector", lambda e: e.tensor_tensor(A_r, msq_r, A_r, ALU.subtract))
                OP("scalar", lambda e: e.activation(A_r, A_r, AF.Sqrt, bias=eps_col[0:1, :]))
                OP("vector", lambda e: e.reciprocal(A_r, A_r))
                OP("vector", lambda e: e.tensor_tensor(B_r, mean_r, A_r, ALU.mult))
                OP("vector", lambda e: e.tensor_scalar(B_r, B_r, -1.0, None, ALU.mult))
                cx.bcast_row(pa[:], A_r, LC)
                cx.bcast_row(pa2[:], B_r, LC)
                for dc in range(ND):
                    OP("vector", lambda e, dc=dc: e.tensor_tensor(
                        sqs[:], xs8[:, dc * LC:(dc + 1) * LC], pa[:], ALU.mult))
                    OP("vector", lambda e, dc=dc, lc=lc: e.tensor_tensor(
                        xn[:, dc * SEQ + lc * LC: dc * SEQ + (lc + 1) * LC],
                        sqs[:], pa2[:], ALU.add))
            for lt in range(NKT):
                for half in range(2):
                    for dc in range(ND):
                        OP("tensor", lambda e, lt=lt, half=half, dc=dc: e.matmul(
                            pa[:], xn[:, dc * SEQ + lt * 128: dc * SEQ + (lt + 1) * 128],
                            wv[:, dc * D + half * 512:(dc * D) + (half + 1) * 512],
                            start=(dc == 0), stop=(dc == ND - 1)))
                    OP("vector", lambda e, lt=lt, half=half: e.tensor_copy(
                        vtm[:, lt * D + half * 512: lt * D + (half + 1) * 512], pa[:]))
            for h in range(NH):
                for (dst, wmat) in ((qh, wq), (kh, wk)):
                    for qc in range(NQC):
                        for dc in range(ND):
                            OP("tensor", lambda e, dst=dst, wmat=wmat, qc=qc, dc=dc, h=h: e.matmul(
                                pav[:], wmat[:, dc * D + h * 64: dc * D + h * 64 + 64],
                                xn[:, dc * SEQ + qc * LC: dc * SEQ + (qc + 1) * LC],
                                start=(dc == 0), stop=(dc == ND - 1)))
                        OP("vector", lambda e, dst=dst, qc=qc: e.tensor_copy(
                            dst[:, qc * LC:(qc + 1) * LC], pav[:]))
                for qc in range(NQC):
                    for kt in range(NKT):
                        OP("tensor", lambda e, kt=kt, qc=qc: e.matmul(
                            pa[:], kh[:, kt * 128:(kt + 1) * 128],
                            qh[:, qc * LC:(qc + 1) * LC], start=True, stop=True))
                        OP("scalar", lambda e, kt=kt: e.activation(
                            sT[:, kt * LC:(kt + 1) * LC], pa[:], AF.Exp, scale=0.125))
                        OP("tensor", lambda e, kt=kt: e.matmul(
                            pz[:], cx.ones_col_bf16[:], sT[:, kt * LC:(kt + 1) * LC],
                            start=(kt == 0), stop=(kt == NKT - 1)))
                    for kt in range(NKT):
                        OP("tensor", lambda e, kt=kt, h=h: e.matmul(
                            pav[:], vtm[:, kt * D + h * 64: kt * D + h * 64 + 64],
                            sT[:, kt * LC:(kt + 1) * LC],
                            start=(kt == 0), stop=(kt == NKT - 1)))
                    OP("vector", lambda e: e.reciprocal(zrow[:], pz[:]))
                    cx.bcast_row(pbc[:], zrow[:], LC, dt="f32", m=64)
                    OP("scalar", lambda e: e.activation(avs[:], pav[:], AF.Copy))
                    OP("vector", lambda e: e.tensor_tensor(avs[:], avs[:], pbc[:], ALU.mult))
                    dc = h // 2
                    p0 = (h % 2) * 64
                    l0 = base + qc * LC
                    DMA("sync", lambda e, dc=dc, p0=p0, l0=l0: e.dma_start(
                        out=xtmp[:], in_=x_in[dc * 128 + p0: dc * 128 + p0 + 64, l0:l0 + LC]))
                    OP("vector", lambda e: e.tensor_tensor(xtmp[:], xtmp[:], avs[:], ALU.add))
                    DMA("sync", lambda e, dc=dc, p0=p0, l0=l0: e.dma_start(
                        out=x_out[dc * 128 + p0: dc * 128 + p0 + 64, l0:l0 + LC], in_=xtmp[:]))

        if own:
            sq.emit(nc, sems)
    return nc


def _reorder_W_qkv(W):
    H, Fqk, Fv = 16, 64, 64
    Wr = W.reshape(1024, H, 192)
    Wq = np.ascontiguousarray(Wr[:, :, 0:64].reshape(1024, 1024))
    Wk = np.ascontiguousarray(Wr[:, :, 64:128].reshape(1024, 1024))
    Wv = np.ascontiguousarray(Wr[:, :, 128:192].reshape(1024, 1024))
    return Wq, Wk, Wv


def run_s2(x_fm_shards, weights):
    nc = build_s2()
    hc = host_constants()
    Wq, Wk, Wv = _reorder_W_qkv(weights["L0_attn_W"])
    in_maps = []
    for c in range(N_CORES):
        m = {"x": x_fm_shards[c], "wq": Wq, "wk": Wk, "wv": Wv}
        m.update(hc)
        in_maps.append(m)
    res = run_bass_kernel_spmd(nc, in_maps, core_ids=list(range(N_CORES)))
    return [r["xo"] for r in res.results]


CAP = 10240


def declare_w_s4(nc):
    w = {}
    w["wrT"] = nc.dram_tensor("wrT", [D, 8], F32, kind="ExternalInput").ap()
    w["w1"] = nc.dram_tensor("w1", [8 * D, 4096], F32, kind="ExternalInput").ap()
    w["w2"] = nc.dram_tensor("w2", [8 * 4096, D], F32, kind="ExternalInput").ap()
    w["b1"] = nc.dram_tensor("b1", [8 * 4096], F32, kind="ExternalInput").ap()
    w["b2"] = nc.dram_tensor("b2", [8 * D], F32, kind="ExternalInput").ap()
    w["cmask"] = nc.dram_tensor("cmask", [8, 1], F32, kind="ExternalInput").ap()
    w["w1b_i"] = nc.dram_tensor("w1b_i", [8 * D, 4096], BF16, kind="Internal").ap()
    w["w2b_i"] = nc.dram_tensor("w2b_i", [8 * 4096, D], BF16, kind="Internal").ap()
    return w


def build_s4(debug=False, fctx=None, skip_conv=False):
    import contextlib
    if fctx is None:
        nc = bass.Bass("TRN2", target_bir_lowering=False)
        x_in = nc.dram_tensor("x", [D, L], F32, kind="ExternalInput").ap()
        x_out = nc.dram_tensor("xo", [D, L], F32, kind="ExternalOutput").ap()
        w4d = declare_w_s4(nc)
        caps = declare_constants(nc)
        sq = Seq()
        P = ""
        own = True
    else:
        nc, sq, caps, P, x_in, x_out, w4d = (fctx["nc"], fctx["sq"], fctx["caps"],
                                             fctx["P"], fctx["x_in"], fctx["x_out"], fctx["w"])
        own = False
    wrT_d, w1_d, w2_d, b1_d, b2_d, cmask_d = (w4d["wrT"], w4d["w1"], w4d["w2"],
                                              w4d["b1"], w4d["b2"], w4d["cmask"])
    w1b_d, w2b_d = w4d["w1b_i"], w4d["w2b_i"]
    xnd = nc.dram_tensor(P + "xn_i", [D, L], BF16, kind="Internal").ap()
    cc_in = nc.dram_tensor(P + "cc_in", [1, 8], F32, kind="Internal").ap()
    cc_out = nc.dram_tensor(P + "cc_out", [8, 8], F32, kind="Internal").ap()
    dbg = nc.dram_tensor("dbg", [16, L], F32, kind="ExternalOutput").ap() if debug else None

    NT = L // 128

    with contextlib.ExitStack() as stack:
        cx = Ctx(nc, sq, stack, caps, psum_scratch=True, prefix=P)
        sb, ps = cx.sb, cx.ps
        if own:
            sems = [stack.enter_context(nc.semaphore(f"S{i}")) for i in range(12)]
            cc_sems = [stack.enter_context(nc.semaphore("SCC0"))]
        eps_col = cx.const_col(1e-5)
        OP = sq.op
        DMA = sq.dma

        ut128 = sb("ut128_sb", [128, 128], F32)
        DMA("sync", lambda e: e.dma_start(out=ut128[:], in_=caps["ut128"]))
        cmask = sb("cmask_sb", [8, 1], F32)
        DMA("sync", lambda e: e.dma_start(out=cmask[:], in_=cmask_d))
        wrT32 = sb("wrT32", [128, 8 * 8], F32)
        for dc in range(8):
            DMA("sync", lambda e, dc=dc: e.dma_start(
                out=wrT32[:, dc * 8:(dc + 1) * 8], in_=wrT_d[dc * 128:(dc + 1) * 128, :]))

        stgb = sb("stgb", [128, 4096], BF16)
        wr8 = sb("wr8", [8, NT * 128], F32)
        off_row = sb("off_row", [1, 8], F32)
        coff_row = sb("coff_row", [1, 8], F32)
        ccsb = sb("ccsb", [8, 8], F32)
        wrow_lc = sb("wrow_lc", [1, LC], F32)
        wbc = sb("wbc", [128, LC], F32)
        tmpy = sb("tmpy", [128, LC], F32)
        xtmp = sb("xtmp", [128, LC], F32)

        pa = ps("pa", [128, LC], F32)
        pa2 = ps("pa2", [128, LC], F32)
        pwb = ps("pwb", [128, LC], F32)
        plog = ps("plog", [8, LC], F32)
        ppos = ps("ppos", [128, 8], F32)
        prow8 = ps("prow8", [1, 8], F32)
        prow = ps("prow", [1, LC], F32)

        OP("vector", lambda e: e.memset(off_row[:], 0.0))

        with contextlib.ExitStack() as stA:
            sa = lambda name, shape, dt: stA.enter_context(nc.sbuf_tensor(P + name, shape, dt))
            stage = sa("stage", [128, 4096], F32)
            sqs = sa("sqs", [128, LC], F32)
            rows = sa("rows", [1, 4 * LC], F32)
            mean_r = rows[:, 0:LC]
            msq_r = rows[:, LC:2 * LC]
            A_r = rows[:, 2 * LC:3 * LC]
            B_r = rows[:, 3 * LC:4 * LC]
            lgfm = sa("lgfm", [8, L], F32)
            lgtm = sa("lgtm", [128, NT * 8], F32)
            wtsb = sa("wtsb", [128, NT * 8], F32)
            maskb = sa("maskb", [128, NT * 8], F32)
            poss = sa("poss", [128, NT * 8], F32)
            csA = sa("csA", [128, 8], F32)
            csB = sa("csB", [128, 8], F32)
            e1b = sa("e1b", [128, 8], F32)
            e2b = sa("e2b", [128, 8], F32)
            lg2b = sa("lg2b", [128, 8], F32)
            is1b = sa("is1b", [128, 8], F32)
            is2b = sa("is2b", [128, 8], F32)
            t8 = sa("t8", [128, 8], F32)
            colsx = sa("colsx", [128, 4], F32)

            for e_ in (range(8) if not skip_conv else range(0)):
                for dc in range(8):
                    r0 = e_ * D + dc * 128
                    DMA("sync", lambda e, r0=r0: e.dma_start(out=stage[:], in_=w1_d[r0:r0 + 128, :]))
                    OP("vector", lambda e: e.tensor_copy(stgb[:], stage[:]))
                    DMA("sync", lambda e, r0=r0: e.dma_start(out=w1b_d[r0:r0 + 128, :], in_=stgb[:]))
                for fc in range(32):
                    r0 = e_ * 4096 + fc * 128
                    DMA("sync", lambda e, r0=r0: e.dma_start(out=stage[:, 0:1024], in_=w2_d[r0:r0 + 128, :]))
                    OP("vector", lambda e: e.tensor_copy(stgb[:, 0:1024], stage[:, 0:1024]))
                    DMA("sync", lambda e, r0=r0: e.dma_start(out=w2b_d[r0:r0 + 128, :], in_=stgb[:, 0:1024]))

            for lc in range(NLC):
                l0 = lc * LC
                for dc in range(ND):
                    DMA("sync", lambda e, dc=dc, l0=l0: e.dma_start(
                        out=stage[:, dc * LC:(dc + 1) * LC],
                        in_=x_in[dc * 128:(dc + 1) * 128, l0:l0 + LC]))
                    OP("tensor", lambda e, dc=dc: e.matmul(
                        prow[:], cx.ones_col_f32[:], stage[:, dc * LC:(dc + 1) * LC],
                        start=(dc == 0), stop=(dc == ND - 1)))
                    OP("scalar", lambda e, dc=dc: e.activation(
                        sqs[:], stage[:, dc * LC:(dc + 1) * LC], AF.Square))
                    OP("tensor", lambda e, dc=dc: e.matmul(
                        pa2[0:1, :], cx.ones_col_f32[:], sqs[:], start=(dc == 0), stop=(dc == ND - 1)))
                OP("vector", lambda e: e.tensor_scalar(mean_r, prow[:], 1.0 / D, None, ALU.mult))
                OP("vector", lambda e: e.tensor_scalar(msq_r, pa2[0:1, :], 1.0 / D, None, ALU.mult))
                OP("vector", lambda e: e.tensor_tensor(A_r, mean_r, mean_r, ALU.mult))
                OP("vector", lambda e: e.tensor_tensor(A_r, msq_r, A_r, ALU.subtract))
                OP("scalar", lambda e: e.activation(A_r, A_r, AF.Sqrt, bias=eps_col[0:1, :]))
                OP("vector", lambda e: e.reciprocal(A_r, A_r))
                OP("vector", lambda e: e.tensor_tensor(B_r, mean_r, A_r, ALU.mult))
                OP("vector", lambda e: e.tensor_scalar(B_r, B_r, -1.0, None, ALU.mult))
                cx.bcast_row(pa[:], A_r, LC)
                cx.bcast_row(pa2[:], B_r, LC)
                for dc in range(ND):
                    OP("vector", lambda e, dc=dc: e.tensor_tensor(
                        sqs[:], stage[:, dc * LC:(dc + 1) * LC], pa[:], ALU.mult))
                    OP("vector", lambda e, dc=dc: e.tensor_tensor(
                        tmpy[:], sqs[:], pa2[:], ALU.add))
                    OP("vector", lambda e, dc=dc: e.tensor_copy(
                        stgb[:, dc * LC:(dc + 1) * LC], tmpy[:]))
                    DMA("sync", lambda e, dc=dc, l0=l0: e.dma_start(
                        out=xnd[dc * 128:(dc + 1) * 128, l0:l0 + LC],
                        in_=stgb[:, dc * LC:(dc + 1) * LC]))
                    OP("tensor", lambda e, dc=dc: e.matmul(
                        plog[:], wrT32[:, dc * 8:(dc + 1) * 8], tmpy[:],
                        start=(dc == 0), stop=(dc == ND - 1)))
                OP("vector", lambda e, l0=l0: e.tensor_copy(lgfm[:, l0:l0 + LC], plog[:]))

            def prefix8(src, dst1, dst2):
                OP("vector", lambda e, src=src: e.tensor_copy(dst1[:, 0:1], src[:, 0:1]))
                OP("vector", lambda e, src=src: e.tensor_tensor(dst1[:, 1:8], src[:, 1:8], src[:, 0:7], ALU.add))
                OP("vector", lambda e: e.tensor_copy(dst2[:, 0:2], dst1[:, 0:2]))
                OP("vector", lambda e: e.tensor_tensor(dst2[:, 2:8], dst1[:, 2:8], dst1[:, 0:6], ALU.add))
                OP("vector", lambda e: e.tensor_copy(dst1[:, 0:4], dst2[:, 0:4]))
                OP("vector", lambda e: e.tensor_tensor(dst1[:, 4:8], dst2[:, 4:8], dst2[:, 0:4], ALU.add))
                return dst1

            m1 = colsx[:, 0:1]
            m2 = colsx[:, 1:2]
            g1 = colsx[:, 2:3]
            g2 = colsx[:, 3:4]
            for t in range(NT):
                lg = lgtm[:, t * 8:(t + 1) * 8]
                OP("tensor", lambda e, t=t: e.transpose(
                    cx.pt[:, 0:8], lgfm[:, t * 128:(t + 1) * 128], cx.ident_f32[0:8, 0:8]))
                OP("vector", lambda e, lg=lg: e.tensor_copy(lg, cx.pt[:, 0:8]))
                OP("vector", lambda e, lg=lg: e.tensor_reduce(m1, lg, AX, ALU.max))
                OP("vector", lambda e, lg=lg: e.tensor_scalar(e1b[:], lg, m1, None, ALU.is_ge))
                cs = prefix8(e1b, csA, csB)
                OP("vector", lambda e, cs=cs: e.tensor_scalar(t8[:], cs[:], 1.0, None, ALU.is_equal))
                OP("vector", lambda e: e.tensor_tensor(is1b[:], t8[:], e1b[:], ALU.mult))
                OP("vector", lambda e, lg=lg: e.scalar_tensor_tensor(
                    lg2b[:], is1b[:], -1e30, lg, ALU.mult, ALU.add))
                OP("vector", lambda e: e.tensor_reduce(m2, lg2b[:], AX, ALU.max))
                OP("vector", lambda e: e.tensor_scalar(e2b[:], lg2b[:], m2, None, ALU.is_ge))
                cs = prefix8(e2b, csA, csB)
                OP("vector", lambda e, cs=cs: e.tensor_scalar(t8[:], cs[:], 1.0, None, ALU.is_equal))
                OP("vector", lambda e: e.tensor_tensor(is2b[:], t8[:], e2b[:], ALU.mult))
                OP("vector", lambda e: e.tensor_tensor(g1, m1, m2, ALU.subtract))
                OP("scalar", lambda e: e.activation(g1, g1, AF.Sigmoid))
                OP("vector", lambda e: e.tensor_scalar(g2, g1, -1.0, 1.0, ALU.mult, ALU.add))
                OP("vector", lambda e: e.tensor_scalar(t8[:], is1b[:], g1, None, ALU.mult))
                OP("vector", lambda e, t=t: e.scalar_tensor_tensor(
                    wtsb[:, t * 8:(t + 1) * 8], is2b[:], g2, t8[:], ALU.mult, ALU.add))
                OP("vector", lambda e, t=t: e.tensor_tensor(
                    maskb[:, t * 8:(t + 1) * 8], is1b[:], is2b[:], ALU.add))
                OP("tensor", lambda e, t=t: e.matmul(
                    ppos[:], ut128[:], maskb[:, t * 8:(t + 1) * 8], start=True, stop=False))
                OP("tensor", lambda e: e.matmul(
                    ppos[:], cx.ones_row_f32[:], off_row[:], start=False, stop=True))
                OP("vector", lambda e, t=t: e.tensor_copy(poss[:, t * 8:(t + 1) * 8], ppos[:]))
                OP("tensor", lambda e, t=t: e.matmul(
                    prow8[:], cx.ones_col_f32[:], maskb[:, t * 8:(t + 1) * 8], start=True, stop=True))
                OP("vector", lambda e: e.tensor_tensor(off_row[:], off_row[:], prow8[:], ALU.add))

            DMA("sync", lambda e: e.dma_start(out=cc_in, in_=off_row[:]))
            sq.cc(lambda e: e.collective_compute(
                "AllGather", ALU.bypass, replica_groups=[list(range(8))],
                ins=[cc_in.opt()], outs=[cc_out.opt()]))
            DMA("sync", lambda e: e.dma_start(out=ccsb[:], in_=cc_out))
            OP("tensor", lambda e: e.matmul(prow8[:], cmask[:], ccsb[:], start=True, stop=True))
            OP("vector", lambda e: e.tensor_copy(coff_row[:], prow8[:]))

            for t in range(NT):
                cx.bcast_row(ppos[:], coff_row[:], 8)
                OP("vector", lambda e, t=t: e.tensor_tensor(
                    t8[:], poss[:, t * 8:(t + 1) * 8], ppos[:], ALU.add))
                OP("vector", lambda e: e.tensor_scalar(t8[:], t8[:], CAP + 0.5, None, ALU.is_le))
                OP("vector", lambda e, t=t: e.tensor_tensor(
                    wtsb[:, t * 8:(t + 1) * 8], wtsb[:, t * 8:(t + 1) * 8], t8[:], ALU.mult))
                OP("vector", lambda e, t=t: e.tensor_scalar(
                    wtsb[:, t * 8:(t + 1) * 8], wtsb[:, t * 8:(t + 1) * 8], 0.5, None, ALU.mult))
                OP("tensor", lambda e, t=t: e.transpose(
                    cx.pt[0:8, 0:128], wtsb[:, t * 8:(t + 1) * 8], cx.ident_f32[:]))
                OP("vector", lambda e, t=t: e.tensor_copy(wr8[:, t * 128:(t + 1) * 128], cx.pt[0:8, 0:128]))

            if debug:
                DMA("sync", lambda e: e.dma_start(out=dbg[0:8, :], in_=lgfm[:]))
                DMA("sync", lambda e: e.dma_start(out=dbg[8:16, :], in_=wr8[:]))

        w1b = sb("w1b", [128, 8 * 4096], BF16)
        w2b = sb("w2b", [128, 32 * D], BF16)
        h1b = sb("h1b", [128, 32 * LC], BF16)
        for e_ in range(8):
            for dc in range(8):
                DMA("sync", lambda e, e_=e_, dc=dc: e.dma_start(
                    out=w1b[:, dc * 4096:(dc + 1) * 4096],
                    in_=w1b_d[e_ * D + dc * 128: e_ * D + (dc + 1) * 128, :]))
            for fc in range(32):
                DMA("sync", lambda e, e_=e_, fc=fc: e.dma_start(
                    out=w2b[:, fc * D:(fc + 1) * D],
                    in_=w2b_d[e_ * 4096 + fc * 128: e_ * 4096 + (fc + 1) * 128, :]))
            b1c = cx.load_vec_cols(b1_d[e_ * 4096:(e_ + 1) * 4096], 4096, f"b1c_{e_}")
            b2c = cx.load_vec_cols(b2_d[e_ * D:(e_ + 1) * D], D, f"b2c_{e_}")
            for lc in range(NLC):
                l0 = lc * LC
                DMA("sync", lambda e, e_=e_, l0=l0: e.dma_start(
                    out=wrow_lc[:], in_=wr8[e_:e_ + 1, l0:l0 + LC]))
                for dc in range(ND):
                    DMA("sync", lambda e, dc=dc, l0=l0: e.dma_start(
                        out=stgb[:, dc * LC:(dc + 1) * LC],
                        in_=xnd[dc * 128:(dc + 1) * 128, l0:l0 + LC]))
                for ft in range(32):
                    for dc in range(ND):
                        OP("tensor", lambda e, ft=ft, dc=dc: e.matmul(
                            pa[:], w1b[:, dc * 4096 + ft * 128: dc * 4096 + ft * 128 + 128],
                            stgb[:, dc * LC:(dc + 1) * LC],
                            start=(dc == 0), stop=(dc == ND - 1)))
                    OP("scalar", lambda e, ft=ft, b1c=b1c: e.activation(
                        h1b[:, ft * LC:(ft + 1) * LC], pa[:], AF.Gelu_apprx_tanh,
                        bias=b1c[:, ft:ft + 1]))
                cx.bcast_row(pwb[:], wrow_lc[:], LC)
                OP("vector", lambda e: e.tensor_copy(wbc[:], pwb[:]))
                for dc2 in range(ND):
                    for ft in range(32):
                        OP("tensor", lambda e, dc2=dc2, ft=ft: e.matmul(
                            pa2[:], w2b[:, ft * D + dc2 * 128: ft * D + dc2 * 128 + 128],
                            h1b[:, ft * LC:(ft + 1) * LC],
                            start=(ft == 0), stop=(ft == 31)))
                    OP("vector", lambda e, dc2=dc2, b2c=b2c: e.scalar_tensor_tensor(
                        tmpy[:], pa2[:], b2c[:, dc2:dc2 + 1], wbc[:], ALU.add, ALU.mult))
                    src_x = x_in if e_ == 0 else x_out
                    DMA("sync", lambda e, dc2=dc2, l0=l0, src_x=src_x: e.dma_start(
                        out=xtmp[:], in_=src_x[dc2 * 128:(dc2 + 1) * 128, l0:l0 + LC]))
                    OP("vector", lambda e: e.tensor_tensor(xtmp[:], xtmp[:], tmpy[:], ALU.add))
                    DMA("sync", lambda e, dc2=dc2, l0=l0: e.dma_start(
                        out=x_out[dc2 * 128:(dc2 + 1) * 128, l0:l0 + LC], in_=xtmp[:]))

        if own:
            sq.emit(nc, sems, cc_sems)
    return nc


def run_s4(x_fm_shards, weights, prefix="L1_ffn"):
    nc = build_s4()
    hc = host_constants()
    wrT = np.ascontiguousarray(weights[prefix + "_wr"].T)
    w1 = np.ascontiguousarray(weights[prefix + "_w1"].reshape(8 * 1024, 4096))
    w2 = np.ascontiguousarray(weights[prefix + "_w2"].reshape(8 * 4096, 1024))
    b1 = np.ascontiguousarray(weights[prefix + "_b1"].reshape(-1))
    b2 = np.ascontiguousarray(weights[prefix + "_b2"].reshape(-1))
    in_maps = []
    for c in range(N_CORES):
        cm = np.zeros((8, 1), np.float32)
        cm[:c] = 1.0
        m = {"x": x_fm_shards[c], "wrT": wrT, "w1": w1, "w2": w2, "b1": b1,
             "b2": b2, "cmask": cm}
        m.update(hc)
        in_maps.append(m)
    res = run_bass_kernel_spmd(nc, in_maps, core_ids=list(range(N_CORES)))
    return [r["xo"] for r in res.results]


CH = 256
NCH = SEQ // CH
EPS_OUT = 1.1920929e-07
EPS_GATE = 1e-05


def declare_w_s5(nc):
    w = {}
    w["w_in"] = nc.dram_tensor("w_in", [D, 4384], F32, kind="ExternalInput").ap()
    w["conv_w"] = nc.dram_tensor("conv_w", [2304, 4], F32, kind="ExternalInput").ap()
    w["conv_b"] = nc.dram_tensor("conv_b", [2304], F32, kind="ExternalInput").ap()
    w["dt_bias"] = nc.dram_tensor("dt_bias", [32], F32, kind="ExternalInput").ap()
    w["a_log"] = nc.dram_tensor("a_log", [32], F32, kind="ExternalInput").ap()
    w["d_skip"] = nc.dram_tensor("d_skip", [32], F32, kind="ExternalInput").ap()
    w["norm_w"] = nc.dram_tensor("norm_w", [2048], F32, kind="ExternalInput").ap()
    w["w_out"] = nc.dram_tensor("w_out", [2048, D], F32, kind="ExternalInput").ap()
    w["w_fc"] = nc.dram_tensor("w_fc", [D, 4096], F32, kind="ExternalInput").ap()
    w["w_proj"] = nc.dram_tensor("w_proj", [4096, D], F32, kind="ExternalInput").ap()
    w["ltm"] = nc.dram_tensor("ltm", [128, 512], F32, kind="ExternalInput").ap()
    return w


def build_s5(fctx=None):
    import contextlib
    if fctx is None:
        nc = bass.Bass("TRN2", target_bir_lowering=False)
        x_in = nc.dram_tensor("x", [D, L], F32, kind="ExternalInput").ap()
        x_out = nc.dram_tensor("xo", [D, L], F32, kind="ExternalOutput").ap()
        w5d = declare_w_s5(nc)
        caps = declare_constants(nc)
        sq = Seq()
        P = ""
        own = True
    else:
        nc, sq, caps, P, x_in, x_out, w5d = (fctx["nc"], fctx["sq"], fctx["caps"],
                                             fctx["P"], fctx["x_in"], fctx["x_out"], fctx["w"])
        own = False
    w_in_d, convw_d, convb_d = w5d["w_in"], w5d["conv_w"], w5d["conv_b"]
    dtb_d, alog_d, dskip_d = w5d["dt_bias"], w5d["a_log"], w5d["d_skip"]
    normw_d, w_out_d, w_fc_d, w_proj_d, ltm_d = (w5d["norm_w"], w5d["w_out"],
                                                 w5d["w_fc"], w5d["w_proj"], w5d["ltm"])
    xi_d = nc.dram_tensor(P + "xi_i", [D, L], BF16, kind="Internal").ap()
    xnm_d = nc.dram_tensor(P + "xnm_i", [D, L], BF16, kind="Internal").ap()
    h_d = nc.dram_tensor(P + "h_i", [D, L], F32, kind="Internal").ap()

    with contextlib.ExitStack() as stack:
        cx = Ctx(nc, sq, stack, caps, psum_scratch=True, prefix=P)
        sb, ps = cx.sb, cx.ps
        if own:
            sems = [stack.enter_context(nc.semaphore(f"S{i}")) for i in range(12)]
        eps_ln = cx.const_col(1e-5)
        eps_out_c = cx.const_col(EPS_OUT)
        epsg_col = cx.const_col(EPS_GATE)[:]
        OP = sq.op
        DMA = sq.dma

        ptb = ps("ptb", [128, 128], BF16)
        pa = ps("pa", [128, LC], F32)
        pb2 = ps("pb2", [128, LC], F32)
        pcb = ps("pcb", [128, LC], F32)
        py = ps("py", [128, 64], F32)
        pdt = ps("pdt", [32, CH], F32)
        prow = ps("prow", [1, LC], F32)

        def load32col(dram_ap, name):
            row = sb(name + "_row", [1, 32], F32)
            col = sb(name + "_col", [32, 1], F32)
            DMA("sync", lambda e: e.dma_start(out=row[:], in_=dram_ap.rearrange("(a x) -> a x", a=1)))
            OP("tensor", lambda e: e.transpose(cx.pt[0:32, 0:1], row[:], cx.ident_f32[0:1, 0:1]))
            OP("vector", lambda e: e.tensor_copy(col[:], cx.pt[0:32, 0:1]))
            return row, col

        dtb_row, dtb_col = load32col(dtb_d, "dtb")
        alog_row, alog_col = load32col(alog_d, "alog")
        dskip_row, dskip_col = load32col(dskip_d, "dskip")
        negea = sb("negea", [32, 1], F32)
        OP("scalar", lambda e: e.activation(negea[:], alog_col[:], AF.Exp))
        OP("vector", lambda e: e.tensor_scalar(negea[:], negea[:], -1.0, None, ALU.mult))
        dskip_bc = sb("dskip_bc", [128, 32], F32)
        cx.bcast_row(cx.pt[:, 0:32], dskip_row[:], 32)
        OP("vector", lambda e: e.tensor_copy(dskip_bc[:], cx.pt[:, 0:32]))
        convw = sb("convw", [128, 18 * 4], F32)
        for ch in range(18):
            DMA("sync", lambda e, ch=ch: e.dma_start(
                out=convw[:, ch * 4:(ch + 1) * 4], in_=convw_d[ch * 128:(ch + 1) * 128, :]))
        convb = cx.load_vec_cols(convb_d, 2304, "convb")
        ltm = sb("ltm_sb", [128, 512], F32)
        DMA("sync", lambda e: e.dma_start(out=ltm[:], in_=ltm_d))

        with contextlib.ExitStack() as stA:
            sa = lambda name, shape, dt: stA.enter_context(nc.sbuf_tensor(P + name, shape, dt))
            wstage = sa("wstage", [128, 4384], F32)
            w_inb = cx.load_w_bf16_in(stA, w_in_d, D, 4384, "w_inb", stage=wstage)
            w_outb = cx.load_w_bf16_in(stA, w_out_d, 2048, D, "w_outb", stage=wstage)
            state_sb = sa("state_sb", [128, 32 * 64], F32)
            state_bf = sa("state_bf", [128, 32 * 64], BF16)
            y_tm = sa("y_tm", [128, 2 * 2048], BF16)
            gaten = sa("gaten", [128, 2048], BF16)
            g2buf = wstage[:, 0:2048]
            mcol = sa("mcol", [128, 4], F32)
            xs8 = wstage[:, 0:ND * LC]
            arena = sa("arena32", [128, 2048], F32)
            sqs = sa("sqs", [128, LC], F32)
            mean_r = arena[0:1, 0:LC]
            msq_r = arena[0:1, LC:2 * LC]
            A_r = arena[0:1, 2 * LC:3 * LC]
            B_r = arena[0:1, 3 * LC:4 * LC]
            xnmc = sa("xnmc", [128, ND * CH], BF16)
            z_tm = sa("z_tm", [128, 2 * 2048], BF16)
            xbcf = sa("xbcf", [128, 18 * 260], BF16)
            xh_fm = sa("xh_fm", [128, 16 * CH], BF16)
            B_fm = sa("B_fm", [128, CH], BF16)
            C_fm = sa("C_fm", [128, CH], BF16)
            B_tm = sa("B_tm", [128, 2 * 128], BF16)
            xh_tm = sa("xh_tm", [128, 2 * 2048], BF16)
            upre_sm = sa("upre_sm", [128, 2 * 64], BF16)
            dtr_f = arena[0:32, 0:CH]
            dt_fm = arena[0:32, CH:2 * CH]
            acum = arena[0:32, 2 * CH:3 * CH]
            DSf = arena[0:32, 3 * CH:4 * CH]
            PREf = arena[0:32, 4 * CH:5 * CH]
            EAf = arena[0:32, 5 * CH:6 * CH]
            scr32 = arena[0:32, 6 * CH:7 * CH]
            endc = sa("endc", [32, 2], F32)
            tpose = sa("tpose", [128, 6 * 32], F32)
            acumT = lambda lt: tpose[:, lt * 32:(lt + 1) * 32]
            dtT = lambda lt: tpose[:, 64 + lt * 32: 64 + (lt + 1) * 32]
            PRET = lambda lt: tpose[:, 128 + lt * 32: 128 + (lt + 1) * 32]
            dec_bc = sa("dec_bc", [128, 32], F32)
            arow = sa("arow", [1, CH], F32)
            PTs = sa("PTs", [128, 2 * CH], BF16)
            Csh = sa("Csh", [128, CH], BF16)
            t256a = wstage[:, 2048:2048 + CH]
            t256b = wstage[:, 2048 + CH:2048 + 2 * CH]
            xtc = sa("xtc", [128, CH], BF16)
            xib = sa("xib", [128, LC], BF16)
            normw_row = sa("normw_row", [1, 512], F32)
            normw_bc = sa("normw_bc", [128, 2048], BF16)
            for j in range(4):
                DMA("sync", lambda e, j=j: e.dma_start(
                    out=normw_row[:],
                    in_=normw_d.rearrange("(a x) -> a x", a=1)[0:1, j * 512:(j + 1) * 512]))
                cx.bcast_row(pa[:], normw_row[:], 512)
                OP("vector", lambda e, j=j: e.tensor_copy(normw_bc[:, j * 512:(j + 1) * 512], pa[:]))
            htc = sa("htc", [128, CH], F32)

            for b in range(NB):
                for lc in range(SEQ // LC):
                    l0 = b * SEQ + lc * LC
                    for dc in range(ND):
                        DMA("sync", lambda e, dc=dc, l0=l0: e.dma_start(
                            out=xs8[:, dc * LC:(dc + 1) * LC],
                            in_=x_in[dc * 128:(dc + 1) * 128, l0:l0 + LC]))
                        OP("tensor", lambda e, dc=dc: e.matmul(
                            prow[:], cx.ones_col_f32[:], xs8[:, dc * LC:(dc + 1) * LC],
                            start=(dc == 0), stop=(dc == ND - 1)))
                        OP("scalar", lambda e, dc=dc: e.activation(
                            sqs[:], xs8[:, dc * LC:(dc + 1) * LC], AF.Square))
                        OP("tensor", lambda e, dc=dc: e.matmul(
                            pb2[0:1, :], cx.ones_col_f32[:], sqs[:], start=(dc == 0), stop=(dc == ND - 1)))
                    OP("vector", lambda e: e.tensor_scalar(mean_r, prow[:], 1.0 / D, None, ALU.mult))
                    OP("vector", lambda e: e.tensor_scalar(msq_r, pb2[0:1, :], 1.0 / D, None, ALU.mult))
                    OP("vector", lambda e: e.tensor_tensor(A_r, mean_r, mean_r, ALU.mult))
                    OP("vector", lambda e: e.tensor_tensor(A_r, msq_r, A_r, ALU.subtract))
                    OP("scalar", lambda e: e.activation(A_r, A_r, AF.Sqrt, bias=eps_ln[0:1, :]))
                    OP("vector", lambda e: e.reciprocal(A_r, A_r))
                    OP("vector", lambda e: e.tensor_tensor(B_r, mean_r, A_r, ALU.mult))
                    OP("vector", lambda e: e.tensor_scalar(B_r, B_r, -1.0, None, ALU.mult))
                    cx.bcast_row(pa[:], A_r, LC)
                    cx.bcast_row(pb2[:], B_r, LC)
                    for dc in range(ND):
                        OP("vector", lambda e, dc=dc: e.tensor_tensor(
                            sqs[:], xs8[:, dc * LC:(dc + 1) * LC], pa[:], ALU.mult))
                        OP("vector", lambda e, dc=dc: e.tensor_tensor(
                            xs8[:, dc * LC:(dc + 1) * LC], sqs[:], pb2[:], ALU.add))
                        OP("vector", lambda e, dc=dc: e.tensor_copy(
                            xib[:], xs8[:, dc * LC:(dc + 1) * LC]))
                        DMA("sync", lambda e, dc=dc, l0=l0: e.dma_start(
                            out=xi_d[dc * 128:(dc + 1) * 128, l0:l0 + LC],
                            in_=xib[:]))
                        OP("scalar", lambda e, dc=dc: e.activation(
                            sqs[:], xs8[:, dc * LC:(dc + 1) * LC], AF.Square))
                        OP("tensor", lambda e, dc=dc: e.matmul(
                            prow[:], cx.ones_col_f32[:], sqs[:], start=(dc == 0), stop=(dc == ND - 1)))
                    OP("vector", lambda e: e.tensor_scalar(A_r, prow[:], 1.0 / D, None, ALU.mult))
                    OP("scalar", lambda e: e.activation(A_r, A_r, AF.Sqrt, bias=eps_out_c[0:1, :]))
                    OP("vector", lambda e: e.reciprocal(A_r, A_r))
                    cx.bcast_row(pa[:], A_r, LC)
                    for dc in range(ND):
                        OP("vector", lambda e, dc=dc: e.tensor_tensor(
                            sqs[:], xs8[:, dc * LC:(dc + 1) * LC], pa[:], ALU.mult))
                        OP("vector", lambda e: e.tensor_copy(xib[:], sqs[:]))
                        DMA("sync", lambda e, dc=dc, l0=l0: e.dma_start(
                            out=xnm_d[dc * 128:(dc + 1) * 128, l0:l0 + LC],
                            in_=xib[:]))
                OP("vector", lambda e: e.memset(state_sb[:], 0.0))
                OP("vector", lambda e: e.memset(state_bf[:], 0.0))

                for c in range(NCH):
                    l0 = b * SEQ + c * CH
                    for dc in range(ND):
                        DMA("sync", lambda e, dc=dc, l0=l0: e.dma_start(
                            out=xnmc[:, dc * CH:(dc + 1) * CH],
                            in_=xnm_d[dc * 128:(dc + 1) * 128, l0:l0 + CH]))
                    for lt in range(2):
                        for zb in range(4):
                            for dc in range(ND):
                                OP("tensor", lambda e, lt=lt, zb=zb, dc=dc: e.matmul(
                                    pa[:], xnmc[:, dc * CH + lt * 128: dc * CH + lt * 128 + 128],
                                    w_inb[:, dc * 4384 + zb * 512: dc * 4384 + (zb + 1) * 512],
                                    start=(dc == 0), stop=(dc == ND - 1)))
                            OP("scalar", lambda e, lt=lt, zb=zb: e.activation(
                                z_tm[:, lt * 2048 + zb * 512: lt * 2048 + (zb + 1) * 512],
                                pa[:], AF.Silu))
                    if c == 0:
                        for ch in range(18):
                            OP("vector", lambda e, ch=ch: e.memset(xbcf[:, ch * 260:ch * 260 + 3], 0.0))
                    for ch in range(18):
                        for dc in range(ND):
                            OP("tensor", lambda e, ch=ch, dc=dc: e.matmul(
                                pcb[:, 0:CH],
                                w_inb[:, dc * 4384 + 2048 + ch * 128: dc * 4384 + 2048 + (ch + 1) * 128],
                                xnmc[:, dc * CH:(dc + 1) * CH],
                                start=(dc == 0), stop=(dc == ND - 1)))
                        OP("vector", lambda e, ch=ch: e.tensor_copy(
                            xbcf[:, ch * 260 + 3:ch * 260 + 259], pcb[:, 0:CH]))
                    for ch in range(18):
                        base = ch * 260
                        OP("vector", lambda e, base=base, ch=ch: e.tensor_scalar(
                            t256a[:], xbcf[:, base + 3:base + 259], convw[:, ch * 4 + 3:ch * 4 + 4],
                            None, ALU.mult))
                        for k in range(3):
                            src_t = t256a if k % 2 == 0 else t256b
                            dst_t = t256b if k % 2 == 0 else t256a
                            OP("vector", lambda e, base=base, ch=ch, k=k, src_t=src_t, dst_t=dst_t:
                               e.scalar_tensor_tensor(
                                   dst_t[:], xbcf[:, base + k:base + k + 256],
                                   convw[:, ch * 4 + k:ch * 4 + k + 1], src_t[:],
                                   ALU.mult, ALU.add))
                        dst = (xh_fm[:, (ch) * CH:(ch + 1) * CH] if ch < 16 else
                               (B_fm[:] if ch == 16 else C_fm[:]))
                        OP("scalar", lambda e, dst=dst, ch=ch: e.activation(
                            dst, t256b[:], AF.Silu, bias=convb[:, ch:ch + 1]))
                        OP("vector", lambda e, base=base: e.tensor_copy(
                            xbcf[:, base:base + 3], xbcf[:, base + 256:base + 259]))
                    for dc in range(ND):
                        OP("tensor", lambda e, dc=dc: e.matmul(
                            pdt[:], w_inb[:, dc * 4384 + 4352: dc * 4384 + 4384],
                            xnmc[:, dc * CH:(dc + 1) * CH],
                            start=(dc == 0), stop=(dc == ND - 1)))
                    OP("vector", lambda e: e.tensor_scalar(dtr_f, pdt[:], dtb_col[:], None, ALU.add))
                    OP("scalar", lambda e: e.activation(scr32[:], dtr_f, AF.Abs))
                    OP("scalar", lambda e: e.activation(scr32[:], scr32[:], AF.Exp, scale=-1.0))
                    OP("scalar", lambda e: e.activation(scr32[:], scr32[:], AF.Ln, bias=1.0))
                    OP("vector", lambda e: e.tensor_scalar(dt_fm, dtr_f, 0.0, None, ALU.max))
                    OP("vector", lambda e: e.tensor_tensor(dt_fm, dt_fm, scr32[:], ALU.add))
                    OP("vector", lambda e: e.tensor_scalar(acum, dt_fm, negea[:], None, ALU.mult))
                    OP("vector", lambda e: e.tensor_tensor_scan(acum, acum, acum, 0.0, ALU.add, ALU.bypass))
                    OP("vector", lambda e: e.tensor_copy(endc[:, 0:1], acum[:, CH - 1:CH]))
                    OP("vector", lambda e: e.tensor_scalar(DSf, acum, endc[:, 0:1], None, ALU.subtract))
                    OP("scalar", lambda e: e.activation(DSf, DSf, AF.Exp, scale=-1.0))
                    OP("vector", lambda e: e.tensor_tensor(PREf, dt_fm, DSf, ALU.mult))
                    OP("scalar", lambda e: e.activation(EAf, acum, AF.Exp))
                    OP("scalar", lambda e: e.activation(endc[:, 1:2], endc[:, 0:1], AF.Exp))
                    OP("tensor", lambda e: e.transpose(cx.pt[0:1, 0:32], endc[:, 1:2], cx.ident_f32[0:32, 0:32]))
                    OP("vector", lambda e: e.tensor_copy(arow[:, 0:32], cx.pt[0:1, 0:32]))
                    cx.bcast_row(cx.pt[:, 0:32], arow[:, 0:32], 32)
                    OP("vector", lambda e: e.tensor_copy(dec_bc[:], cx.pt[:, 0:32]))
                    for (off, dstf) in ((2 * CH, acumT), (CH, dtT), (4 * CH, PRET)):
                        for lt in range(2):
                            OP("tensor", lambda e, off=off, lt=lt: e.transpose(
                                cx.pt[:, 0:32],
                                arena[0:32, off + lt * 128: off + (lt + 1) * 128],
                                cx.ident_f32[0:32, 0:32]))
                            OP("vector", lambda e, dstf=dstf, lt=lt: e.tensor_copy(
                                dstf(lt), cx.pt[:, 0:32]))
                    for ch in range(16):
                        for lt in range(2):
                            OP("tensor", lambda e, ch=ch, lt=lt: e.transpose(
                                ptb[:], xh_fm[:, ch * CH + lt * 128: ch * CH + lt * 128 + 128],
                                cx.ident_bf16[:]))
                            OP("vector", lambda e, ch=ch, lt=lt: e.tensor_copy(
                                xh_tm[:, lt * 2048 + ch * 128: lt * 2048 + (ch + 1) * 128], ptb[:]))
                    for st in range(2):
                        OP("tensor", lambda e, st=st: e.transpose(
                            ptb[:], B_fm[:, st * 128:(st + 1) * 128], cx.ident_bf16[:]))
                        OP("vector", lambda e, st=st: e.tensor_copy(
                            B_tm[:, st * 128:(st + 1) * 128], ptb[:]))
                    for st in range(2):
                        OP("tensor", lambda e, st=st: e.matmul(
                            pcb[:, st * CH:(st + 1) * CH], B_fm[:, st * 128:(st + 1) * 128],
                            C_fm[:], start=True, stop=True))
                    for h in range(32):
                        DMA("sync", lambda e, h=h: e.dma_start(
                            out=arow[:], in_=arena[h:h + 1, 2 * CH:3 * CH]))
                        for st in range(2):
                            cx.bcast_row(pb2[:, 0:CH], arow[:], CH)
                            OP("vector", lambda e, st=st, h=h: e.tensor_scalar(
                                t256a[:], pb2[:, 0:CH], acumT(st)[:, h:h + 1], None, ALU.subtract))
                            OP("vector", lambda e: e.tensor_scalar(
                                t256a[:], t256a[:], 0.0, None, ALU.min))
                            OP("scalar", lambda e: e.activation(t256a[:], t256a[:], AF.Exp))
                            OP("vector", lambda e, st=st: e.tensor_tensor(
                                t256b[:], ltm[:, st * CH:(st + 1) * CH], pcb[:, st * CH:(st + 1) * CH],
                                ALU.mult))
                            OP("vector", lambda e: e.tensor_tensor(t256b[:], t256b[:], t256a[:], ALU.mult))
                            OP("vector", lambda e, st=st, h=h: e.tensor_scalar(
                                PTs[:, st * CH:(st + 1) * CH], t256b[:], dtT(st)[:, h:h + 1],
                                None, ALU.mult))
                        DMA("sync", lambda e, h=h: e.dma_start(
                            out=arow[:], in_=arena[h:h + 1, 5 * CH:6 * CH]))
                        cx.bcast_row(pb2[:, 0:CH], arow[:], CH)
                        OP("vector", lambda e: e.tensor_tensor(Csh[:], C_fm[:], pb2[:, 0:CH], ALU.mult))
                        for lt in range(2):
                            for st in range(2):
                                OP("tensor", lambda e, lt=lt, st=st, h=h: e.matmul(
                                    py[:], PTs[:, st * CH + lt * 128: st * CH + lt * 128 + 128],
                                    xh_tm[:, st * 2048 + h * 64: st * 2048 + (h + 1) * 64],
                                    start=(st == 0), stop=False))
                            OP("tensor", lambda e, lt=lt, h=h: e.matmul(
                                py[:], Csh[:, lt * 128:(lt + 1) * 128],
                                state_bf[:, h * 64:(h + 1) * 64], start=False, stop=True))
                            OP("vector", lambda e, lt=lt, h=h: e.scalar_tensor_tensor(
                                y_tm[:, lt * 2048 + h * 64: lt * 2048 + (h + 1) * 64],
                                xh_tm[:, lt * 2048 + h * 64: lt * 2048 + (h + 1) * 64],
                                dskip_bc[:, h:h + 1], py[:], ALU.mult, ALU.add))
                        for st in range(2):
                            OP("vector", lambda e, st=st, h=h: e.tensor_scalar(
                                upre_sm[:, st * 64:(st + 1) * 64],
                                xh_tm[:, st * 2048 + h * 64: st * 2048 + (h + 1) * 64],
                                PRET(st)[:, h:h + 1], None, ALU.mult))
                        for st in range(2):
                            OP("tensor", lambda e, st=st, h=h: e.matmul(
                                pb2[:, 0:64], B_tm[:, st * 128:(st + 1) * 128],
                                upre_sm[:, st * 64:(st + 1) * 64],
                                start=(st == 0), stop=(st == 1)))
                        OP("vector", lambda e, h=h: e.scalar_tensor_tensor(
                            state_sb[:, h * 64:(h + 1) * 64], state_sb[:, h * 64:(h + 1) * 64],
                            dec_bc[:, h:h + 1], pb2[:, 0:64], ALU.mult, ALU.add))
                        OP("vector", lambda e, h=h: e.tensor_copy(
                            state_bf[:, h * 64:(h + 1) * 64], state_sb[:, h * 64:(h + 1) * 64]))
                    for lt in range(2):
                        OP("vector", lambda e, lt=lt: e.tensor_tensor(
                            g2buf[:], y_tm[:, lt * 2048:(lt + 1) * 2048],
                            z_tm[:, lt * 2048:(lt + 1) * 2048], ALU.mult))
                        OP("scalar", lambda e: e.activation(
                            gaten[:], g2buf[:], AF.Square, accum_out=mcol[:, 0:1]))
                        OP("vector", lambda e: e.tensor_scalar(mcol[:, 1:2], mcol[:, 0:1], 1.0 / 2048, None, ALU.mult))
                        OP("scalar", lambda e: e.activation(mcol[:, 1:2], mcol[:, 1:2], AF.Sqrt,
                                                            bias=epsg_col))
                        OP("vector", lambda e: e.reciprocal(mcol[:, 1:2], mcol[:, 1:2]))
                        OP("vector", lambda e: e.scalar_tensor_tensor(
                            gaten[:], g2buf[:], mcol[:, 1:2],
                            normw_bc[:], ALU.mult, ALU.mult))
                        for ch in range(16):
                            OP("tensor", lambda e, ch=ch: e.transpose(
                                ptb[:], gaten[:, ch * 128:(ch + 1) * 128],
                                cx.ident_bf16[:]))
                            OP("vector", lambda e, ch=ch, lt=lt: e.tensor_copy(
                                xh_fm[:, ch * CH + lt * 128: ch * CH + lt * 128 + 128], ptb[:]))
                    for dcO in range(ND):
                        for ch in range(16):
                            OP("tensor", lambda e, dcO=dcO, ch=ch: e.matmul(
                                pa[:, 0:CH], w_outb[:, ch * D + dcO * 128: ch * D + (dcO + 1) * 128],
                                xh_fm[:, ch * CH:(ch + 1) * CH],
                                start=(ch == 0), stop=(ch == 15)))
                        DMA("sync", lambda e, dcO=dcO, l0=l0: e.dma_start(
                            out=xtc[:], in_=xi_d[dcO * 128:(dcO + 1) * 128, l0:l0 + CH]))
                        OP("vector", lambda e, dcO=dcO: e.tensor_tensor(htc[:], xtc[:], pa[:, 0:CH], ALU.add))
                        DMA("sync", lambda e, dcO=dcO, l0=l0: e.dma_start(
                            out=h_d[dcO * 128:(dcO + 1) * 128, l0:l0 + CH], in_=htc[:]))

        with contextlib.ExitStack() as stB:
            sbb = lambda name, shape, dt: stB.enter_context(nc.sbuf_tensor(P + name, shape, dt))
            stageB = sbb("stageB", [128, 4096], F32)
            w_fcb = cx.load_w_bf16_in(stB, w_fc_d, D, 4096, "w_fcb", stage=stageB)
            w_prb = cx.load_w_bf16_in(stB, w_proj_d, 4096, D, "w_prb", stage=stageB)
            hs8 = stageB[:, 0:ND * LC]
            hnb = sbb("hnb", [128, ND * LC], BF16)
            r1 = sbb("r1", [128, LC], BF16)
            r2 = sbb("r2", [128, 32 * LC], BF16)
            sq2 = sbb("sq2", [128, LC], F32)
            rowB = sbb("rowB", [1, LC], F32)
            xt5 = sbb("xt5", [128, LC], F32)
            for lc in range(NLC):
                l0 = lc * LC
                for dc in range(ND):
                    DMA("sync", lambda e, dc=dc, l0=l0: e.dma_start(
                        out=hs8[:, dc * LC:(dc + 1) * LC],
                        in_=h_d[dc * 128:(dc + 1) * 128, l0:l0 + LC]))
                    OP("scalar", lambda e, dc=dc: e.activation(
                        sq2[:], hs8[:, dc * LC:(dc + 1) * LC], AF.Square))
                    OP("tensor", lambda e, dc=dc: e.matmul(
                        prow[:], cx.ones_col_f32[:], sq2[:], start=(dc == 0), stop=(dc == ND - 1)))
                OP("vector", lambda e: e.tensor_scalar(rowB[:], prow[:], 1.0 / D, None, ALU.mult))
                OP("scalar", lambda e: e.activation(rowB[:], rowB[:], AF.Sqrt, bias=eps_out_c[0:1, :]))
                OP("vector", lambda e: e.reciprocal(rowB[:], rowB[:]))
                cx.bcast_row(pa[:], rowB[:], LC)
                for dc in range(ND):
                    OP("vector", lambda e, dc=dc: e.tensor_tensor(
                        hnb[:, dc * LC:(dc + 1) * LC], hs8[:, dc * LC:(dc + 1) * LC],
                        pa[:], ALU.mult))
                for ft in range(32):
                    for dc in range(ND):
                        OP("tensor", lambda e, ft=ft, dc=dc: e.matmul(
                            pa[:], w_fcb[:, dc * 4096 + ft * 128: dc * 4096 + ft * 128 + 128],
                            hnb[:, dc * LC:(dc + 1) * LC],
                            start=(dc == 0), stop=(dc == ND - 1)))
                    OP("scalar", lambda e, ft=ft: e.activation(r1[:], pa[:], AF.Relu))
                    OP("scalar", lambda e, ft=ft: e.activation(
                        r2[:, ft * LC:(ft + 1) * LC], r1[:], AF.Square))
                for dc2 in range(ND):
                    for ft in range(32):
                        OP("tensor", lambda e, dc2=dc2, ft=ft: e.matmul(
                            pb2[:], w_prb[:, ft * D + dc2 * 128: ft * D + dc2 * 128 + 128],
                            r2[:, ft * LC:(ft + 1) * LC],
                            start=(ft == 0), stop=(ft == 31)))
                    DMA("sync", lambda e, dc2=dc2, l0=l0: e.dma_start(
                        out=xt5[:], in_=x_in[dc2 * 128:(dc2 + 1) * 128, l0:l0 + LC]))
                    OP("vector", lambda e, dc2=dc2: e.tensor_tensor(
                        xt5[:], xt5[:], hs8[:, dc2 * LC:(dc2 + 1) * LC], ALU.add))
                    OP("vector", lambda e: e.tensor_tensor(xt5[:], xt5[:], pb2[:], ALU.add))
                    DMA("sync", lambda e, dc2=dc2, l0=l0: e.dma_start(
                        out=x_out[dc2 * 128:(dc2 + 1) * 128, l0:l0 + LC], in_=xt5[:]))

        if own:
            sq.emit(nc, sems)
    return nc


def host_ltm():
    ltm = np.zeros((128, 512), np.float32)
    for st in range(2):
        for p in range(128):
            s = st * 128 + p
            ltm[p, st * 256 + s:st * 256 + 256] = 1.0
    return ltm


def run_s5(x_fm_shards, weights):
    nc = build_s5()
    hc = host_constants()
    in_maps = []
    for c in range(N_CORES):
        m = {"x": x_fm_shards[c],
             "w_in": weights["L1_attn_w_in"], "conv_w": weights["L1_attn_conv_w"],
             "conv_b": weights["L1_attn_conv_b"], "dt_bias": weights["L1_attn_dt_bias"],
             "a_log": weights["L1_attn_a_log"], "d_skip": weights["L1_attn_d_skip"],
             "norm_w": weights["L1_attn_norm_w"], "w_out": weights["L1_attn_w_out"],
             "w_fc": weights["L1_attn_w_fc"], "w_proj": weights["L1_attn_w_proj"],
             "ltm": host_ltm()}
        m.update(hc)
        in_maps.append(m)
    res = run_bass_kernel_spmd(nc, in_maps, core_ids=list(range(N_CORES)))
    return [r["xo"] for r in res.results]


def build_fused():
    import contextlib
    nc = bass.Bass("TRN2", target_bir_lowering=False)
    x_in = nc.dram_tensor("x", [D, L], F32, kind="ExternalInput").ap()
    x_out = nc.dram_tensor("xo", [D, L], F32, kind="ExternalOutput").ap()
    xA = nc.dram_tensor("xA_i", [D, L], F32, kind="Internal").ap()
    xB = nc.dram_tensor("xB_i", [D, L], F32, kind="Internal").ap()
    caps = declare_constants(nc)
    w1d = declare_w_s1(nc)
    w2d = declare_w_s2(nc)
    w4d = declare_w_s4(nc)
    w5d = declare_w_s5(nc)
    sq = Seq()
    with contextlib.ExitStack() as stack:
        sems = [stack.enter_context(nc.semaphore(f"S{i}")) for i in range(24)]
        ccs = [stack.enter_context(nc.semaphore(f"SCC{i}")) for i in range(2)]
        mk = lambda P, xi, xo, w: {"nc": nc, "sq": sq, "caps": caps, "P": P,
                                   "x_in": xi, "x_out": xo, "w": w}
        build_s1(fctx=mk("a_", x_in, xA, w1d))
        build_s2(fctx=mk("b_", xA, xB, w2d))
        build_s1(fctx=mk("c_", xB, xA, w1d))
        build_s4(fctx=mk("d_", xA, xB, w4d))
        build_s5(fctx=mk("e_", xB, xA, w5d))
        build_s4(fctx=mk("f_", xA, x_out, w4d), skip_conv=True)
        sq.emit(nc, sems, ccs)
    return nc


def fused_in_maps(x_fm_shards, d):
    hc = host_constants()
    Wq, Wk, Wv = _reorder_W_qkv(d["L0_attn_W"])
    wrT = np.ascontiguousarray(d["L1_ffn_wr"].T)
    w1 = np.ascontiguousarray(d["L1_ffn_w1"].reshape(8 * 1024, 4096))
    w2 = np.ascontiguousarray(d["L1_ffn_w2"].reshape(8 * 4096, 1024))
    b1 = np.ascontiguousarray(d["L1_ffn_b1"].reshape(-1))
    b2 = np.ascontiguousarray(d["L1_ffn_b2"].reshape(-1))
    in_maps = []
    for c in range(N_CORES):
        cm = np.zeros((8, 1), np.float32)
        cm[:c] = 1.0
        m = {"x": x_fm_shards[c],
             "cn_w1": d["L0_ffn_cn_w1"], "cn_w2": d["L0_ffn_cn_w2"],
             "cn_w3": d["L0_ffn_cn_w3"], "cn_w4": d["L0_ffn_cn_w4"],
             "gc_w1": d["L0_ffn_gc_w1"], "gc_w2": d["L0_ffn_gc_w2"],
             "gc_w3": d["L0_ffn_gc_w3"],
             "cn_b1": d["L0_ffn_cn_b1"], "cn_b2": d["L0_ffn_cn_b2"],
             "cn_b3": d["L0_ffn_cn_b3"], "cn_b4": d["L0_ffn_cn_b4"],
             "gc_b1": d["L0_ffn_gc_b1"], "gc_b2": d["L0_ffn_gc_b2"],
             "gc_b3": d["L0_ffn_gc_b3"], "ar": d["L0_ffn_adaptive_range"],
             "wq": Wq, "wk": Wk, "wv": Wv,
             "wrT": wrT, "w1": w1, "w2": w2, "b1": b1, "b2": b2, "cmask": cm,
             "w_in": d["L1_attn_w_in"], "conv_w": d["L1_attn_conv_w"],
             "conv_b": d["L1_attn_conv_b"], "dt_bias": d["L1_attn_dt_bias"],
             "a_log": d["L1_attn_a_log"], "d_skip": d["L1_attn_d_skip"],
             "norm_w": d["L1_attn_norm_w"], "w_out": d["L1_attn_w_out"],
             "w_fc": d["L1_attn_w_fc"], "w_proj": d["L1_attn_w_proj"],
             "ltm": host_ltm()}
        m.update(hc)
        in_maps.append(m)
    return in_maps


def kernel_fused(**inputs):
    x = inputs["L0_attn_x"]
    shards = to_fm_shards(x)
    nc = build_fused()
    in_maps = fused_in_maps(shards, inputs)
    res = run_bass_kernel_spmd(nc, in_maps, core_ids=list(range(N_CORES)))
    return from_fm_shards([r["xo"] for r in res.results])


# version 06
# speedup vs baseline: 1.5672x; 1.5256x over previous
import numpy as np
import concourse.bass as bass
import concourse.mybir as mybir
from concourse.bass_utils import run_bass_kernel_spmd

F32 = mybir.dt.float32
BF16 = mybir.dt.bfloat16
AF = mybir.ActivationFunctionType
ALU = mybir.AluOpType
AX = mybir.AxisListType.X

N_CORES = 8
D = 1024
L = 4096
SEQ = 2048
NB = 2
LC = 512
NLC = L // LC
ND = D // 128


class Seq:

    def __init__(self):
        self.ops = []

    def op(self, eng, fn, inc=1):
        assert not getattr(self, "frozen", False), "Seq.op after emit started"
        self.ops.append((eng, fn, inc, False))

    def dma(self, eng, fn):
        assert not getattr(self, "frozen", False), "Seq.dma after emit started"
        self.ops.append((eng, fn, 16, False))

    def cc(self, fn):
        assert not getattr(self, "frozen", False)
        idx = getattr(self, "n_cc", 0)
        self.n_cc = idx + 1
        self.ops.append(("gpsimd", fn, 0, ("cc", idx)))

    EPOCH_LIMIT = 24000

    def emit(self, nc, sems, cc_sems=()):
        self.frozen = True
        assert getattr(self, "n_cc", 0) <= len(cc_sems)
        n = len(self.ops)
        epoch = []
        epoch_cum = []
        ep, cum = 0, 0
        for (eng, fn, inc, vl) in self.ops:
            if cum + inc > self.EPOCH_LIMIT:
                ep += 1
                cum = 0
            epoch.append(ep)
            epoch_cum.append(cum)
            cum += inc
        n_ep = ep + 1
        assert n_ep <= len(sems), f"need {n_ep} semaphores, have {len(sems)}"
        totals = [0] * n_ep
        for i, (eng, fn, inc, vl) in enumerate(self.ops):
            totals[epoch[i]] = epoch_cum[i] + inc

        def section(engobj, name):
            last_ep_seen = -1
            for i, (eng, fn, inc, vl) in enumerate(self.ops):
                if eng != name:
                    continue
                ep_i = epoch[i]
                if ep_i != last_ep_seen:
                    for k in range(last_ep_seen if last_ep_seen >= 0 else 0, ep_i):
                        engobj.wait_ge(sems[k], totals[k])
                    last_ep_seen = ep_i
                prev_same = (name == "tensor"
                             and i > 0 and self.ops[i - 1][0] == name
                             and epoch[i - 1] == ep_i
                             and not isinstance(self.ops[i - 1][3], tuple))
                if epoch_cum[i] > 0 and not prev_same:
                    engobj.wait_ge(sems[ep_i], epoch_cum[i])
                if i > 0 and isinstance(self.ops[i - 1][3], tuple):
                    engobj.wait_ge(cc_sems[self.ops[i - 1][3][1]], 1)
                r = fn(engobj)
                if isinstance(vl, tuple):
                    r.then_inc(cc_sems[vl[1]])
                elif inc > 0:
                    r.then_inc(sems[ep_i], inc)
            if name == "tensor":
                for k in range(n_ep):
                    engobj.wait_ge(sems[k], totals[k])

        with nc.Block() as block:
            @block.sync
            def _(e):
                section(e, "sync")

            @block.vector
            def _(e):
                section(e, "vector")

            @block.scalar
            def _(e):
                section(e, "scalar")

            @block.tensor
            def _(e):
                section(e, "tensor")

            @block.gpsimd
            def _(e):
                section(e, "gpsimd")


def host_constants():
    c = {}
    c["ident_bf16"] = np.eye(128, dtype=np.float32)
    import ml_dtypes
    c["ident_bf16"] = np.eye(128, dtype=ml_dtypes.bfloat16)
    c["ident_f32"] = np.eye(128, dtype=np.float32)
    c["ones_row_f32"] = np.ones((1, 128), dtype=np.float32)
    c["ones_row_bf16"] = np.ones((1, 128), dtype=ml_dtypes.bfloat16)
    c["ones_col_f32"] = np.ones((128, 1), dtype=np.float32)
    c["ones_col_bf16"] = np.ones((128, 1), dtype=ml_dtypes.bfloat16)
    c["ut128"] = np.triu(np.ones((128, 128), dtype=np.float32))
    return c


def declare_constants(nc):
    aps = {}
    aps["ident_bf16"] = nc.dram_tensor("ident_bf16", [128, 128], BF16, kind="ExternalInput").ap()
    aps["ident_f32"] = nc.dram_tensor("ident_f32", [128, 128], F32, kind="ExternalInput").ap()
    aps["ones_row_f32"] = nc.dram_tensor("ones_row_f32", [1, 128], F32, kind="ExternalInput").ap()
    aps["ones_row_bf16"] = nc.dram_tensor("ones_row_bf16", [1, 128], BF16, kind="ExternalInput").ap()
    aps["ones_col_f32"] = nc.dram_tensor("ones_col_f32", [128, 1], F32, kind="ExternalInput").ap()
    aps["ones_col_bf16"] = nc.dram_tensor("ones_col_bf16", [128, 1], BF16, kind="ExternalInput").ap()
    aps["ut128"] = nc.dram_tensor("ut128", [128, 128], F32, kind="ExternalInput").ap()
    return aps


class Ctx:

    def __init__(self, nc, sq, stack, caps, psum_scratch=True, prefix=""):
        self.nc = nc
        self.sq = sq
        self.P = prefix
        sb = lambda name, shape, dt: stack.enter_context(nc.sbuf_tensor(prefix + name, shape, dt))
        ps = lambda name, shape, dt: stack.enter_context(nc.psum_tensor(prefix + name, shape, dt))
        self.sb = sb
        self.ps = ps
        self.ident_bf16 = sb("c_ident_b", [128, 128], BF16)
        self.ident_f32 = sb("c_ident_f", [128, 128], F32)
        self.ones_row_f32 = sb("c_or_f", [1, 128], F32)
        self.ones_row_bf16 = sb("c_or_b", [1, 128], BF16)
        self.ones_col_f32 = sb("c_oc_f", [128, 1], F32)
        self.ones_col_bf16 = sb("c_oc_b", [128, 1], BF16)
        self.consts = sb("c_consts", [128, 8], F32)
        self.n_consts = 0
        self.const_map = {}
        for name in ["ident_bf16", "ident_f32", "ones_row_f32", "ones_row_bf16",
                     "ones_col_f32", "ones_col_bf16"]:
            dst = getattr(self, name)
            src = caps[name]
            self.sq.dma("sync", lambda e, d=dst, s=src: e.dma_start(out=d[:], in_=s))
        if psum_scratch:
            self.pt = ps("p_tr", [128, 128], F32)

    def const_col(self, val):
        if val in self.const_map:
            return self.const_map[val]
        i = self.n_consts
        assert i < 8
        self.n_consts += 1
        col = self.consts[:, i:i + 1]
        self.sq.op("vector", lambda e, c=col, v=val: e.memset(c, v))
        self.const_map[val] = col
        return col

    def load_vec_cols(self, dram_ap, n, name, dt=F32):
        k = n // 128
        assert k * 128 == n and k <= 128
        stage = self.sb(name + "_stg", [max(k, 1), 128], F32)
        out = self.sb(name, [128, k], dt)
        self.sq.dma("sync", lambda e: e.dma_start(
            out=stage[:], in_=dram_ap.rearrange("(k p) -> k p", k=k)))
        self.sq.op("tensor", lambda e: e.transpose(self.pt[:, 0:k], stage[:], self.ident_f32[0:k, 0:k]))
        self.sq.op("vector", lambda e: e.tensor_copy(out[:], self.pt[:, 0:k]))
        return out

    def load_scalar(self, dram_ap, name):
        out = self.sb(name + "_sb", [1, 1], F32)
        self.sq.dma("sync", lambda e: e.dma_start(out=out[:], in_=dram_ap.rearrange("(a x) -> a x", a=1)))
        return out

    def load_w_bf16_in(self, stack, dram_ap, din, dout, name, stage=None):
        ndc = din // 128
        if stage is None:
            stage = stack.enter_context(self.nc.sbuf_tensor(self.P + name + "_stg", [128, dout], F32))
        out = stack.enter_context(self.nc.sbuf_tensor(self.P + name, [128, ndc * dout], BF16))
        for dc in range(ndc):
            self.sq.dma("sync", lambda e, dc=dc: e.dma_start(
                out=stage[:, 0:dout], in_=dram_ap[dc * 128:(dc + 1) * 128, :]))
            self.sq.op("vector", lambda e, dc=dc: e.tensor_copy(
                out[:, dc * dout:(dc + 1) * dout], stage[:, 0:dout]))
        return out

    def load_w_bf16(self, dram_ap, din, dout, name):
        ndc = din // 128
        stage = self.sb(name + "_stg", [128, dout], F32)
        out = self.sb(name, [128, ndc * dout], BF16)
        for dc in range(ndc):
            self.sq.dma("sync", lambda e, dc=dc: e.dma_start(
                out=stage[:], in_=dram_ap[dc * 128:(dc + 1) * 128, :]))
            self.sq.op("vector", lambda e, dc=dc: e.tensor_copy(
                out[:, dc * dout:(dc + 1) * dout], stage[:]))
        return out

    def bcast_row(self, psum_ap, row_ap, n, dt="f32", m=128):
        ones = self.ones_row_f32 if dt == "f32" else self.ones_row_bf16
        self.sq.op("tensor", lambda e: e.matmul(psum_ap, ones[:, 0:m], row_ap, start=True, stop=True))


def declare_w_s1(nc):
    w = {}
    for nm, shp in [("cn_w1", [D, D]), ("cn_w2", [D, 512]), ("cn_w3", [512, 256]),
                    ("cn_w4", [256, 1]), ("gc_w1", [D, 512]), ("gc_w2", [512, 256]),
                    ("gc_w3", [256, 1])]:
        w[nm] = nc.dram_tensor(nm, shp, F32, kind="ExternalInput").ap()
    for nm, n in [("cn_b1", D), ("cn_b2", 512), ("cn_b3", 256), ("gc_b1", 512),
                  ("gc_b2", 256)]:
        w[nm] = nc.dram_tensor(nm, [n], F32, kind="ExternalInput").ap()
    for nm in ["cn_b4", "gc_b3", "ar"]:
        w[nm] = nc.dram_tensor(nm, [1], F32, kind="ExternalInput").ap()
    return w


def build_s1(n_lc=NLC, n_cyc=3, debug=False, fctx=None):
    import contextlib
    if fctx is None:
        nc = bass.Bass("TRN2", target_bir_lowering=False)
        x_in = nc.dram_tensor("x", [D, L], F32, kind="ExternalInput").ap()
        x_out = nc.dram_tensor("xo", [D, L], F32, kind="ExternalOutput").ap()
        w = declare_w_s1(nc)
        caps = declare_constants(nc)
        sq = Seq()
        P = ""
        own = True
    else:
        nc, sq, caps, P, x_in, x_out, w = (fctx["nc"], fctx["sq"], fctx["caps"],
                                           fctx["P"], fctx["x_in"], fctx["x_out"], fctx["w"])
        own = False
    dbg = nc.dram_tensor("dbg", [8, L], F32, kind="ExternalOutput").ap() if debug else None

    with contextlib.ExitStack() as stack:
        cx = Ctx(nc, sq, stack, caps, prefix=P)
        sb, ps = cx.sb, cx.ps
        if own:
            sems = [stack.enter_context(nc.semaphore(f"S{i}")) for i in range(12)]

        w1 = cx.load_w_bf16(w["cn_w1"], D, D, "w1")
        w2 = cx.load_w_bf16(w["cn_w2"], D, 512, "w2")
        w3 = cx.load_w_bf16(w["cn_w3"], 512, 256, "w3")
        w4 = cx.load_w_bf16(w["cn_w4"], 256, 1, "w4")
        g1w = cx.load_w_bf16(w["gc_w1"], D, 512, "g1w")
        g2w = cx.load_w_bf16(w["gc_w2"], 512, 256, "g2w")
        g3w = cx.load_w_bf16(w["gc_w3"], 256, 1, "g3w")
        b1 = cx.load_vec_cols(w["cn_b1"], D, "b1")
        b2 = cx.load_vec_cols(w["cn_b2"], 512, "b2")
        b3 = cx.load_vec_cols(w["cn_b3"], 256, "b3")
        gb1 = cx.load_vec_cols(w["gc_b1"], 512, "gb1")
        gb2 = cx.load_vec_cols(w["gc_b2"], 256, "gb2")
        b4 = cx.load_scalar(w["cn_b4"], "b4")
        gb3 = cx.load_scalar(w["gc_b3"], "gb3")
        ar = cx.load_scalar(w["ar"], "ar")

        outb = sb("outb", [128, ND * L], BF16)
        h1b = sb("h1b", [128, 8 * LC], BF16)
        h2b = sb("h2b", [128, 4 * LC], BF16)
        h3b = sb("h3b", [128, 2 * LC], BF16)
        xs8 = sb("xs8", [128, ND * LC], F32)
        sqs = sb("sqs", [128, LC], F32)
        rows = sb("rows", [1, 4 * LC], F32)
        mean_r = rows[:, 0 * LC:1 * LC]
        msq_r = rows[:, 1 * LC:2 * LC]
        A_r = rows[:, 2 * LC:3 * LC]
        B_r = rows[:, 3 * LC:4 * LC]
        tf_r = sb("tf_r", [1, LC], F32)
        gc = sb("gc", [128, 2 * ND], F32)
        gcb = sb("gcb", [128, 2 * ND], BF16)
        g1o = sb("g1o", [128, 2 * 4], BF16)
        g2o = sb("g2o", [128, 2 * 2], BF16)
        gf = sb("gf", [1, 2], F32)
        s_ap = sb("s_ap", [1, 1], F32)
        o_ap = sb("o_ap", [1, 2], F32)
        tmp1 = sb("tmp1", [1, 2], F32)
        twt = sb("twt", [128, LC], F32)
        twu = sb("twu", [128, LC], F32)

        eps_col = cx.const_col(1e-5)
        pa = ps("pa", [128, LC], F32)
        pb = ps("pb", [128, LC], F32)
        prow = ps("prow", [1, LC], F32)
        prow2 = ps("prow2", [1, LC], F32)
        pg = ps("pg", [128, 1], F32)

        OP = sq.op
        DMA = sq.dma

        for lc in range(n_lc):
            l0 = lc * LC
            b = 0 if lc < NLC // 2 else 1
            for dc in range(ND):
                DMA("sync", lambda e, dc=dc, l0=l0: e.dma_start(
                    out=xs8[:, dc * LC:(dc + 1) * LC],
                    in_=x_in[dc * 128:(dc + 1) * 128, l0:l0 + LC]))
                OP("tensor", lambda e, dc=dc: e.matmul(
                    prow[:], cx.ones_col_f32[:], xs8[:, dc * LC:(dc + 1) * LC],
                    start=(dc == 0), stop=(dc == ND - 1)))
                OP("scalar", lambda e, dc=dc: e.activation(
                    sqs[:], xs8[:, dc * LC:(dc + 1) * LC], AF.Square))
                OP("tensor", lambda e, dc=dc: e.matmul(
                    prow2[:], cx.ones_col_f32[:], sqs[:], start=(dc == 0), stop=(dc == ND - 1)))
            OP("vector", lambda e: e.tensor_scalar(mean_r, prow[:], 1.0 / D, None, ALU.mult))
            OP("vector", lambda e: e.tensor_scalar(msq_r, prow2[:], 1.0 / D, None, ALU.mult))
            OP("vector", lambda e: e.tensor_tensor(A_r, mean_r, mean_r, ALU.mult))
            OP("vector", lambda e: e.tensor_tensor(A_r, msq_r, A_r, ALU.subtract))
            OP("scalar", lambda e: e.activation(A_r, A_r, AF.Sqrt, bias=eps_col[0:1, :]))
            OP("vector", lambda e: e.reciprocal(A_r, A_r))
            OP("vector", lambda e: e.tensor_tensor(B_r, mean_r, A_r, ALU.mult))
            OP("vector", lambda e: e.tensor_scalar(B_r, B_r, -1.0, None, ALU.mult))
            if debug and lc == 0:
                DMA("sync", lambda e: e.dma_start(out=dbg[0:1, 0:4*LC], in_=rows[:]))
            cx.bcast_row(pa[:], A_r, LC)
            cx.bcast_row(pb[:], B_r, LC)
            for dc in range(ND):
                ob = lambda dc=dc, l0=l0: outb[:, dc * L + l0:dc * L + l0 + LC]
                OP("vector", lambda e, dc=dc: e.tensor_tensor(
                    twt[:], xs8[:, dc * LC:(dc + 1) * LC], pa[:], ALU.mult))
                OP("vector", lambda e, ob=ob: e.tensor_tensor(ob(), twt[:], pb[:], ALU.add))
        for b in range(NB):
            for dc in range(ND):
                seg = outb[:, dc * L + b * SEQ: dc * L + (b + 1) * SEQ]
                OP("vector", lambda e, seg=seg, b=b, dc=dc: e.tensor_reduce(
                    gc[:, b * ND + dc:b * ND + dc + 1], seg, AX, ALU.add))
        OP("vector", lambda e: e.tensor_scalar(gcb[:], gc[:], 1.0 / SEQ, None, ALU.mult))

        for b in range(NB):
            for ft in range(4):
                for dc in range(ND):
                    OP("tensor", lambda e, b=b, ft=ft, dc=dc: e.matmul(
                        pg[:], g1w[:, dc * 512 + ft * 128: dc * 512 + ft * 128 + 128],
                        gcb[:, b * ND + dc:b * ND + dc + 1],
                        start=(dc == 0), stop=(dc == ND - 1)))
                OP("scalar", lambda e, b=b, ft=ft: e.activation(
                    g1o[:, b * 4 + ft:b * 4 + ft + 1], pg[:], AF.Gelu, bias=gb1[:, ft:ft + 1]))
            for ft in range(2):
                for dc in range(4):
                    OP("tensor", lambda e, b=b, ft=ft, dc=dc: e.matmul(
                        pg[:], g2w[:, dc * 256 + ft * 128: dc * 256 + ft * 128 + 128],
                        g1o[:, b * 4 + dc:b * 4 + dc + 1],
                        start=(dc == 0), stop=(dc == 3)))
                OP("scalar", lambda e, b=b, ft=ft: e.activation(
                    g2o[:, b * 2 + ft:b * 2 + ft + 1], pg[:], AF.Gelu, bias=gb2[:, ft:ft + 1]))
            for dc in range(2):
                OP("tensor", lambda e, b=b, dc=dc: e.matmul(
                    pg[0:1, :], g3w[:, dc:dc + 1], g2o[:, b * 2 + dc:b * 2 + dc + 1],
                    start=(dc == 0), stop=(dc == 1)))
            OP("scalar", lambda e, b=b: e.activation(
                gf[:, b:b + 1], pg[0:1, :], AF.Sigmoid, bias=gb3[:]))
        if debug:
            DMA("sync", lambda e: e.dma_start(out=dbg[1:2, 0:2], in_=gf[:]))
        OP("vector", lambda e: e.tensor_scalar(s_ap[:], ar[:], 0.6, None, ALU.mult))
        OP("vector", lambda e: e.tensor_scalar(tmp1[:, 0:1], ar[:], 1.4, None, ALU.mult))
        for b in range(NB):
            OP("vector", lambda e, b=b: e.tensor_tensor(
                o_ap[:, b:b + 1], gf[:, b:b + 1], tmp1[:, 0:1], ALU.mult))
            OP("vector", lambda e, b=b: e.tensor_tensor(
                o_ap[:, b:b + 1], o_ap[:, b:b + 1], ar[:], ALU.subtract))
            OP("vector", lambda e, b=b: e.tensor_scalar(
                o_ap[:, b:b + 1], o_ap[:, b:b + 1], 0.1, None, ALU.add))

        SGN = [1.0, 1.0, -1.0, -1.0, -1.0, -1.0, 1.0, 1.0]
        SRC = [4, 5, 6, 7, 0, 1, 2, 3]
        for cyc in range(n_cyc):
            for lc in range(n_lc):
                l0 = lc * LC
                b = 0 if lc < NLC // 2 else 1
                ob = lambda dc, l0=l0: outb[:, dc * L + l0:dc * L + l0 + LC]
                for ft in range(8):
                    for dc in range(ND):
                        OP("tensor", lambda e, ft=ft, dc=dc, ob=ob: e.matmul(
                            pa[:], w1[:, dc * D + ft * 128: dc * D + ft * 128 + 128],
                            ob(dc),
                            start=(dc == 0), stop=(dc == ND - 1)))
                    OP("scalar", lambda e, ft=ft: e.activation(
                        h1b[:, ft * LC:(ft + 1) * LC], pa[:], AF.Gelu, bias=b1[:, ft:ft + 1]))
                for ft in range(4):
                    for dc in range(8):
                        OP("tensor", lambda e, ft=ft, dc=dc: e.matmul(
                            pa[:], w2[:, dc * 512 + ft * 128: dc * 512 + ft * 128 + 128],
                            h1b[:, dc * LC:(dc + 1) * LC],
                            start=(dc == 0), stop=(dc == 7)))
                    OP("scalar", lambda e, ft=ft: e.activation(
                        h2b[:, ft * LC:(ft + 1) * LC], pa[:], AF.Gelu, bias=b2[:, ft:ft + 1]))
                for ft in range(2):
                    for dc in range(4):
                        OP("tensor", lambda e, ft=ft, dc=dc: e.matmul(
                            pa[:], w3[:, dc * 256 + ft * 128: dc * 256 + ft * 128 + 128],
                            h2b[:, dc * LC:(dc + 1) * LC],
                            start=(dc == 0), stop=(dc == 3)))
                    OP("scalar", lambda e, ft=ft: e.activation(
                        h3b[:, ft * LC:(ft + 1) * LC], pa[:], AF.Gelu, bias=b3[:, ft:ft + 1]))
                for dc in range(2):
                    OP("tensor", lambda e, dc=dc: e.matmul(
                        prow[:], w4[:, dc:dc + 1], h3b[:, dc * LC:(dc + 1) * LC],
                        start=(dc == 0), stop=(dc == 1)))
                OP("scalar", lambda e: e.activation(tf_r[:], prow[:], AF.Sigmoid, bias=b4[:]))
                OP("vector", lambda e: e.tensor_scalar(tf_r[:], tf_r[:], s_ap[:], None, ALU.mult))
                OP("vector", lambda e, b=b: e.tensor_scalar(
                    tf_r[:], tf_r[:], o_ap[:, b:b + 1], None, ALU.add))
                if debug and cyc == 0:
                    DMA("sync", lambda e, lc=lc: e.dma_start(out=dbg[2:3, lc*LC:(lc+1)*LC], in_=tf_r[:]))
                cx.bcast_row(pb[:], tf_r[:], LC)
                for c in range(4):
                    s2 = SRC[c]
                    OP("vector", lambda e, ob=ob, c=c: e.tensor_copy(twt[:], ob(c)))
                    OP("vector", lambda e, ob=ob, s2=s2: e.tensor_tensor(twu[:], ob(s2), pb[:], ALU.mult))
                    OP("vector", lambda e, ob=ob, c=c: e.scalar_tensor_tensor(
                        ob(c), twu[:], SGN[c], ob(c), ALU.mult, ALU.add))
                    OP("vector", lambda e: e.tensor_tensor(twu[:], twt[:], pb[:], ALU.mult))
                    OP("vector", lambda e, ob=ob, s2=s2: e.scalar_tensor_tensor(
                        ob(s2), twu[:], SGN[s2], ob(s2), ALU.mult, ALU.add))

        for lc in range(n_lc):
            l0 = lc * LC
            for dc in range(ND):
                DMA("sync", lambda e, dc=dc, l0=l0: e.dma_start(
                    out=xs8[:, dc * LC:(dc + 1) * LC],
                    in_=x_in[dc * 128:(dc + 1) * 128, l0:l0 + LC]))
                OP("vector", lambda e, dc=dc, l0=l0: e.scalar_tensor_tensor(
                    xs8[:, dc * LC:(dc + 1) * LC],
                    outb[:, dc * L + l0:dc * L + l0 + LC], 0.5,
                    xs8[:, dc * LC:(dc + 1) * LC], ALU.mult, ALU.add))
                DMA("sync", lambda e, dc=dc, l0=l0: e.dma_start(
                    out=x_out[dc * 128:(dc + 1) * 128, l0:l0 + LC],
                    in_=xs8[:, dc * LC:(dc + 1) * LC]))

        if own:
            sq.emit(nc, sems)
    return nc


def _chunked(c):
    return c


def run_s1(x_fm_shards, weights):
    nc = build_s1()
    hc = host_constants()
    in_maps = []
    for c in range(N_CORES):
        m = {"x": x_fm_shards[c]}
        m.update({
            "cn_w1": weights["L0_ffn_cn_w1"], "cn_w2": weights["L0_ffn_cn_w2"],
            "cn_w3": weights["L0_ffn_cn_w3"], "cn_w4": weights["L0_ffn_cn_w4"],
            "gc_w1": weights["L0_ffn_gc_w1"], "gc_w2": weights["L0_ffn_gc_w2"],
            "gc_w3": weights["L0_ffn_gc_w3"],
            "cn_b1": weights["L0_ffn_cn_b1"], "cn_b2": weights["L0_ffn_cn_b2"],
            "cn_b3": weights["L0_ffn_cn_b3"], "cn_b4": weights["L0_ffn_cn_b4"],
            "gc_b1": weights["L0_ffn_gc_b1"], "gc_b2": weights["L0_ffn_gc_b2"],
            "gc_b3": weights["L0_ffn_gc_b3"], "ar": weights["L0_ffn_adaptive_range"],
        })
        m.update(hc)
        in_maps.append(m)
    res = run_bass_kernel_spmd(nc, in_maps, core_ids=list(range(N_CORES)))
    return [r["xo"] for r in res.results]


def to_fm_shards(x):
    out = []
    for c in range(N_CORES):
        xc = x[2 * c:2 * c + 2].reshape(L, D).T
        out.append(np.ascontiguousarray(xc))
    return out


def from_fm_shards(shards):
    x = np.empty((16, SEQ, D), dtype=np.float32)
    for c in range(N_CORES):
        x[2 * c:2 * c + 2] = shards[c].T.reshape(2, SEQ, D)
    return x


def kernel(**inputs):
    return kernel_fused(**inputs)


def declare_w_s2(nc):
    return {"wq": nc.dram_tensor("wq", [D, D], F32, kind="ExternalInput").ap(),
            "wk": nc.dram_tensor("wk", [D, D], F32, kind="ExternalInput").ap(),
            "wv": nc.dram_tensor("wv", [D, D], F32, kind="ExternalInput").ap()}


def build_s2(fctx=None):
    import contextlib
    if fctx is None:
        nc = bass.Bass("TRN2", target_bir_lowering=False)
        x_in = nc.dram_tensor("x", [D, L], F32, kind="ExternalInput").ap()
        x_out = nc.dram_tensor("xo", [D, L], F32, kind="ExternalOutput").ap()
        w2d = declare_w_s2(nc)
        caps = declare_constants(nc)
        sq = Seq()
        P = ""
        own = True
    else:
        nc, sq, caps, P, x_in, x_out, w2d = (fctx["nc"], fctx["sq"], fctx["caps"],
                                             fctx["P"], fctx["x_in"], fctx["x_out"], fctx["w"])
        own = False
    wq_d, wk_d, wv_d = w2d["wq"], w2d["wk"], w2d["wv"]
    NH = 16
    NQC = SEQ // LC
    NKT = SEQ // 128

    with contextlib.ExitStack() as stack:
        cx = Ctx(nc, sq, stack, caps, psum_scratch=False, prefix=P)
        sb, ps = cx.sb, cx.ps
        if own:
            sems = [stack.enter_context(nc.semaphore(f"S{i}")) for i in range(12)]
        eps_col = cx.const_col(1e-5)

        wq = cx.load_w_bf16(wq_d, D, D, "wq_sb")
        wk = cx.load_w_bf16(wk_d, D, D, "wk_sb")
        wv = cx.load_w_bf16(wv_d, D, D, "wv_sb")

        xn = sb("xn", [128, ND * SEQ], BF16)
        xs8 = sb("xs8", [128, ND * LC], F32)
        sqs = sb("sqs", [128, LC], F32)
        rows = sb("rows", [1, 4 * LC], F32)
        mean_r = rows[:, 0:LC]
        msq_r = rows[:, LC:2 * LC]
        A_r = rows[:, 2 * LC:3 * LC]
        B_r = rows[:, 3 * LC:4 * LC]
        vtm = sb("vtm", [128, NKT * D], BF16)
        qh = sb("qh", [64, NQC * LC], BF16)
        kh = sb("kh", [64, NKT * 128], BF16)
        sT = sb("sT", [128, NKT * LC], BF16)
        zrow = sb("zrow", [1, LC], F32)
        avs = sb("avs", [64, LC], F32)
        xtmp = sb("xtmp", [64, LC], F32)

        pa = ps("pa", [128, LC], F32)
        pa2 = ps("pa2", [128, LC], F32)
        pz = ps("pz", [1, LC], F32)
        pav = ps("pav", [64, LC], F32)
        pbc = ps("pbc", [64, LC], F32)
        prow = ps("prow", [1, LC], F32)
        prow2 = ps("prow2", [1, LC], F32)

        OP = sq.op
        DMA = sq.dma

        for b in range(NB):
            base = b * SEQ
            for lc in range(SEQ // LC):
                l0 = base + lc * LC
                for dc in range(ND):
                    DMA("sync", lambda e, dc=dc, l0=l0: e.dma_start(
                        out=xs8[:, dc * LC:(dc + 1) * LC],
                        in_=x_in[dc * 128:(dc + 1) * 128, l0:l0 + LC]))
                    OP("tensor", lambda e, dc=dc: e.matmul(
                        prow[:], cx.ones_col_f32[:], xs8[:, dc * LC:(dc + 1) * LC],
                        start=(dc == 0), stop=(dc == ND - 1)))
                    OP("scalar", lambda e, dc=dc: e.activation(
                        sqs[:], xs8[:, dc * LC:(dc + 1) * LC], AF.Square))
                    OP("tensor", lambda e, dc=dc: e.matmul(
                        prow2[:], cx.ones_col_f32[:], sqs[:], start=(dc == 0), stop=(dc == ND - 1)))
                OP("vector", lambda e: e.tensor_scalar(mean_r, prow[:], 1.0 / D, None, ALU.mult))
                OP("vector", lambda e: e.tensor_scalar(msq_r, prow2[:], 1.0 / D, None, ALU.mult))
                OP("vector", lambda e: e.tensor_tensor(A_r, mean_r, mean_r, ALU.mult))
                OP("vector", lambda e: e.tensor_tensor(A_r, msq_r, A_r, ALU.subtract))
                OP("scalar", lambda e: e.activation(A_r, A_r, AF.Sqrt, bias=eps_col[0:1, :]))
                OP("vector", lambda e: e.reciprocal(A_r, A_r))
                OP("vector", lambda e: e.tensor_tensor(B_r, mean_r, A_r, ALU.mult))
                OP("vector", lambda e: e.tensor_scalar(B_r, B_r, -1.0, None, ALU.mult))
                cx.bcast_row(pa[:], A_r, LC)
                cx.bcast_row(pa2[:], B_r, LC)
                for dc in range(ND):
                    OP("vector", lambda e, dc=dc: e.tensor_tensor(
                        sqs[:], xs8[:, dc * LC:(dc + 1) * LC], pa[:], ALU.mult))
                    OP("vector", lambda e, dc=dc, lc=lc: e.tensor_tensor(
                        xn[:, dc * SEQ + lc * LC: dc * SEQ + (lc + 1) * LC],
                        sqs[:], pa2[:], ALU.add))
            for lt in range(NKT):
                for half in range(2):
                    for dc in range(ND):
                        OP("tensor", lambda e, lt=lt, half=half, dc=dc: e.matmul(
                            pa[:], xn[:, dc * SEQ + lt * 128: dc * SEQ + (lt + 1) * 128],
                            wv[:, dc * D + half * 512:(dc * D) + (half + 1) * 512],
                            start=(dc == 0), stop=(dc == ND - 1)))
                    OP("vector", lambda e, lt=lt, half=half: e.tensor_copy(
                        vtm[:, lt * D + half * 512: lt * D + (half + 1) * 512], pa[:]))
            for h in range(NH):
                for (dst, wmat) in ((qh, wq), (kh, wk)):
                    for qc in range(NQC):
                        for dc in range(ND):
                            OP("tensor", lambda e, dst=dst, wmat=wmat, qc=qc, dc=dc, h=h: e.matmul(
                                pav[:], wmat[:, dc * D + h * 64: dc * D + h * 64 + 64],
                                xn[:, dc * SEQ + qc * LC: dc * SEQ + (qc + 1) * LC],
                                start=(dc == 0), stop=(dc == ND - 1)))
                        OP("vector", lambda e, dst=dst, qc=qc: e.tensor_copy(
                            dst[:, qc * LC:(qc + 1) * LC], pav[:]))
                for qc in range(NQC):
                    for kt in range(NKT):
                        OP("tensor", lambda e, kt=kt, qc=qc: e.matmul(
                            pa[:], kh[:, kt * 128:(kt + 1) * 128],
                            qh[:, qc * LC:(qc + 1) * LC], start=True, stop=True))
                        OP("scalar", lambda e, kt=kt: e.activation(
                            sT[:, kt * LC:(kt + 1) * LC], pa[:], AF.Exp, scale=0.125))
                        OP("tensor", lambda e, kt=kt: e.matmul(
                            pz[:], cx.ones_col_bf16[:], sT[:, kt * LC:(kt + 1) * LC],
                            start=(kt == 0), stop=(kt == NKT - 1)))
                    for kt in range(NKT):
                        OP("tensor", lambda e, kt=kt, h=h: e.matmul(
                            pav[:], vtm[:, kt * D + h * 64: kt * D + h * 64 + 64],
                            sT[:, kt * LC:(kt + 1) * LC],
                            start=(kt == 0), stop=(kt == NKT - 1)))
                    OP("vector", lambda e: e.reciprocal(zrow[:], pz[:]))
                    cx.bcast_row(pbc[:], zrow[:], LC, dt="f32", m=64)
                    OP("scalar", lambda e: e.activation(avs[:], pav[:], AF.Copy))
                    OP("vector", lambda e: e.tensor_tensor(avs[:], avs[:], pbc[:], ALU.mult))
                    dc = h // 2
                    p0 = (h % 2) * 64
                    l0 = base + qc * LC
                    DMA("sync", lambda e, dc=dc, p0=p0, l0=l0: e.dma_start(
                        out=xtmp[:], in_=x_in[dc * 128 + p0: dc * 128 + p0 + 64, l0:l0 + LC]))
                    OP("vector", lambda e: e.tensor_tensor(xtmp[:], xtmp[:], avs[:], ALU.add))
                    DMA("sync", lambda e, dc=dc, p0=p0, l0=l0: e.dma_start(
                        out=x_out[dc * 128 + p0: dc * 128 + p0 + 64, l0:l0 + LC], in_=xtmp[:]))

        if own:
            sq.emit(nc, sems)
    return nc


def _reorder_W_qkv(W):
    H, Fqk, Fv = 16, 64, 64
    Wr = W.reshape(1024, H, 192)
    Wq = np.ascontiguousarray(Wr[:, :, 0:64].reshape(1024, 1024))
    Wk = np.ascontiguousarray(Wr[:, :, 64:128].reshape(1024, 1024))
    Wv = np.ascontiguousarray(Wr[:, :, 128:192].reshape(1024, 1024))
    return Wq, Wk, Wv


def run_s2(x_fm_shards, weights):
    nc = build_s2()
    hc = host_constants()
    Wq, Wk, Wv = _reorder_W_qkv(weights["L0_attn_W"])
    in_maps = []
    for c in range(N_CORES):
        m = {"x": x_fm_shards[c], "wq": Wq, "wk": Wk, "wv": Wv}
        m.update(hc)
        in_maps.append(m)
    res = run_bass_kernel_spmd(nc, in_maps, core_ids=list(range(N_CORES)))
    return [r["xo"] for r in res.results]


CAP = 10240


def declare_w_s4(nc):
    w = {}
    w["wrT"] = nc.dram_tensor("wrT", [D, 8], F32, kind="ExternalInput").ap()
    w["w1"] = nc.dram_tensor("w1", [8 * D, 4096], F32, kind="ExternalInput").ap()
    w["w2"] = nc.dram_tensor("w2", [8 * 4096, D], F32, kind="ExternalInput").ap()
    w["b1"] = nc.dram_tensor("b1", [8 * 4096], F32, kind="ExternalInput").ap()
    w["b2"] = nc.dram_tensor("b2", [8 * D], F32, kind="ExternalInput").ap()
    w["cmask"] = nc.dram_tensor("cmask", [8, 1], F32, kind="ExternalInput").ap()
    w["w1b_i"] = nc.dram_tensor("w1b_i", [8 * D, 4096], BF16, kind="Internal").ap()
    w["w2b_i"] = nc.dram_tensor("w2b_i", [8 * 4096, D], BF16, kind="Internal").ap()
    return w


def build_s4(debug=False, fctx=None, skip_conv=False):
    import contextlib
    if fctx is None:
        nc = bass.Bass("TRN2", target_bir_lowering=False)
        x_in = nc.dram_tensor("x", [D, L], F32, kind="ExternalInput").ap()
        x_out = nc.dram_tensor("xo", [D, L], F32, kind="ExternalOutput").ap()
        w4d = declare_w_s4(nc)
        caps = declare_constants(nc)
        sq = Seq()
        P = ""
        own = True
    else:
        nc, sq, caps, P, x_in, x_out, w4d = (fctx["nc"], fctx["sq"], fctx["caps"],
                                             fctx["P"], fctx["x_in"], fctx["x_out"], fctx["w"])
        own = False
    wrT_d, w1_d, w2_d, b1_d, b2_d, cmask_d = (w4d["wrT"], w4d["w1"], w4d["w2"],
                                              w4d["b1"], w4d["b2"], w4d["cmask"])
    w1b_d, w2b_d = w4d["w1b_i"], w4d["w2b_i"]
    xnd = nc.dram_tensor(P + "xn_i", [D, L], BF16, kind="Internal").ap()
    cc_in = nc.dram_tensor(P + "cc_in", [1, 8], F32, kind="Internal").ap()
    cc_out = nc.dram_tensor(P + "cc_out", [8, 8], F32, kind="Internal").ap()
    dbg = nc.dram_tensor("dbg", [16, L], F32, kind="ExternalOutput").ap() if debug else None

    NT = L // 128

    with contextlib.ExitStack() as stack:
        cx = Ctx(nc, sq, stack, caps, psum_scratch=True, prefix=P)
        sb, ps = cx.sb, cx.ps
        if own:
            sems = [stack.enter_context(nc.semaphore(f"S{i}")) for i in range(12)]
            cc_sems = [stack.enter_context(nc.semaphore("SCC0"))]
        eps_col = cx.const_col(1e-5)
        OP = sq.op
        DMA = sq.dma

        ut128 = sb("ut128_sb", [128, 128], F32)
        DMA("sync", lambda e: e.dma_start(out=ut128[:], in_=caps["ut128"]))
        cmask = sb("cmask_sb", [8, 1], F32)
        DMA("sync", lambda e: e.dma_start(out=cmask[:], in_=cmask_d))
        wrT32 = sb("wrT32", [128, 8 * 8], F32)
        for dc in range(8):
            DMA("sync", lambda e, dc=dc: e.dma_start(
                out=wrT32[:, dc * 8:(dc + 1) * 8], in_=wrT_d[dc * 128:(dc + 1) * 128, :]))

        stgb = sb("stgb", [128, 4096], BF16)
        wr8 = sb("wr8", [8, NT * 128], F32)
        off_row = sb("off_row", [1, 8], F32)
        coff_row = sb("coff_row", [1, 8], F32)
        ccsb = sb("ccsb", [8, 8], F32)
        wrow_lc = sb("wrow_lc", [1, LC], F32)
        wbc = sb("wbc", [128, LC], F32)
        tmpy = sb("tmpy", [128, LC], F32)
        xtmp = sb("xtmp", [128, LC], F32)

        pa = ps("pa", [128, LC], F32)
        pa2 = ps("pa2", [128, LC], F32)
        pwb = ps("pwb", [128, LC], F32)
        plog = ps("plog", [8, LC], F32)
        ppos = ps("ppos", [128, 8], F32)
        prow8 = ps("prow8", [1, 8], F32)
        prow = ps("prow", [1, LC], F32)

        OP("vector", lambda e: e.memset(off_row[:], 0.0))

        with contextlib.ExitStack() as stA:
            sa = lambda name, shape, dt: stA.enter_context(nc.sbuf_tensor(P + name, shape, dt))
            stage = sa("stage", [128, 4096], F32)
            sqs = sa("sqs", [128, LC], F32)
            rows = sa("rows", [1, 4 * LC], F32)
            mean_r = rows[:, 0:LC]
            msq_r = rows[:, LC:2 * LC]
            A_r = rows[:, 2 * LC:3 * LC]
            B_r = rows[:, 3 * LC:4 * LC]
            lgfm = sa("lgfm", [8, L], F32)
            lgtm = sa("lgtm", [128, NT * 8], F32)
            wtsb = sa("wtsb", [128, NT * 8], F32)
            maskb = sa("maskb", [128, NT * 8], F32)
            poss = sa("poss", [128, NT * 8], F32)
            csA = sa("csA", [128, 8], F32)
            csB = sa("csB", [128, 8], F32)
            e1b = sa("e1b", [128, 8], F32)
            e2b = sa("e2b", [128, 8], F32)
            lg2b = sa("lg2b", [128, 8], F32)
            is1b = sa("is1b", [128, 8], F32)
            is2b = sa("is2b", [128, 8], F32)
            t8 = sa("t8", [128, 8], F32)
            colsx = sa("colsx", [128, 4], F32)

            for e_ in (range(8) if not skip_conv else range(0)):
                for dc in range(8):
                    r0 = e_ * D + dc * 128
                    DMA("sync", lambda e, r0=r0: e.dma_start(out=stage[:], in_=w1_d[r0:r0 + 128, :]))
                    OP("vector", lambda e: e.tensor_copy(stgb[:], stage[:]))
                    DMA("sync", lambda e, r0=r0: e.dma_start(out=w1b_d[r0:r0 + 128, :], in_=stgb[:]))
                for fc in range(32):
                    r0 = e_ * 4096 + fc * 128
                    DMA("sync", lambda e, r0=r0: e.dma_start(out=stage[:, 0:1024], in_=w2_d[r0:r0 + 128, :]))
                    OP("vector", lambda e: e.tensor_copy(stgb[:, 0:1024], stage[:, 0:1024]))
                    DMA("sync", lambda e, r0=r0: e.dma_start(out=w2b_d[r0:r0 + 128, :], in_=stgb[:, 0:1024]))

            for lc in range(NLC):
                l0 = lc * LC
                for dc in range(ND):
                    DMA("sync", lambda e, dc=dc, l0=l0: e.dma_start(
                        out=stage[:, dc * LC:(dc + 1) * LC],
                        in_=x_in[dc * 128:(dc + 1) * 128, l0:l0 + LC]))
                    OP("tensor", lambda e, dc=dc: e.matmul(
                        prow[:], cx.ones_col_f32[:], stage[:, dc * LC:(dc + 1) * LC],
                        start=(dc == 0), stop=(dc == ND - 1)))
                    OP("scalar", lambda e, dc=dc: e.activation(
                        sqs[:], stage[:, dc * LC:(dc + 1) * LC], AF.Square))
                    OP("tensor", lambda e, dc=dc: e.matmul(
                        pa2[0:1, :], cx.ones_col_f32[:], sqs[:], start=(dc == 0), stop=(dc == ND - 1)))
                OP("vector", lambda e: e.tensor_scalar(mean_r, prow[:], 1.0 / D, None, ALU.mult))
                OP("vector", lambda e: e.tensor_scalar(msq_r, pa2[0:1, :], 1.0 / D, None, ALU.mult))
                OP("vector", lambda e: e.tensor_tensor(A_r, mean_r, mean_r, ALU.mult))
                OP("vector", lambda e: e.tensor_tensor(A_r, msq_r, A_r, ALU.subtract))
                OP("scalar", lambda e: e.activation(A_r, A_r, AF.Sqrt, bias=eps_col[0:1, :]))
                OP("vector", lambda e: e.reciprocal(A_r, A_r))
                OP("vector", lambda e: e.tensor_tensor(B_r, mean_r, A_r, ALU.mult))
                OP("vector", lambda e: e.tensor_scalar(B_r, B_r, -1.0, None, ALU.mult))
                cx.bcast_row(pa[:], A_r, LC)
                cx.bcast_row(pa2[:], B_r, LC)
                for dc in range(ND):
                    OP("vector", lambda e, dc=dc: e.tensor_tensor(
                        sqs[:], stage[:, dc * LC:(dc + 1) * LC], pa[:], ALU.mult))
                    OP("vector", lambda e, dc=dc: e.tensor_tensor(
                        tmpy[:], sqs[:], pa2[:], ALU.add))
                    OP("vector", lambda e, dc=dc: e.tensor_copy(
                        stgb[:, dc * LC:(dc + 1) * LC], tmpy[:]))
                    DMA("sync", lambda e, dc=dc, l0=l0: e.dma_start(
                        out=xnd[dc * 128:(dc + 1) * 128, l0:l0 + LC],
                        in_=stgb[:, dc * LC:(dc + 1) * LC]))
                    OP("tensor", lambda e, dc=dc: e.matmul(
                        plog[:], wrT32[:, dc * 8:(dc + 1) * 8], tmpy[:],
                        start=(dc == 0), stop=(dc == ND - 1)))
                OP("vector", lambda e, l0=l0: e.tensor_copy(lgfm[:, l0:l0 + LC], plog[:]))

            def prefix8(src, dst1, dst2):
                OP("vector", lambda e, src=src: e.tensor_copy(dst1[:, 0:1], src[:, 0:1]))
                OP("vector", lambda e, src=src: e.tensor_tensor(dst1[:, 1:8], src[:, 1:8], src[:, 0:7], ALU.add))
                OP("vector", lambda e: e.tensor_copy(dst2[:, 0:2], dst1[:, 0:2]))
                OP("vector", lambda e: e.tensor_tensor(dst2[:, 2:8], dst1[:, 2:8], dst1[:, 0:6], ALU.add))
                OP("vector", lambda e: e.tensor_copy(dst1[:, 0:4], dst2[:, 0:4]))
                OP("vector", lambda e: e.tensor_tensor(dst1[:, 4:8], dst2[:, 4:8], dst2[:, 0:4], ALU.add))
                return dst1

            m1 = colsx[:, 0:1]
            m2 = colsx[:, 1:2]
            g1 = colsx[:, 2:3]
            g2 = colsx[:, 3:4]
            for t in range(NT):
                lg = lgtm[:, t * 8:(t + 1) * 8]
                OP("tensor", lambda e, t=t: e.transpose(
                    cx.pt[:, 0:8], lgfm[:, t * 128:(t + 1) * 128], cx.ident_f32[0:8, 0:8]))
                OP("vector", lambda e, lg=lg: e.tensor_copy(lg, cx.pt[:, 0:8]))
                OP("vector", lambda e, lg=lg: e.tensor_reduce(m1, lg, AX, ALU.max))
                OP("vector", lambda e, lg=lg: e.tensor_scalar(e1b[:], lg, m1, None, ALU.is_ge))
                cs = prefix8(e1b, csA, csB)
                OP("vector", lambda e, cs=cs: e.tensor_scalar(t8[:], cs[:], 1.0, None, ALU.is_equal))
                OP("vector", lambda e: e.tensor_tensor(is1b[:], t8[:], e1b[:], ALU.mult))
                OP("vector", lambda e, lg=lg: e.scalar_tensor_tensor(
                    lg2b[:], is1b[:], -1e30, lg, ALU.mult, ALU.add))
                OP("vector", lambda e: e.tensor_reduce(m2, lg2b[:], AX, ALU.max))
                OP("vector", lambda e: e.tensor_scalar(e2b[:], lg2b[:], m2, None, ALU.is_ge))
                cs = prefix8(e2b, csA, csB)
                OP("vector", lambda e, cs=cs: e.tensor_scalar(t8[:], cs[:], 1.0, None, ALU.is_equal))
                OP("vector", lambda e: e.tensor_tensor(is2b[:], t8[:], e2b[:], ALU.mult))
                OP("vector", lambda e: e.tensor_tensor(g1, m1, m2, ALU.subtract))
                OP("scalar", lambda e: e.activation(g1, g1, AF.Sigmoid))
                OP("vector", lambda e: e.tensor_scalar(g2, g1, -1.0, 1.0, ALU.mult, ALU.add))
                OP("vector", lambda e: e.tensor_scalar(t8[:], is1b[:], g1, None, ALU.mult))
                OP("vector", lambda e, t=t: e.scalar_tensor_tensor(
                    wtsb[:, t * 8:(t + 1) * 8], is2b[:], g2, t8[:], ALU.mult, ALU.add))
                OP("vector", lambda e, t=t: e.tensor_tensor(
                    maskb[:, t * 8:(t + 1) * 8], is1b[:], is2b[:], ALU.add))
                OP("tensor", lambda e, t=t: e.matmul(
                    ppos[:], ut128[:], maskb[:, t * 8:(t + 1) * 8], start=True, stop=False))
                OP("tensor", lambda e: e.matmul(
                    ppos[:], cx.ones_row_f32[:], off_row[:], start=False, stop=True))
                OP("vector", lambda e, t=t: e.tensor_copy(poss[:, t * 8:(t + 1) * 8], ppos[:]))
                OP("tensor", lambda e, t=t: e.matmul(
                    prow8[:], cx.ones_col_f32[:], maskb[:, t * 8:(t + 1) * 8], start=True, stop=True))
                OP("vector", lambda e: e.tensor_tensor(off_row[:], off_row[:], prow8[:], ALU.add))

            DMA("sync", lambda e: e.dma_start(out=cc_in, in_=off_row[:]))
            sq.cc(lambda e: e.collective_compute(
                "AllGather", ALU.bypass, replica_groups=[list(range(8))],
                ins=[cc_in.opt()], outs=[cc_out.opt()]))
            DMA("sync", lambda e: e.dma_start(out=ccsb[:], in_=cc_out))
            OP("tensor", lambda e: e.matmul(prow8[:], cmask[:], ccsb[:], start=True, stop=True))
            OP("vector", lambda e: e.tensor_copy(coff_row[:], prow8[:]))

            for t in range(NT):
                cx.bcast_row(ppos[:], coff_row[:], 8)
                OP("vector", lambda e, t=t: e.tensor_tensor(
                    t8[:], poss[:, t * 8:(t + 1) * 8], ppos[:], ALU.add))
                OP("vector", lambda e: e.tensor_scalar(t8[:], t8[:], CAP + 0.5, None, ALU.is_le))
                OP("vector", lambda e, t=t: e.tensor_tensor(
                    wtsb[:, t * 8:(t + 1) * 8], wtsb[:, t * 8:(t + 1) * 8], t8[:], ALU.mult))
                OP("vector", lambda e, t=t: e.tensor_scalar(
                    wtsb[:, t * 8:(t + 1) * 8], wtsb[:, t * 8:(t + 1) * 8], 0.5, None, ALU.mult))
                OP("tensor", lambda e, t=t: e.transpose(
                    cx.pt[0:8, 0:128], wtsb[:, t * 8:(t + 1) * 8], cx.ident_f32[:]))
                OP("vector", lambda e, t=t: e.tensor_copy(wr8[:, t * 128:(t + 1) * 128], cx.pt[0:8, 0:128]))

            if debug:
                DMA("sync", lambda e: e.dma_start(out=dbg[0:8, :], in_=lgfm[:]))
                DMA("sync", lambda e: e.dma_start(out=dbg[8:16, :], in_=wr8[:]))

        w1b = sb("w1b", [128, 8 * 4096], BF16)
        w2b = sb("w2b", [128, 32 * D], BF16)
        h1b = sb("h1b", [128, 32 * LC], BF16)
        for e_ in range(8):
            for dc in range(8):
                DMA("sync", lambda e, e_=e_, dc=dc: e.dma_start(
                    out=w1b[:, dc * 4096:(dc + 1) * 4096],
                    in_=w1b_d[e_ * D + dc * 128: e_ * D + (dc + 1) * 128, :]))
            for fc in range(32):
                DMA("sync", lambda e, e_=e_, fc=fc: e.dma_start(
                    out=w2b[:, fc * D:(fc + 1) * D],
                    in_=w2b_d[e_ * 4096 + fc * 128: e_ * 4096 + (fc + 1) * 128, :]))
            b1c = cx.load_vec_cols(b1_d[e_ * 4096:(e_ + 1) * 4096], 4096, f"b1c_{e_}")
            b2c = cx.load_vec_cols(b2_d[e_ * D:(e_ + 1) * D], D, f"b2c_{e_}")
            for lc in range(NLC):
                l0 = lc * LC
                DMA("sync", lambda e, e_=e_, l0=l0: e.dma_start(
                    out=wrow_lc[:], in_=wr8[e_:e_ + 1, l0:l0 + LC]))
                for dc in range(ND):
                    DMA("sync", lambda e, dc=dc, l0=l0: e.dma_start(
                        out=stgb[:, dc * LC:(dc + 1) * LC],
                        in_=xnd[dc * 128:(dc + 1) * 128, l0:l0 + LC]))
                for ft in range(32):
                    for dc in range(ND):
                        OP("tensor", lambda e, ft=ft, dc=dc: e.matmul(
                            pa[:], w1b[:, dc * 4096 + ft * 128: dc * 4096 + ft * 128 + 128],
                            stgb[:, dc * LC:(dc + 1) * LC],
                            start=(dc == 0), stop=(dc == ND - 1)))
                    OP("scalar", lambda e, ft=ft, b1c=b1c: e.activation(
                        h1b[:, ft * LC:(ft + 1) * LC], pa[:], AF.Gelu_apprx_tanh,
                        bias=b1c[:, ft:ft + 1]))
                cx.bcast_row(pwb[:], wrow_lc[:], LC)
                OP("vector", lambda e: e.tensor_copy(wbc[:], pwb[:]))
                for dc2 in range(ND):
                    for ft in range(32):
                        OP("tensor", lambda e, dc2=dc2, ft=ft: e.matmul(
                            pa2[:], w2b[:, ft * D + dc2 * 128: ft * D + dc2 * 128 + 128],
                            h1b[:, ft * LC:(ft + 1) * LC],
                            start=(ft == 0), stop=(ft == 31)))
                    OP("vector", lambda e, dc2=dc2, b2c=b2c: e.scalar_tensor_tensor(
                        tmpy[:], pa2[:], b2c[:, dc2:dc2 + 1], wbc[:], ALU.add, ALU.mult))
                    src_x = x_in if e_ == 0 else x_out
                    DMA("sync", lambda e, dc2=dc2, l0=l0, src_x=src_x: e.dma_start(
                        out=xtmp[:], in_=src_x[dc2 * 128:(dc2 + 1) * 128, l0:l0 + LC]))
                    OP("vector", lambda e: e.tensor_tensor(xtmp[:], xtmp[:], tmpy[:], ALU.add))
                    DMA("sync", lambda e, dc2=dc2, l0=l0: e.dma_start(
                        out=x_out[dc2 * 128:(dc2 + 1) * 128, l0:l0 + LC], in_=xtmp[:]))

        if own:
            sq.emit(nc, sems, cc_sems)
    return nc


def run_s4(x_fm_shards, weights, prefix="L1_ffn"):
    nc = build_s4()
    hc = host_constants()
    wrT = np.ascontiguousarray(weights[prefix + "_wr"].T)
    w1 = np.ascontiguousarray(weights[prefix + "_w1"].reshape(8 * 1024, 4096))
    w2 = np.ascontiguousarray(weights[prefix + "_w2"].reshape(8 * 4096, 1024))
    b1 = np.ascontiguousarray(weights[prefix + "_b1"].reshape(-1))
    b2 = np.ascontiguousarray(weights[prefix + "_b2"].reshape(-1))
    in_maps = []
    for c in range(N_CORES):
        cm = np.zeros((8, 1), np.float32)
        cm[:c] = 1.0
        m = {"x": x_fm_shards[c], "wrT": wrT, "w1": w1, "w2": w2, "b1": b1,
             "b2": b2, "cmask": cm}
        m.update(hc)
        in_maps.append(m)
    res = run_bass_kernel_spmd(nc, in_maps, core_ids=list(range(N_CORES)))
    return [r["xo"] for r in res.results]


CH = 256
NCH = SEQ // CH
EPS_OUT = 1.1920929e-07
EPS_GATE = 1e-05


def declare_w_s5(nc):
    w = {}
    w["w_in"] = nc.dram_tensor("w_in", [D, 4384], F32, kind="ExternalInput").ap()
    w["conv_w"] = nc.dram_tensor("conv_w", [2304, 4], F32, kind="ExternalInput").ap()
    w["conv_b"] = nc.dram_tensor("conv_b", [2304], F32, kind="ExternalInput").ap()
    w["dt_bias"] = nc.dram_tensor("dt_bias", [32], F32, kind="ExternalInput").ap()
    w["a_log"] = nc.dram_tensor("a_log", [32], F32, kind="ExternalInput").ap()
    w["d_skip"] = nc.dram_tensor("d_skip", [32], F32, kind="ExternalInput").ap()
    w["norm_w"] = nc.dram_tensor("norm_w", [2048], F32, kind="ExternalInput").ap()
    w["w_out"] = nc.dram_tensor("w_out", [2048, D], F32, kind="ExternalInput").ap()
    w["w_fc"] = nc.dram_tensor("w_fc", [D, 4096], F32, kind="ExternalInput").ap()
    w["w_proj"] = nc.dram_tensor("w_proj", [4096, D], F32, kind="ExternalInput").ap()
    w["ltm"] = nc.dram_tensor("ltm", [128, 512], F32, kind="ExternalInput").ap()
    return w


def build_s5(fctx=None):
    import contextlib
    if fctx is None:
        nc = bass.Bass("TRN2", target_bir_lowering=False)
        x_in = nc.dram_tensor("x", [D, L], F32, kind="ExternalInput").ap()
        x_out = nc.dram_tensor("xo", [D, L], F32, kind="ExternalOutput").ap()
        w5d = declare_w_s5(nc)
        caps = declare_constants(nc)
        sq = Seq()
        P = ""
        own = True
    else:
        nc, sq, caps, P, x_in, x_out, w5d = (fctx["nc"], fctx["sq"], fctx["caps"],
                                             fctx["P"], fctx["x_in"], fctx["x_out"], fctx["w"])
        own = False
    w_in_d, convw_d, convb_d = w5d["w_in"], w5d["conv_w"], w5d["conv_b"]
    dtb_d, alog_d, dskip_d = w5d["dt_bias"], w5d["a_log"], w5d["d_skip"]
    normw_d, w_out_d, w_fc_d, w_proj_d, ltm_d = (w5d["norm_w"], w5d["w_out"],
                                                 w5d["w_fc"], w5d["w_proj"], w5d["ltm"])
    xi_d = nc.dram_tensor(P + "xi_i", [D, L], BF16, kind="Internal").ap()
    xnm_d = nc.dram_tensor(P + "xnm_i", [D, L], BF16, kind="Internal").ap()
    h_d = nc.dram_tensor(P + "h_i", [D, L], F32, kind="Internal").ap()

    with contextlib.ExitStack() as stack:
        cx = Ctx(nc, sq, stack, caps, psum_scratch=True, prefix=P)
        sb, ps = cx.sb, cx.ps
        if own:
            sems = [stack.enter_context(nc.semaphore(f"S{i}")) for i in range(12)]
        eps_ln = cx.const_col(1e-5)
        eps_out_c = cx.const_col(EPS_OUT)
        epsg_col = cx.const_col(EPS_GATE)[:]
        OP = sq.op
        DMA = sq.dma

        ptb = ps("ptb", [128, 128], BF16)
        pa = ps("pa", [128, LC], F32)
        pb2 = ps("pb2", [128, LC], F32)
        pcb = ps("pcb", [128, LC], F32)
        py = ps("py", [128, 64], F32)
        pdt = ps("pdt", [32, CH], F32)
        prow = ps("prow", [1, LC], F32)

        def load32col(dram_ap, name):
            row = sb(name + "_row", [1, 32], F32)
            col = sb(name + "_col", [32, 1], F32)
            DMA("sync", lambda e: e.dma_start(out=row[:], in_=dram_ap.rearrange("(a x) -> a x", a=1)))
            OP("tensor", lambda e: e.transpose(cx.pt[0:32, 0:1], row[:], cx.ident_f32[0:1, 0:1]))
            OP("vector", lambda e: e.tensor_copy(col[:], cx.pt[0:32, 0:1]))
            return row, col

        dtb_row, dtb_col = load32col(dtb_d, "dtb")
        alog_row, alog_col = load32col(alog_d, "alog")
        dskip_row, dskip_col = load32col(dskip_d, "dskip")
        negea = sb("negea", [32, 1], F32)
        OP("scalar", lambda e: e.activation(negea[:], alog_col[:], AF.Exp))
        OP("vector", lambda e: e.tensor_scalar(negea[:], negea[:], -1.0, None, ALU.mult))
        dskip_bc = sb("dskip_bc", [128, 32], F32)
        cx.bcast_row(cx.pt[:, 0:32], dskip_row[:], 32)
        OP("vector", lambda e: e.tensor_copy(dskip_bc[:], cx.pt[:, 0:32]))
        convw = sb("convw", [128, 18 * 4], F32)
        for ch in range(18):
            DMA("sync", lambda e, ch=ch: e.dma_start(
                out=convw[:, ch * 4:(ch + 1) * 4], in_=convw_d[ch * 128:(ch + 1) * 128, :]))
        convb = cx.load_vec_cols(convb_d, 2304, "convb")
        ltm = sb("ltm_sb", [128, 512], F32)
        DMA("sync", lambda e: e.dma_start(out=ltm[:], in_=ltm_d))

        with contextlib.ExitStack() as stA:
            sa = lambda name, shape, dt: stA.enter_context(nc.sbuf_tensor(P + name, shape, dt))
            wstage = sa("wstage", [128, 4384], F32)
            w_inb = cx.load_w_bf16_in(stA, w_in_d, D, 4384, "w_inb", stage=wstage)
            w_outb = cx.load_w_bf16_in(stA, w_out_d, 2048, D, "w_outb", stage=wstage)
            state_sb = sa("state_sb", [128, 32 * 64], F32)
            state_bf = sa("state_bf", [128, 32 * 64], BF16)
            y_tm = sa("y_tm", [128, 2 * 2048], BF16)
            gaten = sa("gaten", [128, 2048], BF16)
            g2buf = wstage[:, 0:2048]
            mcol = sa("mcol", [128, 4], F32)
            xs8 = wstage[:, 0:ND * LC]
            arena = sa("arena32", [128, 2048], F32)
            sqs = sa("sqs", [128, LC], F32)
            mean_r = arena[0:1, 0:LC]
            msq_r = arena[0:1, LC:2 * LC]
            A_r = arena[0:1, 2 * LC:3 * LC]
            B_r = arena[0:1, 3 * LC:4 * LC]
            xnmc = sa("xnmc", [128, ND * CH], BF16)
            z_tm = sa("z_tm", [128, 2 * 2048], BF16)
            xbcf = sa("xbcf", [128, 18 * 260], BF16)
            xh_fm = sa("xh_fm", [128, 16 * CH], BF16)
            B_fm = sa("B_fm", [128, CH], BF16)
            C_fm = sa("C_fm", [128, CH], BF16)
            B_tm = sa("B_tm", [128, 2 * 128], BF16)
            xh_tm = sa("xh_tm", [128, 2 * 2048], BF16)
            upre_sm = sa("upre_sm", [128, 2 * 64], BF16)
            dtr_f = arena[0:32, 0:CH]
            dt_fm = arena[0:32, CH:2 * CH]
            acum = arena[0:32, 2 * CH:3 * CH]
            DSf = arena[0:32, 3 * CH:4 * CH]
            PREf = arena[0:32, 4 * CH:5 * CH]
            EAf = arena[0:32, 5 * CH:6 * CH]
            scr32 = arena[0:32, 6 * CH:7 * CH]
            endc = sa("endc", [32, 2], F32)
            tpose = sa("tpose", [128, 6 * 32], F32)
            acumT = lambda lt: tpose[:, lt * 32:(lt + 1) * 32]
            dtT = lambda lt: tpose[:, 64 + lt * 32: 64 + (lt + 1) * 32]
            PRET = lambda lt: tpose[:, 128 + lt * 32: 128 + (lt + 1) * 32]
            dec_bc = sa("dec_bc", [128, 32], F32)
            arow = sa("arow", [1, CH], F32)
            PTs = sa("PTs", [128, 2 * CH], BF16)
            Csh = sa("Csh", [128, CH], BF16)
            t256a = wstage[:, 2048:2048 + CH]
            t256b = wstage[:, 2048 + CH:2048 + 2 * CH]
            xtc = sa("xtc", [128, CH], BF16)
            xib = sa("xib", [128, LC], BF16)
            normw_row = sa("normw_row", [1, 512], F32)
            normw_bc = sa("normw_bc", [128, 2048], BF16)
            for j in range(4):
                DMA("sync", lambda e, j=j: e.dma_start(
                    out=normw_row[:],
                    in_=normw_d.rearrange("(a x) -> a x", a=1)[0:1, j * 512:(j + 1) * 512]))
                cx.bcast_row(pa[:], normw_row[:], 512)
                OP("vector", lambda e, j=j: e.tensor_copy(normw_bc[:, j * 512:(j + 1) * 512], pa[:]))
            htc = sa("htc", [128, CH], F32)

            for b in range(NB):
                for lc in range(SEQ // LC):
                    l0 = b * SEQ + lc * LC
                    for dc in range(ND):
                        DMA("sync", lambda e, dc=dc, l0=l0: e.dma_start(
                            out=xs8[:, dc * LC:(dc + 1) * LC],
                            in_=x_in[dc * 128:(dc + 1) * 128, l0:l0 + LC]))
                        OP("tensor", lambda e, dc=dc: e.matmul(
                            prow[:], cx.ones_col_f32[:], xs8[:, dc * LC:(dc + 1) * LC],
                            start=(dc == 0), stop=(dc == ND - 1)))
                        OP("scalar", lambda e, dc=dc: e.activation(
                            sqs[:], xs8[:, dc * LC:(dc + 1) * LC], AF.Square))
                        OP("tensor", lambda e, dc=dc: e.matmul(
                            pb2[0:1, :], cx.ones_col_f32[:], sqs[:], start=(dc == 0), stop=(dc == ND - 1)))
                    OP("vector", lambda e: e.tensor_scalar(mean_r, prow[:], 1.0 / D, None, ALU.mult))
                    OP("vector", lambda e: e.tensor_scalar(msq_r, pb2[0:1, :], 1.0 / D, None, ALU.mult))
                    OP("vector", lambda e: e.tensor_tensor(A_r, mean_r, mean_r, ALU.mult))
                    OP("vector", lambda e: e.tensor_tensor(A_r, msq_r, A_r, ALU.subtract))
                    OP("scalar", lambda e: e.activation(A_r, A_r, AF.Sqrt, bias=eps_ln[0:1, :]))
                    OP("vector", lambda e: e.reciprocal(A_r, A_r))
                    OP("vector", lambda e: e.tensor_tensor(B_r, mean_r, A_r, ALU.mult))
                    OP("vector", lambda e: e.tensor_scalar(B_r, B_r, -1.0, None, ALU.mult))
                    cx.bcast_row(pa[:], A_r, LC)
                    cx.bcast_row(pb2[:], B_r, LC)
                    for dc in range(ND):
                        OP("vector", lambda e, dc=dc: e.tensor_tensor(
                            sqs[:], xs8[:, dc * LC:(dc + 1) * LC], pa[:], ALU.mult))
                        OP("vector", lambda e, dc=dc: e.tensor_tensor(
                            xs8[:, dc * LC:(dc + 1) * LC], sqs[:], pb2[:], ALU.add))
                        OP("vector", lambda e, dc=dc: e.tensor_copy(
                            xib[:], xs8[:, dc * LC:(dc + 1) * LC]))
                        DMA("sync", lambda e, dc=dc, l0=l0: e.dma_start(
                            out=xi_d[dc * 128:(dc + 1) * 128, l0:l0 + LC],
                            in_=xib[:]))
                        OP("scalar", lambda e, dc=dc: e.activation(
                            sqs[:], xs8[:, dc * LC:(dc + 1) * LC], AF.Square))
                        OP("tensor", lambda e, dc=dc: e.matmul(
                            prow[:], cx.ones_col_f32[:], sqs[:], start=(dc == 0), stop=(dc == ND - 1)))
                    OP("vector", lambda e: e.tensor_scalar(A_r, prow[:], 1.0 / D, None, ALU.mult))
                    OP("scalar", lambda e: e.activation(A_r, A_r, AF.Sqrt, bias=eps_out_c[0:1, :]))
                    OP("vector", lambda e: e.reciprocal(A_r, A_r))
                    cx.bcast_row(pa[:], A_r, LC)
                    for dc in range(ND):
                        OP("vector", lambda e, dc=dc: e.tensor_tensor(
                            sqs[:], xs8[:, dc * LC:(dc + 1) * LC], pa[:], ALU.mult))
                        OP("vector", lambda e: e.tensor_copy(xib[:], sqs[:]))
                        DMA("sync", lambda e, dc=dc, l0=l0: e.dma_start(
                            out=xnm_d[dc * 128:(dc + 1) * 128, l0:l0 + LC],
                            in_=xib[:]))
                OP("vector", lambda e: e.memset(state_sb[:], 0.0))
                OP("vector", lambda e: e.memset(state_bf[:], 0.0))

                for c in range(NCH):
                    l0 = b * SEQ + c * CH
                    for dc in range(ND):
                        DMA("sync", lambda e, dc=dc, l0=l0: e.dma_start(
                            out=xnmc[:, dc * CH:(dc + 1) * CH],
                            in_=xnm_d[dc * 128:(dc + 1) * 128, l0:l0 + CH]))
                    for lt in range(2):
                        for zb in range(4):
                            for dc in range(ND):
                                OP("tensor", lambda e, lt=lt, zb=zb, dc=dc: e.matmul(
                                    pa[:], xnmc[:, dc * CH + lt * 128: dc * CH + lt * 128 + 128],
                                    w_inb[:, dc * 4384 + zb * 512: dc * 4384 + (zb + 1) * 512],
                                    start=(dc == 0), stop=(dc == ND - 1)))
                            OP("scalar", lambda e, lt=lt, zb=zb: e.activation(
                                z_tm[:, lt * 2048 + zb * 512: lt * 2048 + (zb + 1) * 512],
                                pa[:], AF.Silu))
                    if c == 0:
                        for ch in range(18):
                            OP("vector", lambda e, ch=ch: e.memset(xbcf[:, ch * 260:ch * 260 + 3], 0.0))
                    for ch in range(18):
                        for dc in range(ND):
                            OP("tensor", lambda e, ch=ch, dc=dc: e.matmul(
                                pcb[:, 0:CH],
                                w_inb[:, dc * 4384 + 2048 + ch * 128: dc * 4384 + 2048 + (ch + 1) * 128],
                                xnmc[:, dc * CH:(dc + 1) * CH],
                                start=(dc == 0), stop=(dc == ND - 1)))
                        OP("vector", lambda e, ch=ch: e.tensor_copy(
                            xbcf[:, ch * 260 + 3:ch * 260 + 259], pcb[:, 0:CH]))
                    for ch in range(18):
                        base = ch * 260
                        OP("vector", lambda e, base=base, ch=ch: e.tensor_scalar(
                            t256a[:], xbcf[:, base + 3:base + 259], convw[:, ch * 4 + 3:ch * 4 + 4],
                            None, ALU.mult))
                        for k in range(3):
                            src_t = t256a if k % 2 == 0 else t256b
                            dst_t = t256b if k % 2 == 0 else t256a
                            OP("vector", lambda e, base=base, ch=ch, k=k, src_t=src_t, dst_t=dst_t:
                               e.scalar_tensor_tensor(
                                   dst_t[:], xbcf[:, base + k:base + k + 256],
                                   convw[:, ch * 4 + k:ch * 4 + k + 1], src_t[:],
                                   ALU.mult, ALU.add))
                        dst = (xh_fm[:, (ch) * CH:(ch + 1) * CH] if ch < 16 else
                               (B_fm[:] if ch == 16 else C_fm[:]))
                        OP("scalar", lambda e, dst=dst, ch=ch: e.activation(
                            dst, t256b[:], AF.Silu, bias=convb[:, ch:ch + 1]))
                        OP("vector", lambda e, base=base: e.tensor_copy(
                            xbcf[:, base:base + 3], xbcf[:, base + 256:base + 259]))
                    for dc in range(ND):
                        OP("tensor", lambda e, dc=dc: e.matmul(
                            pdt[:], w_inb[:, dc * 4384 + 4352: dc * 4384 + 4384],
                            xnmc[:, dc * CH:(dc + 1) * CH],
                            start=(dc == 0), stop=(dc == ND - 1)))
                    OP("vector", lambda e: e.tensor_scalar(dtr_f, pdt[:], dtb_col[:], None, ALU.add))
                    OP("scalar", lambda e: e.activation(scr32[:], dtr_f, AF.Abs))
                    OP("scalar", lambda e: e.activation(scr32[:], scr32[:], AF.Exp, scale=-1.0))
                    OP("scalar", lambda e: e.activation(scr32[:], scr32[:], AF.Ln, bias=1.0))
                    OP("vector", lambda e: e.tensor_scalar(dt_fm, dtr_f, 0.0, None, ALU.max))
                    OP("vector", lambda e: e.tensor_tensor(dt_fm, dt_fm, scr32[:], ALU.add))
                    OP("vector", lambda e: e.tensor_scalar(acum, dt_fm, negea[:], None, ALU.mult))
                    OP("vector", lambda e: e.tensor_tensor_scan(acum, acum, acum, 0.0, ALU.add, ALU.bypass))
                    OP("vector", lambda e: e.tensor_copy(endc[:, 0:1], acum[:, CH - 1:CH]))
                    OP("vector", lambda e: e.tensor_scalar(DSf, acum, endc[:, 0:1], None, ALU.subtract))
                    OP("scalar", lambda e: e.activation(DSf, DSf, AF.Exp, scale=-1.0))
                    OP("vector", lambda e: e.tensor_tensor(PREf, dt_fm, DSf, ALU.mult))
                    OP("scalar", lambda e: e.activation(EAf, acum, AF.Exp))
                    OP("scalar", lambda e: e.activation(endc[:, 1:2], endc[:, 0:1], AF.Exp))
                    OP("tensor", lambda e: e.transpose(cx.pt[0:1, 0:32], endc[:, 1:2], cx.ident_f32[0:32, 0:32]))
                    OP("vector", lambda e: e.tensor_copy(arow[:, 0:32], cx.pt[0:1, 0:32]))
                    cx.bcast_row(cx.pt[:, 0:32], arow[:, 0:32], 32)
                    OP("vector", lambda e: e.tensor_copy(dec_bc[:], cx.pt[:, 0:32]))
                    for (off, dstf) in ((2 * CH, acumT), (CH, dtT), (4 * CH, PRET)):
                        for lt in range(2):
                            OP("tensor", lambda e, off=off, lt=lt: e.transpose(
                                cx.pt[:, 0:32],
                                arena[0:32, off + lt * 128: off + (lt + 1) * 128],
                                cx.ident_f32[0:32, 0:32]))
                            OP("vector", lambda e, dstf=dstf, lt=lt: e.tensor_copy(
                                dstf(lt), cx.pt[:, 0:32]))
                    for ch in range(16):
                        for lt in range(2):
                            OP("tensor", lambda e, ch=ch, lt=lt: e.transpose(
                                ptb[:], xh_fm[:, ch * CH + lt * 128: ch * CH + lt * 128 + 128],
                                cx.ident_bf16[:]))
                            OP("vector", lambda e, ch=ch, lt=lt: e.tensor_copy(
                                xh_tm[:, lt * 2048 + ch * 128: lt * 2048 + (ch + 1) * 128], ptb[:]))
                    for st in range(2):
                        OP("tensor", lambda e, st=st: e.transpose(
                            ptb[:], B_fm[:, st * 128:(st + 1) * 128], cx.ident_bf16[:]))
                        OP("vector", lambda e, st=st: e.tensor_copy(
                            B_tm[:, st * 128:(st + 1) * 128], ptb[:]))
                    for st in range(2):
                        OP("tensor", lambda e, st=st: e.matmul(
                            pcb[:, st * CH:(st + 1) * CH], B_fm[:, st * 128:(st + 1) * 128],
                            C_fm[:], start=True, stop=True))
                    for h in range(32):
                        DMA("sync", lambda e, h=h: e.dma_start(
                            out=arow[:], in_=arena[h:h + 1, 2 * CH:3 * CH]))
                        for st in range(2):
                            cx.bcast_row(pb2[:, 0:CH], arow[:], CH)
                            OP("vector", lambda e, st=st, h=h: e.tensor_scalar(
                                t256a[:], pb2[:, 0:CH], acumT(st)[:, h:h + 1], None, ALU.subtract))
                            OP("vector", lambda e: e.tensor_scalar(
                                t256a[:], t256a[:], 0.0, None, ALU.min))
                            OP("scalar", lambda e: e.activation(t256a[:], t256a[:], AF.Exp))
                            OP("vector", lambda e, st=st: e.tensor_tensor(
                                t256b[:], ltm[:, st * CH:(st + 1) * CH], pcb[:, st * CH:(st + 1) * CH],
                                ALU.mult))
                            OP("vector", lambda e: e.tensor_tensor(t256b[:], t256b[:], t256a[:], ALU.mult))
                            OP("vector", lambda e, st=st, h=h: e.tensor_scalar(
                                PTs[:, st * CH:(st + 1) * CH], t256b[:], dtT(st)[:, h:h + 1],
                                None, ALU.mult))
                        DMA("sync", lambda e, h=h: e.dma_start(
                            out=arow[:], in_=arena[h:h + 1, 5 * CH:6 * CH]))
                        cx.bcast_row(pb2[:, 0:CH], arow[:], CH)
                        OP("vector", lambda e: e.tensor_tensor(Csh[:], C_fm[:], pb2[:, 0:CH], ALU.mult))
                        for lt in range(2):
                            for st in range(2):
                                OP("tensor", lambda e, lt=lt, st=st, h=h: e.matmul(
                                    py[:], PTs[:, st * CH + lt * 128: st * CH + lt * 128 + 128],
                                    xh_tm[:, st * 2048 + h * 64: st * 2048 + (h + 1) * 64],
                                    start=(st == 0), stop=False))
                            OP("tensor", lambda e, lt=lt, h=h: e.matmul(
                                py[:], Csh[:, lt * 128:(lt + 1) * 128],
                                state_bf[:, h * 64:(h + 1) * 64], start=False, stop=True))
                            OP("vector", lambda e, lt=lt, h=h: e.scalar_tensor_tensor(
                                y_tm[:, lt * 2048 + h * 64: lt * 2048 + (h + 1) * 64],
                                xh_tm[:, lt * 2048 + h * 64: lt * 2048 + (h + 1) * 64],
                                dskip_bc[:, h:h + 1], py[:], ALU.mult, ALU.add))
                        for st in range(2):
                            OP("vector", lambda e, st=st, h=h: e.tensor_scalar(
                                upre_sm[:, st * 64:(st + 1) * 64],
                                xh_tm[:, st * 2048 + h * 64: st * 2048 + (h + 1) * 64],
                                PRET(st)[:, h:h + 1], None, ALU.mult))
                        for st in range(2):
                            OP("tensor", lambda e, st=st, h=h: e.matmul(
                                pb2[:, 0:64], B_tm[:, st * 128:(st + 1) * 128],
                                upre_sm[:, st * 64:(st + 1) * 64],
                                start=(st == 0), stop=(st == 1)))
                        OP("vector", lambda e, h=h: e.scalar_tensor_tensor(
                            state_sb[:, h * 64:(h + 1) * 64], state_sb[:, h * 64:(h + 1) * 64],
                            dec_bc[:, h:h + 1], pb2[:, 0:64], ALU.mult, ALU.add))
                        OP("vector", lambda e, h=h: e.tensor_copy(
                            state_bf[:, h * 64:(h + 1) * 64], state_sb[:, h * 64:(h + 1) * 64]))
                    for lt in range(2):
                        OP("vector", lambda e, lt=lt: e.tensor_tensor(
                            g2buf[:], y_tm[:, lt * 2048:(lt + 1) * 2048],
                            z_tm[:, lt * 2048:(lt + 1) * 2048], ALU.mult))
                        OP("scalar", lambda e: e.activation(
                            gaten[:], g2buf[:], AF.Square, accum_out=mcol[:, 0:1]))
                        OP("vector", lambda e: e.tensor_scalar(mcol[:, 1:2], mcol[:, 0:1], 1.0 / 2048, None, ALU.mult))
                        OP("scalar", lambda e: e.activation(mcol[:, 1:2], mcol[:, 1:2], AF.Sqrt,
                                                            bias=epsg_col))
                        OP("vector", lambda e: e.reciprocal(mcol[:, 1:2], mcol[:, 1:2]))
                        OP("vector", lambda e: e.scalar_tensor_tensor(
                            gaten[:], g2buf[:], mcol[:, 1:2],
                            normw_bc[:], ALU.mult, ALU.mult))
                        for ch in range(16):
                            OP("tensor", lambda e, ch=ch: e.transpose(
                                ptb[:], gaten[:, ch * 128:(ch + 1) * 128],
                                cx.ident_bf16[:]))
                            OP("vector", lambda e, ch=ch, lt=lt: e.tensor_copy(
                                xh_fm[:, ch * CH + lt * 128: ch * CH + lt * 128 + 128], ptb[:]))
                    for dcO in range(ND):
                        for ch in range(16):
                            OP("tensor", lambda e, dcO=dcO, ch=ch: e.matmul(
                                pa[:, 0:CH], w_outb[:, ch * D + dcO * 128: ch * D + (dcO + 1) * 128],
                                xh_fm[:, ch * CH:(ch + 1) * CH],
                                start=(ch == 0), stop=(ch == 15)))
                        DMA("sync", lambda e, dcO=dcO, l0=l0: e.dma_start(
                            out=xtc[:], in_=xi_d[dcO * 128:(dcO + 1) * 128, l0:l0 + CH]))
                        OP("vector", lambda e, dcO=dcO: e.tensor_tensor(htc[:], xtc[:], pa[:, 0:CH], ALU.add))
                        DMA("sync", lambda e, dcO=dcO, l0=l0: e.dma_start(
                            out=h_d[dcO * 128:(dcO + 1) * 128, l0:l0 + CH], in_=htc[:]))

        with contextlib.ExitStack() as stB:
            sbb = lambda name, shape, dt: stB.enter_context(nc.sbuf_tensor(P + name, shape, dt))
            stageB = sbb("stageB", [128, 4096], F32)
            w_fcb = cx.load_w_bf16_in(stB, w_fc_d, D, 4096, "w_fcb", stage=stageB)
            w_prb = cx.load_w_bf16_in(stB, w_proj_d, 4096, D, "w_prb", stage=stageB)
            hs8 = stageB[:, 0:ND * LC]
            hnb = sbb("hnb", [128, ND * LC], BF16)
            r1 = sbb("r1", [128, LC], BF16)
            r2 = sbb("r2", [128, 32 * LC], BF16)
            sq2 = sbb("sq2", [128, LC], F32)
            rowB = sbb("rowB", [1, LC], F32)
            xt5 = sbb("xt5", [128, LC], F32)
            for lc in range(NLC):
                l0 = lc * LC
                for dc in range(ND):
                    DMA("sync", lambda e, dc=dc, l0=l0: e.dma_start(
                        out=hs8[:, dc * LC:(dc + 1) * LC],
                        in_=h_d[dc * 128:(dc + 1) * 128, l0:l0 + LC]))
                    OP("scalar", lambda e, dc=dc: e.activation(
                        sq2[:], hs8[:, dc * LC:(dc + 1) * LC], AF.Square))
                    OP("tensor", lambda e, dc=dc: e.matmul(
                        prow[:], cx.ones_col_f32[:], sq2[:], start=(dc == 0), stop=(dc == ND - 1)))
                OP("vector", lambda e: e.tensor_scalar(rowB[:], prow[:], 1.0 / D, None, ALU.mult))
                OP("scalar", lambda e: e.activation(rowB[:], rowB[:], AF.Sqrt, bias=eps_out_c[0:1, :]))
                OP("vector", lambda e: e.reciprocal(rowB[:], rowB[:]))
                cx.bcast_row(pa[:], rowB[:], LC)
                for dc in range(ND):
                    OP("vector", lambda e, dc=dc: e.tensor_tensor(
                        hnb[:, dc * LC:(dc + 1) * LC], hs8[:, dc * LC:(dc + 1) * LC],
                        pa[:], ALU.mult))
                for ft in range(32):
                    for dc in range(ND):
                        OP("tensor", lambda e, ft=ft, dc=dc: e.matmul(
                            pa[:], w_fcb[:, dc * 4096 + ft * 128: dc * 4096 + ft * 128 + 128],
                            hnb[:, dc * LC:(dc + 1) * LC],
                            start=(dc == 0), stop=(dc == ND - 1)))
                    OP("scalar", lambda e, ft=ft: e.activation(r1[:], pa[:], AF.Relu))
                    OP("scalar", lambda e, ft=ft: e.activation(
                        r2[:, ft * LC:(ft + 1) * LC], r1[:], AF.Square))
                for dc2 in range(ND):
                    for ft in range(32):
                        OP("tensor", lambda e, dc2=dc2, ft=ft: e.matmul(
                            pb2[:], w_prb[:, ft * D + dc2 * 128: ft * D + dc2 * 128 + 128],
                            r2[:, ft * LC:(ft + 1) * LC],
                            start=(ft == 0), stop=(ft == 31)))
                    DMA("sync", lambda e, dc2=dc2, l0=l0: e.dma_start(
                        out=xt5[:], in_=x_in[dc2 * 128:(dc2 + 1) * 128, l0:l0 + LC]))
                    OP("vector", lambda e, dc2=dc2: e.tensor_tensor(
                        xt5[:], xt5[:], hs8[:, dc2 * LC:(dc2 + 1) * LC], ALU.add))
                    OP("vector", lambda e: e.tensor_tensor(xt5[:], xt5[:], pb2[:], ALU.add))
                    DMA("sync", lambda e, dc2=dc2, l0=l0: e.dma_start(
                        out=x_out[dc2 * 128:(dc2 + 1) * 128, l0:l0 + LC], in_=xt5[:]))

        if own:
            sq.emit(nc, sems)
    return nc


def host_ltm():
    ltm = np.zeros((128, 512), np.float32)
    for st in range(2):
        for p in range(128):
            s = st * 128 + p
            ltm[p, st * 256 + s:st * 256 + 256] = 1.0
    return ltm


def run_s5(x_fm_shards, weights):
    nc = build_s5()
    hc = host_constants()
    in_maps = []
    for c in range(N_CORES):
        m = {"x": x_fm_shards[c],
             "w_in": weights["L1_attn_w_in"], "conv_w": weights["L1_attn_conv_w"],
             "conv_b": weights["L1_attn_conv_b"], "dt_bias": weights["L1_attn_dt_bias"],
             "a_log": weights["L1_attn_a_log"], "d_skip": weights["L1_attn_d_skip"],
             "norm_w": weights["L1_attn_norm_w"], "w_out": weights["L1_attn_w_out"],
             "w_fc": weights["L1_attn_w_fc"], "w_proj": weights["L1_attn_w_proj"],
             "ltm": host_ltm()}
        m.update(hc)
        in_maps.append(m)
    res = run_bass_kernel_spmd(nc, in_maps, core_ids=list(range(N_CORES)))
    return [r["xo"] for r in res.results]


def build_fused():
    import contextlib
    nc = bass.Bass("TRN2", target_bir_lowering=False)
    x_in = nc.dram_tensor("x", [D, L], F32, kind="ExternalInput").ap()
    x_out = nc.dram_tensor("xo", [D, L], F32, kind="ExternalOutput").ap()
    xA = nc.dram_tensor("xA_i", [D, L], F32, kind="Internal").ap()
    xB = nc.dram_tensor("xB_i", [D, L], F32, kind="Internal").ap()
    caps = declare_constants(nc)
    w1d = declare_w_s1(nc)
    w2d = declare_w_s2(nc)
    w4d = declare_w_s4(nc)
    w5d = declare_w_s5(nc)
    sq = Seq()
    with contextlib.ExitStack() as stack:
        sems = [stack.enter_context(nc.semaphore(f"S{i}")) for i in range(24)]
        ccs = [stack.enter_context(nc.semaphore(f"SCC{i}")) for i in range(2)]
        mk = lambda P, xi, xo, w: {"nc": nc, "sq": sq, "caps": caps, "P": P,
                                   "x_in": xi, "x_out": xo, "w": w}
        build_s1(fctx=mk("a_", x_in, xA, w1d))
        build_s2(fctx=mk("b_", xA, xB, w2d))
        build_s1(fctx=mk("c_", xB, xA, w1d))
        build_s4(fctx=mk("d_", xA, xB, w4d))
        build_s5(fctx=mk("e_", xB, xA, w5d))
        build_s4(fctx=mk("f_", xA, x_out, w4d), skip_conv=True)
        sq.emit(nc, sems, ccs)
    return nc


def fused_in_maps(x_fm_shards, d):
    hc = host_constants()
    Wq, Wk, Wv = _reorder_W_qkv(d["L0_attn_W"])
    wrT = np.ascontiguousarray(d["L1_ffn_wr"].T)
    w1 = np.ascontiguousarray(d["L1_ffn_w1"].reshape(8 * 1024, 4096))
    w2 = np.ascontiguousarray(d["L1_ffn_w2"].reshape(8 * 4096, 1024))
    b1 = np.ascontiguousarray(d["L1_ffn_b1"].reshape(-1))
    b2 = np.ascontiguousarray(d["L1_ffn_b2"].reshape(-1))
    in_maps = []
    for c in range(N_CORES):
        cm = np.zeros((8, 1), np.float32)
        cm[:c] = 1.0
        m = {"x": x_fm_shards[c],
             "cn_w1": d["L0_ffn_cn_w1"], "cn_w2": d["L0_ffn_cn_w2"],
             "cn_w3": d["L0_ffn_cn_w3"], "cn_w4": d["L0_ffn_cn_w4"],
             "gc_w1": d["L0_ffn_gc_w1"], "gc_w2": d["L0_ffn_gc_w2"],
             "gc_w3": d["L0_ffn_gc_w3"],
             "cn_b1": d["L0_ffn_cn_b1"], "cn_b2": d["L0_ffn_cn_b2"],
             "cn_b3": d["L0_ffn_cn_b3"], "cn_b4": d["L0_ffn_cn_b4"],
             "gc_b1": d["L0_ffn_gc_b1"], "gc_b2": d["L0_ffn_gc_b2"],
             "gc_b3": d["L0_ffn_gc_b3"], "ar": d["L0_ffn_adaptive_range"],
             "wq": Wq, "wk": Wk, "wv": Wv,
             "wrT": wrT, "w1": w1, "w2": w2, "b1": b1, "b2": b2, "cmask": cm,
             "w_in": d["L1_attn_w_in"], "conv_w": d["L1_attn_conv_w"],
             "conv_b": d["L1_attn_conv_b"], "dt_bias": d["L1_attn_dt_bias"],
             "a_log": d["L1_attn_a_log"], "d_skip": d["L1_attn_d_skip"],
             "norm_w": d["L1_attn_norm_w"], "w_out": d["L1_attn_w_out"],
             "w_fc": d["L1_attn_w_fc"], "w_proj": d["L1_attn_w_proj"],
             "ltm": host_ltm()}
        m.update(hc)
        in_maps.append(m)
    return in_maps


def kernel_fused(**inputs):
    x = inputs["L0_attn_x"]
    shards = to_fm_shards(x)
    nc = build_fused()
    in_maps = fused_in_maps(shards, inputs)
    res = run_bass_kernel_spmd(nc, in_maps, core_ids=list(range(N_CORES)))
    return from_fm_shards([r["xo"] for r in res.results])


# version 08
# speedup vs baseline: 1.5684x; 1.0008x over previous
import numpy as np
import concourse.bass as bass
import concourse.mybir as mybir
from concourse.bass_utils import run_bass_kernel_spmd

F32 = mybir.dt.float32
BF16 = mybir.dt.bfloat16
AF = mybir.ActivationFunctionType
ALU = mybir.AluOpType
AX = mybir.AxisListType.X

N_CORES = 8
D = 1024
L = 4096
SEQ = 2048
NB = 2
LC = 512
NLC = L // LC
ND = D // 128


class Seq:

    def __init__(self):
        self.ops = []

    def op(self, eng, fn, inc=1):
        assert not getattr(self, "frozen", False), "Seq.op after emit started"
        self.ops.append((eng, fn, inc, False))

    def dma(self, eng, fn):
        assert not getattr(self, "frozen", False), "Seq.dma after emit started"
        self.ops.append((eng, fn, 16, False))

    def cc(self, fn):
        assert not getattr(self, "frozen", False)
        idx = getattr(self, "n_cc", 0)
        self.n_cc = idx + 1
        self.ops.append(("gpsimd", fn, 0, ("cc", idx)))

    EPOCH_LIMIT = 24000

    def emit(self, nc, sems, cc_sems=()):
        self.frozen = True
        assert getattr(self, "n_cc", 0) <= len(cc_sems)
        n = len(self.ops)
        epoch = []
        epoch_cum = []
        ep, cum = 0, 0
        for (eng, fn, inc, vl) in self.ops:
            if cum + inc > self.EPOCH_LIMIT:
                ep += 1
                cum = 0
            epoch.append(ep)
            epoch_cum.append(cum)
            cum += inc
        n_ep = ep + 1
        assert n_ep <= len(sems), f"need {n_ep} semaphores, have {len(sems)}"
        totals = [0] * n_ep
        for i, (eng, fn, inc, vl) in enumerate(self.ops):
            totals[epoch[i]] = epoch_cum[i] + inc

        def section(engobj, name):
            last_ep_seen = -1
            for i, (eng, fn, inc, vl) in enumerate(self.ops):
                if eng != name:
                    continue
                ep_i = epoch[i]
                if ep_i != last_ep_seen:
                    for k in range(last_ep_seen if last_ep_seen >= 0 else 0, ep_i):
                        engobj.wait_ge(sems[k], totals[k])
                    last_ep_seen = ep_i
                prev_same = (name in ("tensor", "scalar")
                             and i > 0 and self.ops[i - 1][0] == name
                             and epoch[i - 1] == ep_i
                             and not isinstance(self.ops[i - 1][3], tuple))
                if epoch_cum[i] > 0 and not prev_same:
                    engobj.wait_ge(sems[ep_i], epoch_cum[i])
                if i > 0 and isinstance(self.ops[i - 1][3], tuple):
                    engobj.wait_ge(cc_sems[self.ops[i - 1][3][1]], 1)
                r = fn(engobj)
                if isinstance(vl, tuple):
                    r.then_inc(cc_sems[vl[1]])
                elif inc > 0:
                    r.then_inc(sems[ep_i], inc)
            if name == "tensor":
                for k in range(n_ep):
                    engobj.wait_ge(sems[k], totals[k])

        with nc.Block() as block:
            @block.sync
            def _(e):
                section(e, "sync")

            @block.vector
            def _(e):
                section(e, "vector")

            @block.scalar
            def _(e):
                section(e, "scalar")

            @block.tensor
            def _(e):
                section(e, "tensor")

            @block.gpsimd
            def _(e):
                section(e, "gpsimd")


def host_constants():
    c = {}
    c["ident_bf16"] = np.eye(128, dtype=np.float32)
    import ml_dtypes
    c["ident_bf16"] = np.eye(128, dtype=ml_dtypes.bfloat16)
    c["ident_f32"] = np.eye(128, dtype=np.float32)
    c["ones_row_f32"] = np.ones((1, 128), dtype=np.float32)
    c["ones_row_bf16"] = np.ones((1, 128), dtype=ml_dtypes.bfloat16)
    c["ones_col_f32"] = np.ones((128, 1), dtype=np.float32)
    c["ones_col_bf16"] = np.ones((128, 1), dtype=ml_dtypes.bfloat16)
    c["ut128"] = np.triu(np.ones((128, 128), dtype=np.float32))
    return c


def declare_constants(nc):
    aps = {}
    aps["ident_bf16"] = nc.dram_tensor("ident_bf16", [128, 128], BF16, kind="ExternalInput").ap()
    aps["ident_f32"] = nc.dram_tensor("ident_f32", [128, 128], F32, kind="ExternalInput").ap()
    aps["ones_row_f32"] = nc.dram_tensor("ones_row_f32", [1, 128], F32, kind="ExternalInput").ap()
    aps["ones_row_bf16"] = nc.dram_tensor("ones_row_bf16", [1, 128], BF16, kind="ExternalInput").ap()
    aps["ones_col_f32"] = nc.dram_tensor("ones_col_f32", [128, 1], F32, kind="ExternalInput").ap()
    aps["ones_col_bf16"] = nc.dram_tensor("ones_col_bf16", [128, 1], BF16, kind="ExternalInput").ap()
    aps["ut128"] = nc.dram_tensor("ut128", [128, 128], F32, kind="ExternalInput").ap()
    return aps


class Ctx:

    def __init__(self, nc, sq, stack, caps, psum_scratch=True, prefix=""):
        self.nc = nc
        self.sq = sq
        self.P = prefix
        sb = lambda name, shape, dt: stack.enter_context(nc.sbuf_tensor(prefix + name, shape, dt))
        ps = lambda name, shape, dt: stack.enter_context(nc.psum_tensor(prefix + name, shape, dt))
        self.sb = sb
        self.ps = ps
        self.ident_bf16 = sb("c_ident_b", [128, 128], BF16)
        self.ident_f32 = sb("c_ident_f", [128, 128], F32)
        self.ones_row_f32 = sb("c_or_f", [1, 128], F32)
        self.ones_row_bf16 = sb("c_or_b", [1, 128], BF16)
        self.ones_col_f32 = sb("c_oc_f", [128, 1], F32)
        self.ones_col_bf16 = sb("c_oc_b", [128, 1], BF16)
        self.consts = sb("c_consts", [128, 8], F32)
        self.n_consts = 0
        self.const_map = {}
        for name in ["ident_bf16", "ident_f32", "ones_row_f32", "ones_row_bf16",
                     "ones_col_f32", "ones_col_bf16"]:
            dst = getattr(self, name)
            src = caps[name]
            self.sq.dma("sync", lambda e, d=dst, s=src: e.dma_start(out=d[:], in_=s))
        if psum_scratch:
            self.pt = ps("p_tr", [128, 128], F32)

    def const_col(self, val):
        if val in self.const_map:
            return self.const_map[val]
        i = self.n_consts
        assert i < 8
        self.n_consts += 1
        col = self.consts[:, i:i + 1]
        self.sq.op("vector", lambda e, c=col, v=val: e.memset(c, v))
        self.const_map[val] = col
        return col

    def load_vec_cols(self, dram_ap, n, name, dt=F32):
        k = n // 128
        assert k * 128 == n and k <= 128
        stage = self.sb(name + "_stg", [max(k, 1), 128], F32)
        out = self.sb(name, [128, k], dt)
        self.sq.dma("sync", lambda e: e.dma_start(
            out=stage[:], in_=dram_ap.rearrange("(k p) -> k p", k=k)))
        self.sq.op("tensor", lambda e: e.transpose(self.pt[:, 0:k], stage[:], self.ident_f32[0:k, 0:k]))
        self.sq.op("vector", lambda e: e.tensor_copy(out[:], self.pt[:, 0:k]))
        return out

    def load_scalar(self, dram_ap, name):
        out = self.sb(name + "_sb", [1, 1], F32)
        self.sq.dma("sync", lambda e: e.dma_start(out=out[:], in_=dram_ap.rearrange("(a x) -> a x", a=1)))
        return out

    def load_w_bf16_in(self, stack, dram_ap, din, dout, name, stage=None):
        ndc = din // 128
        if stage is None:
            stage = stack.enter_context(self.nc.sbuf_tensor(self.P + name + "_stg", [128, dout], F32))
        out = stack.enter_context(self.nc.sbuf_tensor(self.P + name, [128, ndc * dout], BF16))
        for dc in range(ndc):
            self.sq.dma("sync", lambda e, dc=dc: e.dma_start(
                out=stage[:, 0:dout], in_=dram_ap[dc * 128:(dc + 1) * 128, :]))
            self.sq.op("vector", lambda e, dc=dc: e.tensor_copy(
                out[:, dc * dout:(dc + 1) * dout], stage[:, 0:dout]))
        return out

    def load_w_bf16(self, dram_ap, din, dout, name):
        ndc = din // 128
        stage = self.sb(name + "_stg", [128, dout], F32)
        out = self.sb(name, [128, ndc * dout], BF16)
        for dc in range(ndc):
            self.sq.dma("sync", lambda e, dc=dc: e.dma_start(
                out=stage[:], in_=dram_ap[dc * 128:(dc + 1) * 128, :]))
            self.sq.op("vector", lambda e, dc=dc: e.tensor_copy(
                out[:, dc * dout:(dc + 1) * dout], stage[:]))
        return out

    def bcast_row(self, psum_ap, row_ap, n, dt="f32", m=128):
        ones = self.ones_row_f32 if dt == "f32" else self.ones_row_bf16
        self.sq.op("tensor", lambda e: e.matmul(psum_ap, ones[:, 0:m], row_ap, start=True, stop=True))


def declare_w_s1(nc):
    w = {}
    for nm, shp in [("cn_w1", [D, D]), ("cn_w2", [D, 512]), ("cn_w3", [512, 256]),
                    ("cn_w4", [256, 1]), ("gc_w1", [D, 512]), ("gc_w2", [512, 256]),
                    ("gc_w3", [256, 1])]:
        w[nm] = nc.dram_tensor(nm, shp, F32, kind="ExternalInput").ap()
    for nm, n in [("cn_b1", D), ("cn_b2", 512), ("cn_b3", 256), ("gc_b1", 512),
                  ("gc_b2", 256)]:
        w[nm] = nc.dram_tensor(nm, [n], F32, kind="ExternalInput").ap()
    for nm in ["cn_b4", "gc_b3", "ar"]:
        w[nm] = nc.dram_tensor(nm, [1], F32, kind="ExternalInput").ap()
    return w


def build_s1(n_lc=NLC, n_cyc=3, debug=False, fctx=None):
    import contextlib
    if fctx is None:
        nc = bass.Bass("TRN2", target_bir_lowering=False)
        x_in = nc.dram_tensor("x", [D, L], F32, kind="ExternalInput").ap()
        x_out = nc.dram_tensor("xo", [D, L], F32, kind="ExternalOutput").ap()
        w = declare_w_s1(nc)
        caps = declare_constants(nc)
        sq = Seq()
        P = ""
        own = True
    else:
        nc, sq, caps, P, x_in, x_out, w = (fctx["nc"], fctx["sq"], fctx["caps"],
                                           fctx["P"], fctx["x_in"], fctx["x_out"], fctx["w"])
        own = False
    dbg = nc.dram_tensor("dbg", [8, L], F32, kind="ExternalOutput").ap() if debug else None

    with contextlib.ExitStack() as stack:
        cx = Ctx(nc, sq, stack, caps, prefix=P)
        sb, ps = cx.sb, cx.ps
        if own:
            sems = [stack.enter_context(nc.semaphore(f"S{i}")) for i in range(12)]

        w1 = cx.load_w_bf16(w["cn_w1"], D, D, "w1")
        w2 = cx.load_w_bf16(w["cn_w2"], D, 512, "w2")
        w3 = cx.load_w_bf16(w["cn_w3"], 512, 256, "w3")
        w4 = cx.load_w_bf16(w["cn_w4"], 256, 1, "w4")
        g1w = cx.load_w_bf16(w["gc_w1"], D, 512, "g1w")
        g2w = cx.load_w_bf16(w["gc_w2"], 512, 256, "g2w")
        g3w = cx.load_w_bf16(w["gc_w3"], 256, 1, "g3w")
        b1 = cx.load_vec_cols(w["cn_b1"], D, "b1")
        b2 = cx.load_vec_cols(w["cn_b2"], 512, "b2")
        b3 = cx.load_vec_cols(w["cn_b3"], 256, "b3")
        gb1 = cx.load_vec_cols(w["gc_b1"], 512, "gb1")
        gb2 = cx.load_vec_cols(w["gc_b2"], 256, "gb2")
        b4 = cx.load_scalar(w["cn_b4"], "b4")
        gb3 = cx.load_scalar(w["gc_b3"], "gb3")
        ar = cx.load_scalar(w["ar"], "ar")

        outb = sb("outb", [128, ND * L], BF16)
        h1b = sb("h1b", [128, 8 * LC], BF16)
        h2b = sb("h2b", [128, 4 * LC], BF16)
        h3b = sb("h3b", [128, 2 * LC], BF16)
        xs8 = sb("xs8", [128, ND * LC], F32)
        sqs = sb("sqs", [128, LC], F32)
        rows = sb("rows", [1, 4 * LC], F32)
        mean_r = rows[:, 0 * LC:1 * LC]
        msq_r = rows[:, 1 * LC:2 * LC]
        A_r = rows[:, 2 * LC:3 * LC]
        B_r = rows[:, 3 * LC:4 * LC]
        tf_r = sb("tf_r", [1, LC], F32)
        gc = sb("gc", [128, 2 * ND], F32)
        gcb = sb("gcb", [128, 2 * ND], BF16)
        g1o = sb("g1o", [128, 2 * 4], BF16)
        g2o = sb("g2o", [128, 2 * 2], BF16)
        gf = sb("gf", [1, 2], F32)
        s_ap = sb("s_ap", [1, 1], F32)
        o_ap = sb("o_ap", [1, 2], F32)
        tmp1 = sb("tmp1", [1, 2], F32)
        twt = sb("twt", [128, LC], F32)
        twu = sb("twu", [128, LC], F32)

        eps_col = cx.const_col(1e-5)
        pa = ps("pa", [128, LC], F32)
        pb = ps("pb", [128, LC], F32)
        prow = ps("prow", [1, LC], F32)
        prow2 = ps("prow2", [1, LC], F32)
        pg = ps("pg", [128, 1], F32)

        OP = sq.op
        DMA = sq.dma

        for lc in range(n_lc):
            l0 = lc * LC
            b = 0 if lc < NLC // 2 else 1
            for dc in range(ND):
                DMA("sync", lambda e, dc=dc, l0=l0: e.dma_start(
                    out=xs8[:, dc * LC:(dc + 1) * LC],
                    in_=x_in[dc * 128:(dc + 1) * 128, l0:l0 + LC]))
                OP("tensor", lambda e, dc=dc: e.matmul(
                    prow[:], cx.ones_col_f32[:], xs8[:, dc * LC:(dc + 1) * LC],
                    start=(dc == 0), stop=(dc == ND - 1)))
                OP("scalar", lambda e, dc=dc: e.activation(
                    sqs[:], xs8[:, dc * LC:(dc + 1) * LC], AF.Square))
                OP("tensor", lambda e, dc=dc: e.matmul(
                    prow2[:], cx.ones_col_f32[:], sqs[:], start=(dc == 0), stop=(dc == ND - 1)))
            OP("vector", lambda e: e.tensor_scalar(mean_r, prow[:], 1.0 / D, None, ALU.mult))
            OP("vector", lambda e: e.tensor_scalar(msq_r, prow2[:], 1.0 / D, None, ALU.mult))
            OP("vector", lambda e: e.tensor_tensor(A_r, mean_r, mean_r, ALU.mult))
            OP("vector", lambda e: e.tensor_tensor(A_r, msq_r, A_r, ALU.subtract))
            OP("scalar", lambda e: e.activation(A_r, A_r, AF.Sqrt, bias=eps_col[0:1, :]))
            OP("vector", lambda e: e.reciprocal(A_r, A_r))
            OP("vector", lambda e: e.tensor_tensor(B_r, mean_r, A_r, ALU.mult))
            OP("vector", lambda e: e.tensor_scalar(B_r, B_r, -1.0, None, ALU.mult))
            if debug and lc == 0:
                DMA("sync", lambda e: e.dma_start(out=dbg[0:1, 0:4*LC], in_=rows[:]))
            cx.bcast_row(pa[:], A_r, LC)
            cx.bcast_row(pb[:], B_r, LC)
            for dc in range(ND):
                ob = lambda dc=dc, l0=l0: outb[:, dc * L + l0:dc * L + l0 + LC]
                OP("vector", lambda e, dc=dc: e.tensor_tensor(
                    twt[:], xs8[:, dc * LC:(dc + 1) * LC], pa[:], ALU.mult))
                OP("vector", lambda e, ob=ob: e.tensor_tensor(ob(), twt[:], pb[:], ALU.add))
        for b in range(NB):
            for dc in range(ND):
                seg = outb[:, dc * L + b * SEQ: dc * L + (b + 1) * SEQ]
                OP("vector", lambda e, seg=seg, b=b, dc=dc: e.tensor_reduce(
                    gc[:, b * ND + dc:b * ND + dc + 1], seg, AX, ALU.add))
        OP("vector", lambda e: e.tensor_scalar(gcb[:], gc[:], 1.0 / SEQ, None, ALU.mult))

        for b in range(NB):
            for ft in range(4):
                for dc in range(ND):
                    OP("tensor", lambda e, b=b, ft=ft, dc=dc: e.matmul(
                        pg[:], g1w[:, dc * 512 + ft * 128: dc * 512 + ft * 128 + 128],
                        gcb[:, b * ND + dc:b * ND + dc + 1],
                        start=(dc == 0), stop=(dc == ND - 1)))
                OP("scalar", lambda e, b=b, ft=ft: e.activation(
                    g1o[:, b * 4 + ft:b * 4 + ft + 1], pg[:], AF.Gelu, bias=gb1[:, ft:ft + 1]))
            for ft in range(2):
                for dc in range(4):
                    OP("tensor", lambda e, b=b, ft=ft, dc=dc: e.matmul(
                        pg[:], g2w[:, dc * 256 + ft * 128: dc * 256 + ft * 128 + 128],
                        g1o[:, b * 4 + dc:b * 4 + dc + 1],
                        start=(dc == 0), stop=(dc == 3)))
                OP("scalar", lambda e, b=b, ft=ft: e.activation(
                    g2o[:, b * 2 + ft:b * 2 + ft + 1], pg[:], AF.Gelu, bias=gb2[:, ft:ft + 1]))
            for dc in range(2):
                OP("tensor", lambda e, b=b, dc=dc: e.matmul(
                    pg[0:1, :], g3w[:, dc:dc + 1], g2o[:, b * 2 + dc:b * 2 + dc + 1],
                    start=(dc == 0), stop=(dc == 1)))
            OP("scalar", lambda e, b=b: e.activation(
                gf[:, b:b + 1], pg[0:1, :], AF.Sigmoid, bias=gb3[:]))
        if debug:
            DMA("sync", lambda e: e.dma_start(out=dbg[1:2, 0:2], in_=gf[:]))
        OP("vector", lambda e: e.tensor_scalar(s_ap[:], ar[:], 0.6, None, ALU.mult))
        OP("vector", lambda e: e.tensor_scalar(tmp1[:, 0:1], ar[:], 1.4, None, ALU.mult))
        for b in range(NB):
            OP("vector", lambda e, b=b: e.tensor_tensor(
                o_ap[:, b:b + 1], gf[:, b:b + 1], tmp1[:, 0:1], ALU.mult))
            OP("vector", lambda e, b=b: e.tensor_tensor(
                o_ap[:, b:b + 1], o_ap[:, b:b + 1], ar[:], ALU.subtract))
            OP("vector", lambda e, b=b: e.tensor_scalar(
                o_ap[:, b:b + 1], o_ap[:, b:b + 1], 0.1, None, ALU.add))

        SGN = [1.0, 1.0, -1.0, -1.0, -1.0, -1.0, 1.0, 1.0]
        SRC = [4, 5, 6, 7, 0, 1, 2, 3]
        for cyc in range(n_cyc):
            for lc in range(n_lc):
                l0 = lc * LC
                b = 0 if lc < NLC // 2 else 1
                ob = lambda dc, l0=l0: outb[:, dc * L + l0:dc * L + l0 + LC]
                for ft in range(8):
                    for dc in range(ND):
                        OP("tensor", lambda e, ft=ft, dc=dc, ob=ob: e.matmul(
                            pa[:], w1[:, dc * D + ft * 128: dc * D + ft * 128 + 128],
                            ob(dc),
                            start=(dc == 0), stop=(dc == ND - 1)))
                    OP("scalar", lambda e, ft=ft: e.activation(
                        h1b[:, ft * LC:(ft + 1) * LC], pa[:], AF.Gelu, bias=b1[:, ft:ft + 1]))
                for ft in range(4):
                    for dc in range(8):
                        OP("tensor", lambda e, ft=ft, dc=dc: e.matmul(
                            pa[:], w2[:, dc * 512 + ft * 128: dc * 512 + ft * 128 + 128],
                            h1b[:, dc * LC:(dc + 1) * LC],
                            start=(dc == 0), stop=(dc == 7)))
                    OP("scalar", lambda e, ft=ft: e.activation(
                        h2b[:, ft * LC:(ft + 1) * LC], pa[:], AF.Gelu, bias=b2[:, ft:ft + 1]))
                for ft in range(2):
                    for dc in range(4):
                        OP("tensor", lambda e, ft=ft, dc=dc: e.matmul(
                            pa[:], w3[:, dc * 256 + ft * 128: dc * 256 + ft * 128 + 128],
                            h2b[:, dc * LC:(dc + 1) * LC],
                            start=(dc == 0), stop=(dc == 3)))
                    OP("scalar", lambda e, ft=ft: e.activation(
                        h3b[:, ft * LC:(ft + 1) * LC], pa[:], AF.Gelu, bias=b3[:, ft:ft + 1]))
                for dc in range(2):
                    OP("tensor", lambda e, dc=dc: e.matmul(
                        prow[:], w4[:, dc:dc + 1], h3b[:, dc * LC:(dc + 1) * LC],
                        start=(dc == 0), stop=(dc == 1)))
                OP("scalar", lambda e: e.activation(tf_r[:], prow[:], AF.Sigmoid, bias=b4[:]))
                OP("vector", lambda e: e.tensor_scalar(tf_r[:], tf_r[:], s_ap[:], None, ALU.mult))
                OP("vector", lambda e, b=b: e.tensor_scalar(
                    tf_r[:], tf_r[:], o_ap[:, b:b + 1], None, ALU.add))
                if debug and cyc == 0:
                    DMA("sync", lambda e, lc=lc: e.dma_start(out=dbg[2:3, lc*LC:(lc+1)*LC], in_=tf_r[:]))
                cx.bcast_row(pb[:], tf_r[:], LC)
                for c in range(4):
                    s2 = SRC[c]
                    OP("vector", lambda e, ob=ob, c=c: e.tensor_copy(twt[:], ob(c)))
                    OP("vector", lambda e, ob=ob, s2=s2: e.tensor_tensor(twu[:], ob(s2), pb[:], ALU.mult))
                    OP("vector", lambda e, ob=ob, c=c: e.scalar_tensor_tensor(
                        ob(c), twu[:], SGN[c], ob(c), ALU.mult, ALU.add))
                    OP("vector", lambda e: e.tensor_tensor(twu[:], twt[:], pb[:], ALU.mult))
                    OP("vector", lambda e, ob=ob, s2=s2: e.scalar_tensor_tensor(
                        ob(s2), twu[:], SGN[s2], ob(s2), ALU.mult, ALU.add))

        for lc in range(n_lc):
            l0 = lc * LC
            for dc in range(ND):
                DMA("sync", lambda e, dc=dc, l0=l0: e.dma_start(
                    out=xs8[:, dc * LC:(dc + 1) * LC],
                    in_=x_in[dc * 128:(dc + 1) * 128, l0:l0 + LC]))
                OP("vector", lambda e, dc=dc, l0=l0: e.scalar_tensor_tensor(
                    xs8[:, dc * LC:(dc + 1) * LC],
                    outb[:, dc * L + l0:dc * L + l0 + LC], 0.5,
                    xs8[:, dc * LC:(dc + 1) * LC], ALU.mult, ALU.add))
                DMA("sync", lambda e, dc=dc, l0=l0: e.dma_start(
                    out=x_out[dc * 128:(dc + 1) * 128, l0:l0 + LC],
                    in_=xs8[:, dc * LC:(dc + 1) * LC]))

        if own:
            sq.emit(nc, sems)
    return nc


def _chunked(c):
    return c


def run_s1(x_fm_shards, weights):
    nc = build_s1()
    hc = host_constants()
    in_maps = []
    for c in range(N_CORES):
        m = {"x": x_fm_shards[c]}
        m.update({
            "cn_w1": weights["L0_ffn_cn_w1"], "cn_w2": weights["L0_ffn_cn_w2"],
            "cn_w3": weights["L0_ffn_cn_w3"], "cn_w4": weights["L0_ffn_cn_w4"],
            "gc_w1": weights["L0_ffn_gc_w1"], "gc_w2": weights["L0_ffn_gc_w2"],
            "gc_w3": weights["L0_ffn_gc_w3"],
            "cn_b1": weights["L0_ffn_cn_b1"], "cn_b2": weights["L0_ffn_cn_b2"],
            "cn_b3": weights["L0_ffn_cn_b3"], "cn_b4": weights["L0_ffn_cn_b4"],
            "gc_b1": weights["L0_ffn_gc_b1"], "gc_b2": weights["L0_ffn_gc_b2"],
            "gc_b3": weights["L0_ffn_gc_b3"], "ar": weights["L0_ffn_adaptive_range"],
        })
        m.update(hc)
        in_maps.append(m)
    res = run_bass_kernel_spmd(nc, in_maps, core_ids=list(range(N_CORES)))
    return [r["xo"] for r in res.results]


def to_fm_shards(x):
    out = []
    for c in range(N_CORES):
        xc = x[2 * c:2 * c + 2].reshape(L, D).T
        out.append(np.ascontiguousarray(xc))
    return out


def from_fm_shards(shards):
    x = np.empty((16, SEQ, D), dtype=np.float32)
    for c in range(N_CORES):
        x[2 * c:2 * c + 2] = shards[c].T.reshape(2, SEQ, D)
    return x


def kernel(**inputs):
    return kernel_fused(**inputs)


def declare_w_s2(nc):
    return {"wq": nc.dram_tensor("wq", [D, D], F32, kind="ExternalInput").ap(),
            "wk": nc.dram_tensor("wk", [D, D], F32, kind="ExternalInput").ap(),
            "wv": nc.dram_tensor("wv", [D, D], F32, kind="ExternalInput").ap()}


def build_s2(fctx=None):
    import contextlib
    if fctx is None:
        nc = bass.Bass("TRN2", target_bir_lowering=False)
        x_in = nc.dram_tensor("x", [D, L], F32, kind="ExternalInput").ap()
        x_out = nc.dram_tensor("xo", [D, L], F32, kind="ExternalOutput").ap()
        w2d = declare_w_s2(nc)
        caps = declare_constants(nc)
        sq = Seq()
        P = ""
        own = True
    else:
        nc, sq, caps, P, x_in, x_out, w2d = (fctx["nc"], fctx["sq"], fctx["caps"],
                                             fctx["P"], fctx["x_in"], fctx["x_out"], fctx["w"])
        own = False
    wq_d, wk_d, wv_d = w2d["wq"], w2d["wk"], w2d["wv"]
    NH = 16
    NQC = SEQ // LC
    NKT = SEQ // 128

    with contextlib.ExitStack() as stack:
        cx = Ctx(nc, sq, stack, caps, psum_scratch=False, prefix=P)
        sb, ps = cx.sb, cx.ps
        if own:
            sems = [stack.enter_context(nc.semaphore(f"S{i}")) for i in range(12)]
        eps_col = cx.const_col(1e-5)

        wq = cx.load_w_bf16(wq_d, D, D, "wq_sb")
        wk = cx.load_w_bf16(wk_d, D, D, "wk_sb")
        wv = cx.load_w_bf16(wv_d, D, D, "wv_sb")

        xn = sb("xn", [128, ND * SEQ], BF16)
        xs8 = sb("xs8", [128, ND * LC], F32)
        sqs = sb("sqs", [128, LC], F32)
        rows = sb("rows", [1, 4 * LC], F32)
        mean_r = rows[:, 0:LC]
        msq_r = rows[:, LC:2 * LC]
        A_r = rows[:, 2 * LC:3 * LC]
        B_r = rows[:, 3 * LC:4 * LC]
        vtm = sb("vtm", [128, NKT * D], BF16)
        qh = sb("qh", [64, NQC * LC], BF16)
        kh = sb("kh", [64, NKT * 128], BF16)
        sT = sb("sT", [128, NKT * LC], BF16)
        zrow = sb("zrow", [1, LC], F32)
        avs = sb("avs", [64, LC], F32)
        xtmp = sb("xtmp", [64, LC], F32)

        pa = ps("pa", [128, LC], F32)
        pa2 = ps("pa2", [128, LC], F32)
        pz = ps("pz", [1, LC], F32)
        pav = ps("pav", [64, LC], F32)
        pbc = ps("pbc", [64, LC], F32)
        prow = ps("prow", [1, LC], F32)
        prow2 = ps("prow2", [1, LC], F32)

        OP = sq.op
        DMA = sq.dma

        for b in range(NB):
            base = b * SEQ
            for lc in range(SEQ // LC):
                l0 = base + lc * LC
                for dc in range(ND):
                    DMA("sync", lambda e, dc=dc, l0=l0: e.dma_start(
                        out=xs8[:, dc * LC:(dc + 1) * LC],
                        in_=x_in[dc * 128:(dc + 1) * 128, l0:l0 + LC]))
                    OP("tensor", lambda e, dc=dc: e.matmul(
                        prow[:], cx.ones_col_f32[:], xs8[:, dc * LC:(dc + 1) * LC],
                        start=(dc == 0), stop=(dc == ND - 1)))
                    OP("scalar", lambda e, dc=dc: e.activation(
                        sqs[:], xs8[:, dc * LC:(dc + 1) * LC], AF.Square))
                    OP("tensor", lambda e, dc=dc: e.matmul(
                        prow2[:], cx.ones_col_f32[:], sqs[:], start=(dc == 0), stop=(dc == ND - 1)))
                OP("vector", lambda e: e.tensor_scalar(mean_r, prow[:], 1.0 / D, None, ALU.mult))
                OP("vector", lambda e: e.tensor_scalar(msq_r, prow2[:], 1.0 / D, None, ALU.mult))
                OP("vector", lambda e: e.tensor_tensor(A_r, mean_r, mean_r, ALU.mult))
                OP("vector", lambda e: e.tensor_tensor(A_r, msq_r, A_r, ALU.subtract))
                OP("scalar", lambda e: e.activation(A_r, A_r, AF.Sqrt, bias=eps_col[0:1, :]))
                OP("vector", lambda e: e.reciprocal(A_r, A_r))
                OP("vector", lambda e: e.tensor_tensor(B_r, mean_r, A_r, ALU.mult))
                OP("vector", lambda e: e.tensor_scalar(B_r, B_r, -1.0, None, ALU.mult))
                cx.bcast_row(pa[:], A_r, LC)
                cx.bcast_row(pa2[:], B_r, LC)
                for dc in range(ND):
                    OP("vector", lambda e, dc=dc: e.tensor_tensor(
                        sqs[:], xs8[:, dc * LC:(dc + 1) * LC], pa[:], ALU.mult))
                    OP("vector", lambda e, dc=dc, lc=lc: e.tensor_tensor(
                        xn[:, dc * SEQ + lc * LC: dc * SEQ + (lc + 1) * LC],
                        sqs[:], pa2[:], ALU.add))
            for lt in range(NKT):
                for half in range(2):
                    for dc in range(ND):
                        OP("tensor", lambda e, lt=lt, half=half, dc=dc: e.matmul(
                            pa[:], xn[:, dc * SEQ + lt * 128: dc * SEQ + (lt + 1) * 128],
                            wv[:, dc * D + half * 512:(dc * D) + (half + 1) * 512],
                            start=(dc == 0), stop=(dc == ND - 1)))
                    OP("vector", lambda e, lt=lt, half=half: e.tensor_copy(
                        vtm[:, lt * D + half * 512: lt * D + (half + 1) * 512], pa[:]))
            for h in range(NH):
                for (dst, wmat) in ((qh, wq), (kh, wk)):
                    for qc in range(NQC):
                        for dc in range(ND):
                            OP("tensor", lambda e, dst=dst, wmat=wmat, qc=qc, dc=dc, h=h: e.matmul(
                                pav[:], wmat[:, dc * D + h * 64: dc * D + h * 64 + 64],
                                xn[:, dc * SEQ + qc * LC: dc * SEQ + (qc + 1) * LC],
                                start=(dc == 0), stop=(dc == ND - 1)))
                        OP("vector", lambda e, dst=dst, qc=qc: e.tensor_copy(
                            dst[:, qc * LC:(qc + 1) * LC], pav[:]))
                for qc in range(NQC):
                    for kt in range(NKT):
                        OP("tensor", lambda e, kt=kt, qc=qc: e.matmul(
                            pa[:], kh[:, kt * 128:(kt + 1) * 128],
                            qh[:, qc * LC:(qc + 1) * LC], start=True, stop=True))
                        OP("scalar", lambda e, kt=kt: e.activation(
                            sT[:, kt * LC:(kt + 1) * LC], pa[:], AF.Exp, scale=0.125))
                        OP("tensor", lambda e, kt=kt: e.matmul(
                            pz[:], cx.ones_col_bf16[:], sT[:, kt * LC:(kt + 1) * LC],
                            start=(kt == 0), stop=(kt == NKT - 1)))
                    for kt in range(NKT):
                        OP("tensor", lambda e, kt=kt, h=h: e.matmul(
                            pav[:], vtm[:, kt * D + h * 64: kt * D + h * 64 + 64],
                            sT[:, kt * LC:(kt + 1) * LC],
                            start=(kt == 0), stop=(kt == NKT - 1)))
                    OP("vector", lambda e: e.reciprocal(zrow[:], pz[:]))
                    cx.bcast_row(pbc[:], zrow[:], LC, dt="f32", m=64)
                    OP("scalar", lambda e: e.activation(avs[:], pav[:], AF.Copy))
                    OP("vector", lambda e: e.tensor_tensor(avs[:], avs[:], pbc[:], ALU.mult))
                    dc = h // 2
                    p0 = (h % 2) * 64
                    l0 = base + qc * LC
                    DMA("sync", lambda e, dc=dc, p0=p0, l0=l0: e.dma_start(
                        out=xtmp[:], in_=x_in[dc * 128 + p0: dc * 128 + p0 + 64, l0:l0 + LC]))
                    OP("vector", lambda e: e.tensor_tensor(xtmp[:], xtmp[:], avs[:], ALU.add))
                    DMA("sync", lambda e, dc=dc, p0=p0, l0=l0: e.dma_start(
                        out=x_out[dc * 128 + p0: dc * 128 + p0 + 64, l0:l0 + LC], in_=xtmp[:]))

        if own:
            sq.emit(nc, sems)
    return nc


def _reorder_W_qkv(W):
    H, Fqk, Fv = 16, 64, 64
    Wr = W.reshape(1024, H, 192)
    Wq = np.ascontiguousarray(Wr[:, :, 0:64].reshape(1024, 1024))
    Wk = np.ascontiguousarray(Wr[:, :, 64:128].reshape(1024, 1024))
    Wv = np.ascontiguousarray(Wr[:, :, 128:192].reshape(1024, 1024))
    return Wq, Wk, Wv


def run_s2(x_fm_shards, weights):
    nc = build_s2()
    hc = host_constants()
    Wq, Wk, Wv = _reorder_W_qkv(weights["L0_attn_W"])
    in_maps = []
    for c in range(N_CORES):
        m = {"x": x_fm_shards[c], "wq": Wq, "wk": Wk, "wv": Wv}
        m.update(hc)
        in_maps.append(m)
    res = run_bass_kernel_spmd(nc, in_maps, core_ids=list(range(N_CORES)))
    return [r["xo"] for r in res.results]


CAP = 10240


def declare_w_s4(nc):
    w = {}
    w["wrT"] = nc.dram_tensor("wrT", [D, 8], F32, kind="ExternalInput").ap()
    w["w1"] = nc.dram_tensor("w1", [8 * D, 4096], F32, kind="ExternalInput").ap()
    w["w2"] = nc.dram_tensor("w2", [8 * 4096, D], F32, kind="ExternalInput").ap()
    w["b1"] = nc.dram_tensor("b1", [8 * 4096], F32, kind="ExternalInput").ap()
    w["b2"] = nc.dram_tensor("b2", [8 * D], F32, kind="ExternalInput").ap()
    w["cmask"] = nc.dram_tensor("cmask", [8, 1], F32, kind="ExternalInput").ap()
    w["w1b_i"] = nc.dram_tensor("w1b_i", [8 * D, 4096], BF16, kind="Internal").ap()
    w["w2b_i"] = nc.dram_tensor("w2b_i", [8 * 4096, D], BF16, kind="Internal").ap()
    return w


def build_s4(debug=False, fctx=None, skip_conv=False):
    import contextlib
    if fctx is None:
        nc = bass.Bass("TRN2", target_bir_lowering=False)
        x_in = nc.dram_tensor("x", [D, L], F32, kind="ExternalInput").ap()
        x_out = nc.dram_tensor("xo", [D, L], F32, kind="ExternalOutput").ap()
        w4d = declare_w_s4(nc)
        caps = declare_constants(nc)
        sq = Seq()
        P = ""
        own = True
    else:
        nc, sq, caps, P, x_in, x_out, w4d = (fctx["nc"], fctx["sq"], fctx["caps"],
                                             fctx["P"], fctx["x_in"], fctx["x_out"], fctx["w"])
        own = False
    wrT_d, w1_d, w2_d, b1_d, b2_d, cmask_d = (w4d["wrT"], w4d["w1"], w4d["w2"],
                                              w4d["b1"], w4d["b2"], w4d["cmask"])
    w1b_d, w2b_d = w4d["w1b_i"], w4d["w2b_i"]
    xnd = nc.dram_tensor(P + "xn_i", [D, L], BF16, kind="Internal").ap()
    cc_in = nc.dram_tensor(P + "cc_in", [1, 8], F32, kind="Internal").ap()
    cc_out = nc.dram_tensor(P + "cc_out", [8, 8], F32, kind="Internal").ap()
    dbg = nc.dram_tensor("dbg", [16, L], F32, kind="ExternalOutput").ap() if debug else None

    NT = L // 128

    with contextlib.ExitStack() as stack:
        cx = Ctx(nc, sq, stack, caps, psum_scratch=True, prefix=P)
        sb, ps = cx.sb, cx.ps
        if own:
            sems = [stack.enter_context(nc.semaphore(f"S{i}")) for i in range(12)]
            cc_sems = [stack.enter_context(nc.semaphore("SCC0"))]
        eps_col = cx.const_col(1e-5)
        OP = sq.op
        DMA = sq.dma

        ut128 = sb("ut128_sb", [128, 128], F32)
        DMA("sync", lambda e: e.dma_start(out=ut128[:], in_=caps["ut128"]))
        cmask = sb("cmask_sb", [8, 1], F32)
        DMA("sync", lambda e: e.dma_start(out=cmask[:], in_=cmask_d))
        wrT32 = sb("wrT32", [128, 8 * 8], F32)
        for dc in range(8):
            DMA("sync", lambda e, dc=dc: e.dma_start(
                out=wrT32[:, dc * 8:(dc + 1) * 8], in_=wrT_d[dc * 128:(dc + 1) * 128, :]))

        stgb = sb("stgb", [128, 4096], BF16)
        wr8 = sb("wr8", [8, NT * 128], F32)
        off_row = sb("off_row", [1, 8], F32)
        coff_row = sb("coff_row", [1, 8], F32)
        ccsb = sb("ccsb", [8, 8], F32)
        wrow_lc = sb("wrow_lc", [1, LC], F32)
        wbc = sb("wbc", [128, LC], F32)
        tmpy = sb("tmpy", [128, LC], F32)
        xtmp = sb("xtmp", [128, LC], F32)

        pa = ps("pa", [128, LC], F32)
        pa2 = ps("pa2", [128, LC], F32)
        pwb = ps("pwb", [128, LC], F32)
        plog = ps("plog", [8, LC], F32)
        ppos = ps("ppos", [128, 8], F32)
        prow8 = ps("prow8", [1, 8], F32)
        prow = ps("prow", [1, LC], F32)

        OP("vector", lambda e: e.memset(off_row[:], 0.0))

        with contextlib.ExitStack() as stA:
            sa = lambda name, shape, dt: stA.enter_context(nc.sbuf_tensor(P + name, shape, dt))
            stage = sa("stage", [128, 4096], F32)
            sqs = sa("sqs", [128, LC], F32)
            rows = sa("rows", [1, 4 * LC], F32)
            mean_r = rows[:, 0:LC]
            msq_r = rows[:, LC:2 * LC]
            A_r = rows[:, 2 * LC:3 * LC]
            B_r = rows[:, 3 * LC:4 * LC]
            lgfm = sa("lgfm", [8, L], F32)
            lgtm = sa("lgtm", [128, NT * 8], F32)
            wtsb = sa("wtsb", [128, NT * 8], F32)
            maskb = sa("maskb", [128, NT * 8], F32)
            poss = sa("poss", [128, NT * 8], F32)
            csA = sa("csA", [128, 8], F32)
            csB = sa("csB", [128, 8], F32)
            e1b = sa("e1b", [128, 8], F32)
            e2b = sa("e2b", [128, 8], F32)
            lg2b = sa("lg2b", [128, 8], F32)
            is1b = sa("is1b", [128, 8], F32)
            is2b = sa("is2b", [128, 8], F32)
            t8 = sa("t8", [128, 8], F32)
            colsx = sa("colsx", [128, 4], F32)

            for e_ in (range(8) if not skip_conv else range(0)):
                for dc in range(8):
                    r0 = e_ * D + dc * 128
                    DMA("sync", lambda e, r0=r0: e.dma_start(out=stage[:], in_=w1_d[r0:r0 + 128, :]))
                    OP("vector", lambda e: e.tensor_copy(stgb[:], stage[:]))
                    DMA("sync", lambda e, r0=r0: e.dma_start(out=w1b_d[r0:r0 + 128, :], in_=stgb[:]))
                for fc in range(32):
                    r0 = e_ * 4096 + fc * 128
                    DMA("sync", lambda e, r0=r0: e.dma_start(out=stage[:, 0:1024], in_=w2_d[r0:r0 + 128, :]))
                    OP("vector", lambda e: e.tensor_copy(stgb[:, 0:1024], stage[:, 0:1024]))
                    DMA("sync", lambda e, r0=r0: e.dma_start(out=w2b_d[r0:r0 + 128, :], in_=stgb[:, 0:1024]))

            for lc in range(NLC):
                l0 = lc * LC
                for dc in range(ND):
                    DMA("sync", lambda e, dc=dc, l0=l0: e.dma_start(
                        out=stage[:, dc * LC:(dc + 1) * LC],
                        in_=x_in[dc * 128:(dc + 1) * 128, l0:l0 + LC]))
                    OP("tensor", lambda e, dc=dc: e.matmul(
                        prow[:], cx.ones_col_f32[:], stage[:, dc * LC:(dc + 1) * LC],
                        start=(dc == 0), stop=(dc == ND - 1)))
                    OP("scalar", lambda e, dc=dc: e.activation(
                        sqs[:], stage[:, dc * LC:(dc + 1) * LC], AF.Square))
                    OP("tensor", lambda e, dc=dc: e.matmul(
                        pa2[0:1, :], cx.ones_col_f32[:], sqs[:], start=(dc == 0), stop=(dc == ND - 1)))
                OP("vector", lambda e: e.tensor_scalar(mean_r, prow[:], 1.0 / D, None, ALU.mult))
                OP("vector", lambda e: e.tensor_scalar(msq_r, pa2[0:1, :], 1.0 / D, None, ALU.mult))
                OP("vector", lambda e: e.tensor_tensor(A_r, mean_r, mean_r, ALU.mult))
                OP("vector", lambda e: e.tensor_tensor(A_r, msq_r, A_r, ALU.subtract))
                OP("scalar", lambda e: e.activation(A_r, A_r, AF.Sqrt, bias=eps_col[0:1, :]))
                OP("vector", lambda e: e.reciprocal(A_r, A_r))
                OP("vector", lambda e: e.tensor_tensor(B_r, mean_r, A_r, ALU.mult))
                OP("vector", lambda e: e.tensor_scalar(B_r, B_r, -1.0, None, ALU.mult))
                cx.bcast_row(pa[:], A_r, LC)
                cx.bcast_row(pa2[:], B_r, LC)
                for dc in range(ND):
                    OP("vector", lambda e, dc=dc: e.tensor_tensor(
                        sqs[:], stage[:, dc * LC:(dc + 1) * LC], pa[:], ALU.mult))
                    OP("vector", lambda e, dc=dc: e.tensor_tensor(
                        tmpy[:], sqs[:], pa2[:], ALU.add))
                    OP("vector", lambda e, dc=dc: e.tensor_copy(
                        stgb[:, dc * LC:(dc + 1) * LC], tmpy[:]))
                    DMA("sync", lambda e, dc=dc, l0=l0: e.dma_start(
                        out=xnd[dc * 128:(dc + 1) * 128, l0:l0 + LC],
                        in_=stgb[:, dc * LC:(dc + 1) * LC]))
                    OP("tensor", lambda e, dc=dc: e.matmul(
                        plog[:], wrT32[:, dc * 8:(dc + 1) * 8], tmpy[:],
                        start=(dc == 0), stop=(dc == ND - 1)))
                OP("vector", lambda e, l0=l0: e.tensor_copy(lgfm[:, l0:l0 + LC], plog[:]))

            def prefix8(src, dst1, dst2):
                OP("vector", lambda e, src=src: e.tensor_copy(dst1[:, 0:1], src[:, 0:1]))
                OP("vector", lambda e, src=src: e.tensor_tensor(dst1[:, 1:8], src[:, 1:8], src[:, 0:7], ALU.add))
                OP("vector", lambda e: e.tensor_copy(dst2[:, 0:2], dst1[:, 0:2]))
                OP("vector", lambda e: e.tensor_tensor(dst2[:, 2:8], dst1[:, 2:8], dst1[:, 0:6], ALU.add))
                OP("vector", lambda e: e.tensor_copy(dst1[:, 0:4], dst2[:, 0:4]))
                OP("vector", lambda e: e.tensor_tensor(dst1[:, 4:8], dst2[:, 4:8], dst2[:, 0:4], ALU.add))
                return dst1

            m1 = colsx[:, 0:1]
            m2 = colsx[:, 1:2]
            g1 = colsx[:, 2:3]
            g2 = colsx[:, 3:4]
            for t in range(NT):
                lg = lgtm[:, t * 8:(t + 1) * 8]
                OP("tensor", lambda e, t=t: e.transpose(
                    cx.pt[:, 0:8], lgfm[:, t * 128:(t + 1) * 128], cx.ident_f32[0:8, 0:8]))
                OP("vector", lambda e, lg=lg: e.tensor_copy(lg, cx.pt[:, 0:8]))
                OP("vector", lambda e, lg=lg: e.tensor_reduce(m1, lg, AX, ALU.max))
                OP("vector", lambda e, lg=lg: e.tensor_scalar(e1b[:], lg, m1, None, ALU.is_ge))
                cs = prefix8(e1b, csA, csB)
                OP("vector", lambda e, cs=cs: e.tensor_scalar(t8[:], cs[:], 1.0, None, ALU.is_equal))
                OP("vector", lambda e: e.tensor_tensor(is1b[:], t8[:], e1b[:], ALU.mult))
                OP("vector", lambda e, lg=lg: e.scalar_tensor_tensor(
                    lg2b[:], is1b[:], -1e30, lg, ALU.mult, ALU.add))
                OP("vector", lambda e: e.tensor_reduce(m2, lg2b[:], AX, ALU.max))
                OP("vector", lambda e: e.tensor_scalar(e2b[:], lg2b[:], m2, None, ALU.is_ge))
                cs = prefix8(e2b, csA, csB)
                OP("vector", lambda e, cs=cs: e.tensor_scalar(t8[:], cs[:], 1.0, None, ALU.is_equal))
                OP("vector", lambda e: e.tensor_tensor(is2b[:], t8[:], e2b[:], ALU.mult))
                OP("vector", lambda e: e.tensor_tensor(g1, m1, m2, ALU.subtract))
                OP("scalar", lambda e: e.activation(g1, g1, AF.Sigmoid))
                OP("vector", lambda e: e.tensor_scalar(g2, g1, -1.0, 1.0, ALU.mult, ALU.add))
                OP("vector", lambda e: e.tensor_scalar(t8[:], is1b[:], g1, None, ALU.mult))
                OP("vector", lambda e, t=t: e.scalar_tensor_tensor(
                    wtsb[:, t * 8:(t + 1) * 8], is2b[:], g2, t8[:], ALU.mult, ALU.add))
                OP("vector", lambda e, t=t: e.tensor_tensor(
                    maskb[:, t * 8:(t + 1) * 8], is1b[:], is2b[:], ALU.add))
                OP("tensor", lambda e, t=t: e.matmul(
                    ppos[:], ut128[:], maskb[:, t * 8:(t + 1) * 8], start=True, stop=False))
                OP("tensor", lambda e: e.matmul(
                    ppos[:], cx.ones_row_f32[:], off_row[:], start=False, stop=True))
                OP("vector", lambda e, t=t: e.tensor_copy(poss[:, t * 8:(t + 1) * 8], ppos[:]))
                OP("tensor", lambda e, t=t: e.matmul(
                    prow8[:], cx.ones_col_f32[:], maskb[:, t * 8:(t + 1) * 8], start=True, stop=True))
                OP("vector", lambda e: e.tensor_tensor(off_row[:], off_row[:], prow8[:], ALU.add))

            DMA("sync", lambda e: e.dma_start(out=cc_in, in_=off_row[:]))
            sq.cc(lambda e: e.collective_compute(
                "AllGather", ALU.bypass, replica_groups=[list(range(8))],
                ins=[cc_in.opt()], outs=[cc_out.opt()]))
            DMA("sync", lambda e: e.dma_start(out=ccsb[:], in_=cc_out))
            OP("tensor", lambda e: e.matmul(prow8[:], cmask[:], ccsb[:], start=True, stop=True))
            OP("vector", lambda e: e.tensor_copy(coff_row[:], prow8[:]))

            for t in range(NT):
                cx.bcast_row(ppos[:], coff_row[:], 8)
                OP("vector", lambda e, t=t: e.tensor_tensor(
                    t8[:], poss[:, t * 8:(t + 1) * 8], ppos[:], ALU.add))
                OP("vector", lambda e: e.tensor_scalar(t8[:], t8[:], CAP + 0.5, None, ALU.is_le))
                OP("vector", lambda e, t=t: e.tensor_tensor(
                    wtsb[:, t * 8:(t + 1) * 8], wtsb[:, t * 8:(t + 1) * 8], t8[:], ALU.mult))
                OP("vector", lambda e, t=t: e.tensor_scalar(
                    wtsb[:, t * 8:(t + 1) * 8], wtsb[:, t * 8:(t + 1) * 8], 0.5, None, ALU.mult))
                OP("tensor", lambda e, t=t: e.transpose(
                    cx.pt[0:8, 0:128], wtsb[:, t * 8:(t + 1) * 8], cx.ident_f32[:]))
                OP("vector", lambda e, t=t: e.tensor_copy(wr8[:, t * 128:(t + 1) * 128], cx.pt[0:8, 0:128]))

            if debug:
                DMA("sync", lambda e: e.dma_start(out=dbg[0:8, :], in_=lgfm[:]))
                DMA("sync", lambda e: e.dma_start(out=dbg[8:16, :], in_=wr8[:]))

        w1b = sb("w1b", [128, 8 * 4096], BF16)
        w2b = sb("w2b", [128, 32 * D], BF16)
        h1b = sb("h1b", [128, 32 * LC], BF16)
        for e_ in range(8):
            for dc in range(8):
                DMA("sync", lambda e, e_=e_, dc=dc: e.dma_start(
                    out=w1b[:, dc * 4096:(dc + 1) * 4096],
                    in_=w1b_d[e_ * D + dc * 128: e_ * D + (dc + 1) * 128, :]))
            for fc in range(32):
                DMA("sync", lambda e, e_=e_, fc=fc: e.dma_start(
                    out=w2b[:, fc * D:(fc + 1) * D],
                    in_=w2b_d[e_ * 4096 + fc * 128: e_ * 4096 + (fc + 1) * 128, :]))
            b1c = cx.load_vec_cols(b1_d[e_ * 4096:(e_ + 1) * 4096], 4096, f"b1c_{e_}")
            b2c = cx.load_vec_cols(b2_d[e_ * D:(e_ + 1) * D], D, f"b2c_{e_}")
            for lc in range(NLC):
                l0 = lc * LC
                DMA("sync", lambda e, e_=e_, l0=l0: e.dma_start(
                    out=wrow_lc[:], in_=wr8[e_:e_ + 1, l0:l0 + LC]))
                for dc in range(ND):
                    DMA("sync", lambda e, dc=dc, l0=l0: e.dma_start(
                        out=stgb[:, dc * LC:(dc + 1) * LC],
                        in_=xnd[dc * 128:(dc + 1) * 128, l0:l0 + LC]))
                for ft in range(32):
                    for dc in range(ND):
                        OP("tensor", lambda e, ft=ft, dc=dc: e.matmul(
                            pa[:], w1b[:, dc * 4096 + ft * 128: dc * 4096 + ft * 128 + 128],
                            stgb[:, dc * LC:(dc + 1) * LC],
                            start=(dc == 0), stop=(dc == ND - 1)))
                    OP("scalar", lambda e, ft=ft, b1c=b1c: e.activation(
                        h1b[:, ft * LC:(ft + 1) * LC], pa[:], AF.Gelu_apprx_tanh,
                        bias=b1c[:, ft:ft + 1]))
                cx.bcast_row(pwb[:], wrow_lc[:], LC)
                OP("vector", lambda e: e.tensor_copy(wbc[:], pwb[:]))
                for dc2 in range(ND):
                    for ft in range(32):
                        OP("tensor", lambda e, dc2=dc2, ft=ft: e.matmul(
                            pa2[:], w2b[:, ft * D + dc2 * 128: ft * D + dc2 * 128 + 128],
                            h1b[:, ft * LC:(ft + 1) * LC],
                            start=(ft == 0), stop=(ft == 31)))
                    OP("vector", lambda e, dc2=dc2, b2c=b2c: e.scalar_tensor_tensor(
                        tmpy[:], pa2[:], b2c[:, dc2:dc2 + 1], wbc[:], ALU.add, ALU.mult))
                    src_x = x_in if e_ == 0 else x_out
                    DMA("sync", lambda e, dc2=dc2, l0=l0, src_x=src_x: e.dma_start(
                        out=xtmp[:], in_=src_x[dc2 * 128:(dc2 + 1) * 128, l0:l0 + LC]))
                    OP("vector", lambda e: e.tensor_tensor(xtmp[:], xtmp[:], tmpy[:], ALU.add))
                    DMA("sync", lambda e, dc2=dc2, l0=l0: e.dma_start(
                        out=x_out[dc2 * 128:(dc2 + 1) * 128, l0:l0 + LC], in_=xtmp[:]))

        if own:
            sq.emit(nc, sems, cc_sems)
    return nc


def run_s4(x_fm_shards, weights, prefix="L1_ffn"):
    nc = build_s4()
    hc = host_constants()
    wrT = np.ascontiguousarray(weights[prefix + "_wr"].T)
    w1 = np.ascontiguousarray(weights[prefix + "_w1"].reshape(8 * 1024, 4096))
    w2 = np.ascontiguousarray(weights[prefix + "_w2"].reshape(8 * 4096, 1024))
    b1 = np.ascontiguousarray(weights[prefix + "_b1"].reshape(-1))
    b2 = np.ascontiguousarray(weights[prefix + "_b2"].reshape(-1))
    in_maps = []
    for c in range(N_CORES):
        cm = np.zeros((8, 1), np.float32)
        cm[:c] = 1.0
        m = {"x": x_fm_shards[c], "wrT": wrT, "w1": w1, "w2": w2, "b1": b1,
             "b2": b2, "cmask": cm}
        m.update(hc)
        in_maps.append(m)
    res = run_bass_kernel_spmd(nc, in_maps, core_ids=list(range(N_CORES)))
    return [r["xo"] for r in res.results]


CH = 256
NCH = SEQ // CH
EPS_OUT = 1.1920929e-07
EPS_GATE = 1e-05


def declare_w_s5(nc):
    w = {}
    w["w_in"] = nc.dram_tensor("w_in", [D, 4384], F32, kind="ExternalInput").ap()
    w["conv_w"] = nc.dram_tensor("conv_w", [2304, 4], F32, kind="ExternalInput").ap()
    w["conv_b"] = nc.dram_tensor("conv_b", [2304], F32, kind="ExternalInput").ap()
    w["dt_bias"] = nc.dram_tensor("dt_bias", [32], F32, kind="ExternalInput").ap()
    w["a_log"] = nc.dram_tensor("a_log", [32], F32, kind="ExternalInput").ap()
    w["d_skip"] = nc.dram_tensor("d_skip", [32], F32, kind="ExternalInput").ap()
    w["norm_w"] = nc.dram_tensor("norm_w", [2048], F32, kind="ExternalInput").ap()
    w["w_out"] = nc.dram_tensor("w_out", [2048, D], F32, kind="ExternalInput").ap()
    w["w_fc"] = nc.dram_tensor("w_fc", [D, 4096], F32, kind="ExternalInput").ap()
    w["w_proj"] = nc.dram_tensor("w_proj", [4096, D], F32, kind="ExternalInput").ap()
    w["ltm"] = nc.dram_tensor("ltm", [128, 512], F32, kind="ExternalInput").ap()
    return w


def build_s5(fctx=None):
    import contextlib
    if fctx is None:
        nc = bass.Bass("TRN2", target_bir_lowering=False)
        x_in = nc.dram_tensor("x", [D, L], F32, kind="ExternalInput").ap()
        x_out = nc.dram_tensor("xo", [D, L], F32, kind="ExternalOutput").ap()
        w5d = declare_w_s5(nc)
        caps = declare_constants(nc)
        sq = Seq()
        P = ""
        own = True
    else:
        nc, sq, caps, P, x_in, x_out, w5d = (fctx["nc"], fctx["sq"], fctx["caps"],
                                             fctx["P"], fctx["x_in"], fctx["x_out"], fctx["w"])
        own = False
    w_in_d, convw_d, convb_d = w5d["w_in"], w5d["conv_w"], w5d["conv_b"]
    dtb_d, alog_d, dskip_d = w5d["dt_bias"], w5d["a_log"], w5d["d_skip"]
    normw_d, w_out_d, w_fc_d, w_proj_d, ltm_d = (w5d["norm_w"], w5d["w_out"],
                                                 w5d["w_fc"], w5d["w_proj"], w5d["ltm"])
    xi_d = nc.dram_tensor(P + "xi_i", [D, L], BF16, kind="Internal").ap()
    xnm_d = nc.dram_tensor(P + "xnm_i", [D, L], BF16, kind="Internal").ap()
    h_d = nc.dram_tensor(P + "h_i", [D, L], F32, kind="Internal").ap()

    with contextlib.ExitStack() as stack:
        cx = Ctx(nc, sq, stack, caps, psum_scratch=True, prefix=P)
        sb, ps = cx.sb, cx.ps
        if own:
            sems = [stack.enter_context(nc.semaphore(f"S{i}")) for i in range(12)]
        eps_ln = cx.const_col(1e-5)
        eps_out_c = cx.const_col(EPS_OUT)
        epsg_col = cx.const_col(EPS_GATE)[:]
        OP = sq.op
        DMA = sq.dma

        ptb = ps("ptb", [128, 128], BF16)
        pa = ps("pa", [128, LC], F32)
        pb2 = ps("pb2", [128, LC], F32)
        pcb = ps("pcb", [128, LC], F32)
        py = ps("py", [128, 64], F32)
        pdt = ps("pdt", [32, CH], F32)
        prow = ps("prow", [1, LC], F32)

        def load32col(dram_ap, name):
            row = sb(name + "_row", [1, 32], F32)
            col = sb(name + "_col", [32, 1], F32)
            DMA("sync", lambda e: e.dma_start(out=row[:], in_=dram_ap.rearrange("(a x) -> a x", a=1)))
            OP("tensor", lambda e: e.transpose(cx.pt[0:32, 0:1], row[:], cx.ident_f32[0:1, 0:1]))
            OP("vector", lambda e: e.tensor_copy(col[:], cx.pt[0:32, 0:1]))
            return row, col

        dtb_row, dtb_col = load32col(dtb_d, "dtb")
        alog_row, alog_col = load32col(alog_d, "alog")
        dskip_row, dskip_col = load32col(dskip_d, "dskip")
        negea = sb("negea", [32, 1], F32)
        OP("scalar", lambda e: e.activation(negea[:], alog_col[:], AF.Exp))
        OP("vector", lambda e: e.tensor_scalar(negea[:], negea[:], -1.0, None, ALU.mult))
        dskip_bc = sb("dskip_bc", [128, 32], F32)
        cx.bcast_row(cx.pt[:, 0:32], dskip_row[:], 32)
        OP("vector", lambda e: e.tensor_copy(dskip_bc[:], cx.pt[:, 0:32]))
        convw = sb("convw", [128, 18 * 4], F32)
        for ch in range(18):
            DMA("sync", lambda e, ch=ch: e.dma_start(
                out=convw[:, ch * 4:(ch + 1) * 4], in_=convw_d[ch * 128:(ch + 1) * 128, :]))
        convb = cx.load_vec_cols(convb_d, 2304, "convb")
        ltm = sb("ltm_sb", [128, 512], F32)
        DMA("sync", lambda e: e.dma_start(out=ltm[:], in_=ltm_d))

        with contextlib.ExitStack() as stA:
            sa = lambda name, shape, dt: stA.enter_context(nc.sbuf_tensor(P + name, shape, dt))
            wstage = sa("wstage", [128, 4384], F32)
            w_inb = cx.load_w_bf16_in(stA, w_in_d, D, 4384, "w_inb", stage=wstage)
            w_outb = cx.load_w_bf16_in(stA, w_out_d, 2048, D, "w_outb", stage=wstage)
            state_sb = sa("state_sb", [128, 32 * 64], F32)
            state_bf = sa("state_bf", [128, 32 * 64], BF16)
            y_tm = sa("y_tm", [128, 2 * 2048], BF16)
            gaten = sa("gaten", [128, 2048], BF16)
            g2buf = wstage[:, 0:2048]
            mcol = sa("mcol", [128, 4], F32)
            xs8 = wstage[:, 0:ND * LC]
            arena = sa("arena32", [128, 2048], F32)
            sqs = sa("sqs", [128, LC], F32)
            mean_r = arena[0:1, 0:LC]
            msq_r = arena[0:1, LC:2 * LC]
            A_r = arena[0:1, 2 * LC:3 * LC]
            B_r = arena[0:1, 3 * LC:4 * LC]
            xnmc = sa("xnmc", [128, ND * CH], BF16)
            z_tm = sa("z_tm", [128, 2 * 2048], BF16)
            xbcf = sa("xbcf", [128, 18 * 260], BF16)
            xh_fm = sa("xh_fm", [128, 16 * CH], BF16)
            B_fm = sa("B_fm", [128, CH], BF16)
            C_fm = sa("C_fm", [128, CH], BF16)
            B_tm = sa("B_tm", [128, 2 * 128], BF16)
            xh_tm = sa("xh_tm", [128, 2 * 2048], BF16)
            upre_sm = sa("upre_sm", [128, 2 * 64], BF16)
            dtr_f = arena[0:32, 0:CH]
            dt_fm = arena[0:32, CH:2 * CH]
            acum = arena[0:32, 2 * CH:3 * CH]
            DSf = arena[0:32, 3 * CH:4 * CH]
            PREf = arena[0:32, 4 * CH:5 * CH]
            EAf = arena[0:32, 5 * CH:6 * CH]
            scr32 = arena[0:32, 6 * CH:7 * CH]
            endc = sa("endc", [32, 2], F32)
            tpose = sa("tpose", [128, 6 * 32], F32)
            acumT = lambda lt: tpose[:, lt * 32:(lt + 1) * 32]
            dtT = lambda lt: tpose[:, 64 + lt * 32: 64 + (lt + 1) * 32]
            PRET = lambda lt: tpose[:, 128 + lt * 32: 128 + (lt + 1) * 32]
            dec_bc = sa("dec_bc", [128, 32], F32)
            arow = sa("arow", [1, CH], F32)
            PTs = sa("PTs", [128, 2 * CH], BF16)
            Csh = sa("Csh", [128, CH], BF16)
            t256a = wstage[:, 2048:2048 + CH]
            t256b = wstage[:, 2048 + CH:2048 + 2 * CH]
            xtc = sa("xtc", [128, CH], BF16)
            xib = sa("xib", [128, LC], BF16)
            normw_row = sa("normw_row", [1, 512], F32)
            normw_bc = sa("normw_bc", [128, 2048], BF16)
            for j in range(4):
                DMA("sync", lambda e, j=j: e.dma_start(
                    out=normw_row[:],
                    in_=normw_d.rearrange("(a x) -> a x", a=1)[0:1, j * 512:(j + 1) * 512]))
                cx.bcast_row(pa[:], normw_row[:], 512)
                OP("vector", lambda e, j=j: e.tensor_copy(normw_bc[:, j * 512:(j + 1) * 512], pa[:]))
            htc = sa("htc", [128, CH], F32)

            for b in range(NB):
                for lc in range(SEQ // LC):
                    l0 = b * SEQ + lc * LC
                    for dc in range(ND):
                        DMA("sync", lambda e, dc=dc, l0=l0: e.dma_start(
                            out=xs8[:, dc * LC:(dc + 1) * LC],
                            in_=x_in[dc * 128:(dc + 1) * 128, l0:l0 + LC]))
                        OP("tensor", lambda e, dc=dc: e.matmul(
                            prow[:], cx.ones_col_f32[:], xs8[:, dc * LC:(dc + 1) * LC],
                            start=(dc == 0), stop=(dc == ND - 1)))
                        OP("scalar", lambda e, dc=dc: e.activation(
                            sqs[:], xs8[:, dc * LC:(dc + 1) * LC], AF.Square))
                        OP("tensor", lambda e, dc=dc: e.matmul(
                            pb2[0:1, :], cx.ones_col_f32[:], sqs[:], start=(dc == 0), stop=(dc == ND - 1)))
                    OP("vector", lambda e: e.tensor_scalar(mean_r, prow[:], 1.0 / D, None, ALU.mult))
                    OP("vector", lambda e: e.tensor_scalar(msq_r, pb2[0:1, :], 1.0 / D, None, ALU.mult))
                    OP("vector", lambda e: e.tensor_tensor(A_r, mean_r, mean_r, ALU.mult))
                    OP("vector", lambda e: e.tensor_tensor(A_r, msq_r, A_r, ALU.subtract))
                    OP("scalar", lambda e: e.activation(A_r, A_r, AF.Sqrt, bias=eps_ln[0:1, :]))
                    OP("vector", lambda e: e.reciprocal(A_r, A_r))
                    OP("vector", lambda e: e.tensor_tensor(B_r, mean_r, A_r, ALU.mult))
                    OP("vector", lambda e: e.tensor_scalar(B_r, B_r, -1.0, None, ALU.mult))
                    cx.bcast_row(pa[:], A_r, LC)
                    cx.bcast_row(pb2[:], B_r, LC)
                    for dc in range(ND):
                        OP("vector", lambda e, dc=dc: e.tensor_tensor(
                            sqs[:], xs8[:, dc * LC:(dc + 1) * LC], pa[:], ALU.mult))
                        OP("vector", lambda e, dc=dc: e.tensor_tensor(
                            xs8[:, dc * LC:(dc + 1) * LC], sqs[:], pb2[:], ALU.add))
                        OP("vector", lambda e, dc=dc: e.tensor_copy(
                            xib[:], xs8[:, dc * LC:(dc + 1) * LC]))
                        DMA("sync", lambda e, dc=dc, l0=l0: e.dma_start(
                            out=xi_d[dc * 128:(dc + 1) * 128, l0:l0 + LC],
                            in_=xib[:]))
                        OP("scalar", lambda e, dc=dc: e.activation(
                            sqs[:], xs8[:, dc * LC:(dc + 1) * LC], AF.Square))
                        OP("tensor", lambda e, dc=dc: e.matmul(
                            prow[:], cx.ones_col_f32[:], sqs[:], start=(dc == 0), stop=(dc == ND - 1)))
                    OP("vector", lambda e: e.tensor_scalar(A_r, prow[:], 1.0 / D, None, ALU.mult))
                    OP("scalar", lambda e: e.activation(A_r, A_r, AF.Sqrt, bias=eps_out_c[0:1, :]))
                    OP("vector", lambda e: e.reciprocal(A_r, A_r))
                    cx.bcast_row(pa[:], A_r, LC)
                    for dc in range(ND):
                        OP("vector", lambda e, dc=dc: e.tensor_tensor(
                            sqs[:], xs8[:, dc * LC:(dc + 1) * LC], pa[:], ALU.mult))
                        OP("vector", lambda e: e.tensor_copy(xib[:], sqs[:]))
                        DMA("sync", lambda e, dc=dc, l0=l0: e.dma_start(
                            out=xnm_d[dc * 128:(dc + 1) * 128, l0:l0 + LC],
                            in_=xib[:]))
                OP("vector", lambda e: e.memset(state_sb[:], 0.0))
                OP("vector", lambda e: e.memset(state_bf[:], 0.0))

                for c in range(NCH):
                    l0 = b * SEQ + c * CH
                    for dc in range(ND):
                        DMA("sync", lambda e, dc=dc, l0=l0: e.dma_start(
                            out=xnmc[:, dc * CH:(dc + 1) * CH],
                            in_=xnm_d[dc * 128:(dc + 1) * 128, l0:l0 + CH]))
                    for lt in range(2):
                        for zb in range(4):
                            for dc in range(ND):
                                OP("tensor", lambda e, lt=lt, zb=zb, dc=dc: e.matmul(
                                    pa[:], xnmc[:, dc * CH + lt * 128: dc * CH + lt * 128 + 128],
                                    w_inb[:, dc * 4384 + zb * 512: dc * 4384 + (zb + 1) * 512],
                                    start=(dc == 0), stop=(dc == ND - 1)))
                            OP("scalar", lambda e, lt=lt, zb=zb: e.activation(
                                z_tm[:, lt * 2048 + zb * 512: lt * 2048 + (zb + 1) * 512],
                                pa[:], AF.Silu))
                    if c == 0:
                        for ch in range(18):
                            OP("vector", lambda e, ch=ch: e.memset(xbcf[:, ch * 260:ch * 260 + 3], 0.0))
                    for ch in range(18):
                        for dc in range(ND):
                            OP("tensor", lambda e, ch=ch, dc=dc: e.matmul(
                                pcb[:, 0:CH],
                                w_inb[:, dc * 4384 + 2048 + ch * 128: dc * 4384 + 2048 + (ch + 1) * 128],
                                xnmc[:, dc * CH:(dc + 1) * CH],
                                start=(dc == 0), stop=(dc == ND - 1)))
                        OP("vector", lambda e, ch=ch: e.tensor_copy(
                            xbcf[:, ch * 260 + 3:ch * 260 + 259], pcb[:, 0:CH]))
                    for ch in range(18):
                        base = ch * 260
                        OP("vector", lambda e, base=base, ch=ch: e.tensor_scalar(
                            t256a[:], xbcf[:, base + 3:base + 259], convw[:, ch * 4 + 3:ch * 4 + 4],
                            None, ALU.mult))
                        for k in range(3):
                            src_t = t256a if k % 2 == 0 else t256b
                            dst_t = t256b if k % 2 == 0 else t256a
                            OP("vector", lambda e, base=base, ch=ch, k=k, src_t=src_t, dst_t=dst_t:
                               e.scalar_tensor_tensor(
                                   dst_t[:], xbcf[:, base + k:base + k + 256],
                                   convw[:, ch * 4 + k:ch * 4 + k + 1], src_t[:],
                                   ALU.mult, ALU.add))
                        dst = (xh_fm[:, (ch) * CH:(ch + 1) * CH] if ch < 16 else
                               (B_fm[:] if ch == 16 else C_fm[:]))
                        OP("scalar", lambda e, dst=dst, ch=ch: e.activation(
                            dst, t256b[:], AF.Silu, bias=convb[:, ch:ch + 1]))
                        OP("vector", lambda e, base=base: e.tensor_copy(
                            xbcf[:, base:base + 3], xbcf[:, base + 256:base + 259]))
                    for dc in range(ND):
                        OP("tensor", lambda e, dc=dc: e.matmul(
                            pdt[:], w_inb[:, dc * 4384 + 4352: dc * 4384 + 4384],
                            xnmc[:, dc * CH:(dc + 1) * CH],
                            start=(dc == 0), stop=(dc == ND - 1)))
                    OP("vector", lambda e: e.tensor_scalar(dtr_f, pdt[:], dtb_col[:], None, ALU.add))
                    OP("scalar", lambda e: e.activation(scr32[:], dtr_f, AF.Abs))
                    OP("scalar", lambda e: e.activation(scr32[:], scr32[:], AF.Exp, scale=-1.0))
                    OP("scalar", lambda e: e.activation(scr32[:], scr32[:], AF.Ln, bias=1.0))
                    OP("vector", lambda e: e.tensor_scalar(dt_fm, dtr_f, 0.0, None, ALU.max))
                    OP("vector", lambda e: e.tensor_tensor(dt_fm, dt_fm, scr32[:], ALU.add))
                    OP("vector", lambda e: e.tensor_scalar(acum, dt_fm, negea[:], None, ALU.mult))
                    OP("vector", lambda e: e.tensor_tensor_scan(acum, acum, acum, 0.0, ALU.add, ALU.bypass))
                    OP("vector", lambda e: e.tensor_copy(endc[:, 0:1], acum[:, CH - 1:CH]))
                    OP("vector", lambda e: e.tensor_scalar(DSf, acum, endc[:, 0:1], None, ALU.subtract))
                    OP("scalar", lambda e: e.activation(DSf, DSf, AF.Exp, scale=-1.0))
                    OP("vector", lambda e: e.tensor_tensor(PREf, dt_fm, DSf, ALU.mult))
                    OP("scalar", lambda e: e.activation(EAf, acum, AF.Exp))
                    OP("scalar", lambda e: e.activation(endc[:, 1:2], endc[:, 0:1], AF.Exp))
                    OP("tensor", lambda e: e.transpose(cx.pt[0:1, 0:32], endc[:, 1:2], cx.ident_f32[0:32, 0:32]))
                    OP("vector", lambda e: e.tensor_copy(arow[:, 0:32], cx.pt[0:1, 0:32]))
                    cx.bcast_row(cx.pt[:, 0:32], arow[:, 0:32], 32)
                    OP("vector", lambda e: e.tensor_copy(dec_bc[:], cx.pt[:, 0:32]))
                    for (off, dstf) in ((2 * CH, acumT), (CH, dtT), (4 * CH, PRET)):
                        for lt in range(2):
                            OP("tensor", lambda e, off=off, lt=lt: e.transpose(
                                cx.pt[:, 0:32],
                                arena[0:32, off + lt * 128: off + (lt + 1) * 128],
                                cx.ident_f32[0:32, 0:32]))
                            OP("vector", lambda e, dstf=dstf, lt=lt: e.tensor_copy(
                                dstf(lt), cx.pt[:, 0:32]))
                    for ch in range(16):
                        for lt in range(2):
                            OP("tensor", lambda e, ch=ch, lt=lt: e.transpose(
                                ptb[:], xh_fm[:, ch * CH + lt * 128: ch * CH + lt * 128 + 128],
                                cx.ident_bf16[:]))
                            OP("vector", lambda e, ch=ch, lt=lt: e.tensor_copy(
                                xh_tm[:, lt * 2048 + ch * 128: lt * 2048 + (ch + 1) * 128], ptb[:]))
                    for st in range(2):
                        OP("tensor", lambda e, st=st: e.transpose(
                            ptb[:], B_fm[:, st * 128:(st + 1) * 128], cx.ident_bf16[:]))
                        OP("vector", lambda e, st=st: e.tensor_copy(
                            B_tm[:, st * 128:(st + 1) * 128], ptb[:]))
                    for st in range(2):
                        OP("tensor", lambda e, st=st: e.matmul(
                            pcb[:, st * CH:(st + 1) * CH], B_fm[:, st * 128:(st + 1) * 128],
                            C_fm[:], start=True, stop=True))
                    for h in range(32):
                        DMA("sync", lambda e, h=h: e.dma_start(
                            out=arow[:], in_=arena[h:h + 1, 2 * CH:3 * CH]))
                        for st in range(2):
                            cx.bcast_row(pb2[:, 0:CH], arow[:], CH)
                            OP("vector", lambda e, st=st, h=h: e.tensor_scalar(
                                t256a[:], pb2[:, 0:CH], acumT(st)[:, h:h + 1], None, ALU.subtract))
                            OP("vector", lambda e: e.tensor_scalar(
                                t256a[:], t256a[:], 0.0, None, ALU.min))
                            OP("scalar", lambda e: e.activation(t256a[:], t256a[:], AF.Exp))
                            OP("vector", lambda e, st=st: e.tensor_tensor(
                                t256b[:], ltm[:, st * CH:(st + 1) * CH], pcb[:, st * CH:(st + 1) * CH],
                                ALU.mult))
                            OP("vector", lambda e: e.tensor_tensor(t256b[:], t256b[:], t256a[:], ALU.mult))
                            OP("vector", lambda e, st=st, h=h: e.tensor_scalar(
                                PTs[:, st * CH:(st + 1) * CH], t256b[:], dtT(st)[:, h:h + 1],
                                None, ALU.mult))
                        DMA("sync", lambda e, h=h: e.dma_start(
                            out=arow[:], in_=arena[h:h + 1, 5 * CH:6 * CH]))
                        cx.bcast_row(pb2[:, 0:CH], arow[:], CH)
                        OP("vector", lambda e: e.tensor_tensor(Csh[:], C_fm[:], pb2[:, 0:CH], ALU.mult))
                        for lt in range(2):
                            for st in range(2):
                                OP("tensor", lambda e, lt=lt, st=st, h=h: e.matmul(
                                    py[:], PTs[:, st * CH + lt * 128: st * CH + lt * 128 + 128],
                                    xh_tm[:, st * 2048 + h * 64: st * 2048 + (h + 1) * 64],
                                    start=(st == 0), stop=False))
                            OP("tensor", lambda e, lt=lt, h=h: e.matmul(
                                py[:], Csh[:, lt * 128:(lt + 1) * 128],
                                state_bf[:, h * 64:(h + 1) * 64], start=False, stop=True))
                            OP("vector", lambda e, lt=lt, h=h: e.scalar_tensor_tensor(
                                y_tm[:, lt * 2048 + h * 64: lt * 2048 + (h + 1) * 64],
                                xh_tm[:, lt * 2048 + h * 64: lt * 2048 + (h + 1) * 64],
                                dskip_bc[:, h:h + 1], py[:], ALU.mult, ALU.add))
                        for st in range(2):
                            OP("vector", lambda e, st=st, h=h: e.tensor_scalar(
                                upre_sm[:, st * 64:(st + 1) * 64],
                                xh_tm[:, st * 2048 + h * 64: st * 2048 + (h + 1) * 64],
                                PRET(st)[:, h:h + 1], None, ALU.mult))
                        for st in range(2):
                            OP("tensor", lambda e, st=st, h=h: e.matmul(
                                pb2[:, 0:64], B_tm[:, st * 128:(st + 1) * 128],
                                upre_sm[:, st * 64:(st + 1) * 64],
                                start=(st == 0), stop=(st == 1)))
                        OP("vector", lambda e, h=h: e.scalar_tensor_tensor(
                            state_sb[:, h * 64:(h + 1) * 64], state_sb[:, h * 64:(h + 1) * 64],
                            dec_bc[:, h:h + 1], pb2[:, 0:64], ALU.mult, ALU.add))
                        OP("vector", lambda e, h=h: e.tensor_copy(
                            state_bf[:, h * 64:(h + 1) * 64], state_sb[:, h * 64:(h + 1) * 64]))
                    for lt in range(2):
                        OP("vector", lambda e, lt=lt: e.tensor_tensor(
                            g2buf[:], y_tm[:, lt * 2048:(lt + 1) * 2048],
                            z_tm[:, lt * 2048:(lt + 1) * 2048], ALU.mult))
                        OP("scalar", lambda e: e.activation(
                            gaten[:], g2buf[:], AF.Square, accum_out=mcol[:, 0:1]))
                        OP("vector", lambda e: e.tensor_scalar(mcol[:, 1:2], mcol[:, 0:1], 1.0 / 2048, None, ALU.mult))
                        OP("scalar", lambda e: e.activation(mcol[:, 1:2], mcol[:, 1:2], AF.Sqrt,
                                                            bias=epsg_col))
                        OP("vector", lambda e: e.reciprocal(mcol[:, 1:2], mcol[:, 1:2]))
                        OP("vector", lambda e: e.scalar_tensor_tensor(
                            gaten[:], g2buf[:], mcol[:, 1:2],
                            normw_bc[:], ALU.mult, ALU.mult))
                        for ch in range(16):
                            OP("tensor", lambda e, ch=ch: e.transpose(
                                ptb[:], gaten[:, ch * 128:(ch + 1) * 128],
                                cx.ident_bf16[:]))
                            OP("vector", lambda e, ch=ch, lt=lt: e.tensor_copy(
                                xh_fm[:, ch * CH + lt * 128: ch * CH + lt * 128 + 128], ptb[:]))
                    for dcO in range(ND):
                        for ch in range(16):
                            OP("tensor", lambda e, dcO=dcO, ch=ch: e.matmul(
                                pa[:, 0:CH], w_outb[:, ch * D + dcO * 128: ch * D + (dcO + 1) * 128],
                                xh_fm[:, ch * CH:(ch + 1) * CH],
                                start=(ch == 0), stop=(ch == 15)))
                        DMA("sync", lambda e, dcO=dcO, l0=l0: e.dma_start(
                            out=xtc[:], in_=xi_d[dcO * 128:(dcO + 1) * 128, l0:l0 + CH]))
                        OP("vector", lambda e, dcO=dcO: e.tensor_tensor(htc[:], xtc[:], pa[:, 0:CH], ALU.add))
                        DMA("sync", lambda e, dcO=dcO, l0=l0: e.dma_start(
                            out=h_d[dcO * 128:(dcO + 1) * 128, l0:l0 + CH], in_=htc[:]))

        with contextlib.ExitStack() as stB:
            sbb = lambda name, shape, dt: stB.enter_context(nc.sbuf_tensor(P + name, shape, dt))
            stageB = sbb("stageB", [128, 4096], F32)
            w_fcb = cx.load_w_bf16_in(stB, w_fc_d, D, 4096, "w_fcb", stage=stageB)
            w_prb = cx.load_w_bf16_in(stB, w_proj_d, 4096, D, "w_prb", stage=stageB)
            hs8 = stageB[:, 0:ND * LC]
            hnb = sbb("hnb", [128, ND * LC], BF16)
            r1 = sbb("r1", [128, LC], BF16)
            r2 = sbb("r2", [128, 32 * LC], BF16)
            sq2 = sbb("sq2", [128, LC], F32)
            rowB = sbb("rowB", [1, LC], F32)
            xt5 = sbb("xt5", [128, LC], F32)
            for lc in range(NLC):
                l0 = lc * LC
                for dc in range(ND):
                    DMA("sync", lambda e, dc=dc, l0=l0: e.dma_start(
                        out=hs8[:, dc * LC:(dc + 1) * LC],
                        in_=h_d[dc * 128:(dc + 1) * 128, l0:l0 + LC]))
                    OP("scalar", lambda e, dc=dc: e.activation(
                        sq2[:], hs8[:, dc * LC:(dc + 1) * LC], AF.Square))
                    OP("tensor", lambda e, dc=dc: e.matmul(
                        prow[:], cx.ones_col_f32[:], sq2[:], start=(dc == 0), stop=(dc == ND - 1)))
                OP("vector", lambda e: e.tensor_scalar(rowB[:], prow[:], 1.0 / D, None, ALU.mult))
                OP("scalar", lambda e: e.activation(rowB[:], rowB[:], AF.Sqrt, bias=eps_out_c[0:1, :]))
                OP("vector", lambda e: e.reciprocal(rowB[:], rowB[:]))
                cx.bcast_row(pa[:], rowB[:], LC)
                for dc in range(ND):
                    OP("vector", lambda e, dc=dc: e.tensor_tensor(
                        hnb[:, dc * LC:(dc + 1) * LC], hs8[:, dc * LC:(dc + 1) * LC],
                        pa[:], ALU.mult))
                for ft in range(32):
                    for dc in range(ND):
                        OP("tensor", lambda e, ft=ft, dc=dc: e.matmul(
                            pa[:], w_fcb[:, dc * 4096 + ft * 128: dc * 4096 + ft * 128 + 128],
                            hnb[:, dc * LC:(dc + 1) * LC],
                            start=(dc == 0), stop=(dc == ND - 1)))
                    OP("scalar", lambda e, ft=ft: e.activation(r1[:], pa[:], AF.Relu))
                    OP("scalar", lambda e, ft=ft: e.activation(
                        r2[:, ft * LC:(ft + 1) * LC], r1[:], AF.Square))
                for dc2 in range(ND):
                    for ft in range(32):
                        OP("tensor", lambda e, dc2=dc2, ft=ft: e.matmul(
                            pb2[:], w_prb[:, ft * D + dc2 * 128: ft * D + dc2 * 128 + 128],
                            r2[:, ft * LC:(ft + 1) * LC],
                            start=(ft == 0), stop=(ft == 31)))
                    DMA("sync", lambda e, dc2=dc2, l0=l0: e.dma_start(
                        out=xt5[:], in_=x_in[dc2 * 128:(dc2 + 1) * 128, l0:l0 + LC]))
                    OP("vector", lambda e, dc2=dc2: e.tensor_tensor(
                        xt5[:], xt5[:], hs8[:, dc2 * LC:(dc2 + 1) * LC], ALU.add))
                    OP("vector", lambda e: e.tensor_tensor(xt5[:], xt5[:], pb2[:], ALU.add))
                    DMA("sync", lambda e, dc2=dc2, l0=l0: e.dma_start(
                        out=x_out[dc2 * 128:(dc2 + 1) * 128, l0:l0 + LC], in_=xt5[:]))

        if own:
            sq.emit(nc, sems)
    return nc


def host_ltm():
    ltm = np.zeros((128, 512), np.float32)
    for st in range(2):
        for p in range(128):
            s = st * 128 + p
            ltm[p, st * 256 + s:st * 256 + 256] = 1.0
    return ltm


def run_s5(x_fm_shards, weights):
    nc = build_s5()
    hc = host_constants()
    in_maps = []
    for c in range(N_CORES):
        m = {"x": x_fm_shards[c],
             "w_in": weights["L1_attn_w_in"], "conv_w": weights["L1_attn_conv_w"],
             "conv_b": weights["L1_attn_conv_b"], "dt_bias": weights["L1_attn_dt_bias"],
             "a_log": weights["L1_attn_a_log"], "d_skip": weights["L1_attn_d_skip"],
             "norm_w": weights["L1_attn_norm_w"], "w_out": weights["L1_attn_w_out"],
             "w_fc": weights["L1_attn_w_fc"], "w_proj": weights["L1_attn_w_proj"],
             "ltm": host_ltm()}
        m.update(hc)
        in_maps.append(m)
    res = run_bass_kernel_spmd(nc, in_maps, core_ids=list(range(N_CORES)))
    return [r["xo"] for r in res.results]


def build_fused():
    import contextlib
    nc = bass.Bass("TRN2", target_bir_lowering=False)
    x_in = nc.dram_tensor("x", [D, L], F32, kind="ExternalInput").ap()
    x_out = nc.dram_tensor("xo", [D, L], F32, kind="ExternalOutput").ap()
    xA = nc.dram_tensor("xA_i", [D, L], F32, kind="Internal").ap()
    xB = nc.dram_tensor("xB_i", [D, L], F32, kind="Internal").ap()
    caps = declare_constants(nc)
    w1d = declare_w_s1(nc)
    w2d = declare_w_s2(nc)
    w4d = declare_w_s4(nc)
    w5d = declare_w_s5(nc)
    sq = Seq()
    with contextlib.ExitStack() as stack:
        sems = [stack.enter_context(nc.semaphore(f"S{i}")) for i in range(24)]
        ccs = [stack.enter_context(nc.semaphore(f"SCC{i}")) for i in range(2)]
        mk = lambda P, xi, xo, w: {"nc": nc, "sq": sq, "caps": caps, "P": P,
                                   "x_in": xi, "x_out": xo, "w": w}
        build_s1(fctx=mk("a_", x_in, xA, w1d))
        build_s2(fctx=mk("b_", xA, xB, w2d))
        build_s1(fctx=mk("c_", xB, xA, w1d))
        build_s4(fctx=mk("d_", xA, xB, w4d))
        build_s5(fctx=mk("e_", xB, xA, w5d))
        build_s4(fctx=mk("f_", xA, x_out, w4d), skip_conv=True)
        sq.emit(nc, sems, ccs)
    return nc


def fused_in_maps(x_fm_shards, d):
    hc = host_constants()
    Wq, Wk, Wv = _reorder_W_qkv(d["L0_attn_W"])
    wrT = np.ascontiguousarray(d["L1_ffn_wr"].T)
    w1 = np.ascontiguousarray(d["L1_ffn_w1"].reshape(8 * 1024, 4096))
    w2 = np.ascontiguousarray(d["L1_ffn_w2"].reshape(8 * 4096, 1024))
    b1 = np.ascontiguousarray(d["L1_ffn_b1"].reshape(-1))
    b2 = np.ascontiguousarray(d["L1_ffn_b2"].reshape(-1))
    in_maps = []
    for c in range(N_CORES):
        cm = np.zeros((8, 1), np.float32)
        cm[:c] = 1.0
        m = {"x": x_fm_shards[c],
             "cn_w1": d["L0_ffn_cn_w1"], "cn_w2": d["L0_ffn_cn_w2"],
             "cn_w3": d["L0_ffn_cn_w3"], "cn_w4": d["L0_ffn_cn_w4"],
             "gc_w1": d["L0_ffn_gc_w1"], "gc_w2": d["L0_ffn_gc_w2"],
             "gc_w3": d["L0_ffn_gc_w3"],
             "cn_b1": d["L0_ffn_cn_b1"], "cn_b2": d["L0_ffn_cn_b2"],
             "cn_b3": d["L0_ffn_cn_b3"], "cn_b4": d["L0_ffn_cn_b4"],
             "gc_b1": d["L0_ffn_gc_b1"], "gc_b2": d["L0_ffn_gc_b2"],
             "gc_b3": d["L0_ffn_gc_b3"], "ar": d["L0_ffn_adaptive_range"],
             "wq": Wq, "wk": Wk, "wv": Wv,
             "wrT": wrT, "w1": w1, "w2": w2, "b1": b1, "b2": b2, "cmask": cm,
             "w_in": d["L1_attn_w_in"], "conv_w": d["L1_attn_conv_w"],
             "conv_b": d["L1_attn_conv_b"], "dt_bias": d["L1_attn_dt_bias"],
             "a_log": d["L1_attn_a_log"], "d_skip": d["L1_attn_d_skip"],
             "norm_w": d["L1_attn_norm_w"], "w_out": d["L1_attn_w_out"],
             "w_fc": d["L1_attn_w_fc"], "w_proj": d["L1_attn_w_proj"],
             "ltm": host_ltm()}
        m.update(hc)
        in_maps.append(m)
    return in_maps


def kernel_fused(**inputs):
    x = inputs["L0_attn_x"]
    shards = to_fm_shards(x)
    nc = build_fused()
    in_maps = fused_in_maps(shards, inputs)
    res = run_bass_kernel_spmd(nc, in_maps, core_ids=list(range(N_CORES)))
    return from_fm_shards([r["xo"] for r in res.results])
